# version 21
# baseline (speedup 1.0000x reference)
import contextlib
import numpy as np
import ml_dtypes
import concourse.bass as bass
import concourse.mybir as mybir
from concourse.bass_utils import run_bass_kernel_spmd

F32 = mybir.dt.float32
BF16 = mybir.dt.bfloat16
F16 = mybir.dt.float16
U8 = mybir.dt.uint8
I32 = mybir.dt.int32
ALU = mybir.AluOpType
AF = mybir.ActivationFunctionType
AXX = mybir.AxisListType.X

T = 2048
D = 2048
NCORES = 8
NT = T // 128
ND = D // 128

D_INNER = 4096
D_STATE = 64
NHEADS = 64
CONV_DIM = D_INNER + 2 * D_STATE
NCT = CONV_DIM // 128
ZXDT = 2 * D_INNER + 2 * D_STATE + NHEADS

BH = 16
BD = 128

FI = 5632
GU = 2 * FI

PERM = np.concatenate([np.arange(0, D, 2), np.arange(1, D, 2)])
INVPERM = np.argsort(PERM)

RS_D = float(1.0 / np.sqrt(128.0))


DBG_STAGE = "full"
_STAGES = ["rope", "bound", "scores", "av1", "av2", "av3", "av", "full"]


def _stg(s):
    return _STAGES.index(DBG_STAGE) >= _STAGES.index(s)


def ts(i, size=128):
    return slice(i * size, (i + 1) * size)



def _ffn_col_order():
    cols = []
    for p2 in range(2):
        for tau in range(22):
            base = 2 * (128 * tau + np.arange(128)) + p2
            cols.append(base)
            cols.append(FI + base)
    return np.concatenate(cols)


FFN_COL_ORDER = _ffn_col_order()


def _qkv_row_order():
    order = []
    for h in range(BH):
        for sec in range(3):
            base = sec * D + h * BD
            if sec < 2:
                order.append(base + np.arange(0, BD, 2))
                order.append(base + np.arange(1, BD, 2))
            else:
                order.append(base + np.arange(BD))
    return np.concatenate(order)


QKV_ROW_ORDER = _qkv_row_order()


def host_prep(inputs):
    f = np.ascontiguousarray
    out = {}
    out["w_inproj"] = f(inputs["L0_attn_a_in_proj_w"].T[PERM, :])
    out["w_outproj"] = f(inputs["L0_attn_a_out_proj_w"][PERM, :].T)
    out["rms_w"] = f(inputs["L0_attn_a_rms_weight"][PERM][None, :])
    out["conv_w"] = f(inputs["L0_attn_a_conv_w"])
    out["conv_b"] = f(inputs["L0_attn_a_conv_b"].reshape(NCT, 128).T)
    out["dt_bias"] = f(inputs["L0_attn_a_dt_bias"][None, :])
    out["a_log"] = f(inputs["L0_attn_a_A_log"][None, :])
    out["d_vec"] = f(inputs["L0_attn_a_D"][None, :])
    out["norm_w"] = f(inputs["L0_attn_a_norm_w"].reshape(32, 128))

    out["w_qkv"] = f(inputs["L0_attn_b_W_qkv"][QKV_ROW_ORDER, :].T[PERM, :])
    out["w_proj"] = f(inputs["L0_attn_b_W_proj"][PERM, :].T)

    gub = inputs["L0_ffn_gate_up_packed"].view(np.uint8).reshape(GU, 1024)
    out["gu_bytes"] = f(gub.T[:, FFN_COL_ORDER])
    gus = inputs["L0_ffn_gate_up_scales"][:, FFN_COL_ORDER]
    out["gu_scex"] = f(gus[np.arange(1024) // 64])
    dpb = inputs["L0_ffn_down_packed"].view(np.uint8).reshape(D, 2816)
    out["down_bytes"] = f(dpb.T[:, PERM])
    dns = inputs["L0_ffn_down_scales"][:, PERM]
    out["dn_scex"] = f(dns[np.arange(2816) // 64])

    out["ident"] = np.eye(128, dtype=np.float32)
    ij = np.arange(128)
    out["masku"] = f(np.where(ij[:, None] < ij[None, :], np.float32(-1.0e30),
                              np.float32(0.0)).astype(np.float32))
    r127 = np.zeros((128, 128), np.float32)
    r127[127, :] = 1.0
    out["r127"] = r127
    out["ones_row"] = np.ones((1, 128), dtype=np.float32)
    ii = np.arange(128)
    out["ltri"] = f((ii[:, None] <= ii[None, :]).astype(np.float32))
    jm = np.zeros((128, 128), np.float32)
    jm[:64, :] = 1.0
    out["jmid"] = jm
    out["dmask"] = f(np.where(ii[:, None] > ii[None, :], np.float32(-3.0e38),
                              np.float32(0.0)).astype(np.float32))
    out["dmask_t"] = f(out["dmask"].T)
    pos = np.arange(T, dtype=np.float64)
    invf = 1.0 / 10000.0 ** (2.0 * np.arange(64, dtype=np.float64) / 128.0)
    ang = invf[:, None] * pos[None, :]
    cos_t = np.cos(ang).astype(np.float32)
    sin_t = np.sin(ang).astype(np.float32)
    out["cos_d"] = f(np.concatenate([cos_t, cos_t], 0))
    out["sin_np"] = f(np.concatenate([-sin_t, sin_t], 0))
    out["sin_pn"] = f(np.concatenate([sin_t, -sin_t], 0))
    return out



class Prog:
    def __init__(self, nc, ctx, prev_marks):
        self.nc = nc
        self.ctx = ctx
        self.prev = prev_marks
        self.marks = {}
        self.sems = {}
        self.counts = {}
        self.dma_sems = set()
        self.logmap = {}

    def _sem(self, name):
        if name not in self.sems:
            self.sems[name] = self.ctx.enter_context(self.nc.semaphore(name))
            self.counts[name] = 0
        return self.sems[name]

    def inc(self, instr, semname, k=1, mark=None):
        c = instr.concise()
        toks = c.split()
        logical = semname
        if "DMACopy" not in toks[1] and "Collective" not in toks[1]:
            semname = "X" + toks[0]
        else:
            self.dma_sems.add(semname)
        self.logmap[logical] = semname
        h = self._sem(semname)
        instr.then_inc(h, k)
        self.counts[semname] += k
        if mark is not None:
            assert mark not in self.marks, f"duplicate mark {mark}"
            self.marks[mark] = (semname, self.counts[semname])
        return instr

    def mark_here(self, semname, mark):
        semname = self.logmap.get(semname, semname)
        assert mark not in self.marks, mark
        self.marks[mark] = (semname, self.counts.get(semname, 0))

    def wait(self, eng, mark):
        if mark is None:
            return
        if isinstance(mark, (list, tuple)):
            for m in mark:
                self.wait(eng, m)
            return
        if self.prev is None:
            if mark in self.marks:
                sn, v = self.marks[mark]
                eng.wait_ge(self._sem(sn), v)
            else:
                eng.wait_ge(self._sem("_p1dummy"), 0)
        else:
            if mark not in self.prev:
                assert DBG_STAGE != "full", f"unknown mark {mark}"
                eng.wait_ge(self._sem("_p1dummy"), 0)
                return
            sn, v = self.prev[mark]
            eng.wait_ge(self._sem(sn), v)

    def waits(self, eng, *ms):
        for m in ms:
            self.wait(eng, m)

    def selfsync(self, eng, instr, semname):
        toks = instr.concise().split()
        logical = semname
        if "DMACopy" not in toks[1]:
            semname = "X" + toks[0]
        self.logmap[logical] = semname
        h = self._sem(semname)
        instr.then_inc(h, 1)
        self.counts[semname] += 1
        eng.wait_ge(h, self.counts[semname])



def declare_io(nc):
    g = {}
    g["x_in"] = nc.dram_tensor("x_in", [T, D], F32, kind="ExternalInput")
    g["x_out"] = nc.dram_tensor("x_out", [T, D], F32, kind="ExternalOutput")
    g["x_res"] = nc.dram_tensor("x_res", [T, D], F32, kind="Internal")
    g["w_inproj"] = nc.dram_tensor("w_inproj", [D, ZXDT], F32, kind="ExternalInput")
    g["w_outproj"] = nc.dram_tensor("w_outproj", [D_INNER, D], F32, kind="ExternalInput")
    g["rms_w"] = nc.dram_tensor("rms_w", [1, D], F32, kind="ExternalInput")
    g["conv_w"] = nc.dram_tensor("conv_w", [CONV_DIM, 4], F32, kind="ExternalInput")
    g["conv_b"] = nc.dram_tensor("conv_b", [128, NCT], F32, kind="ExternalInput")
    g["dt_bias"] = nc.dram_tensor("dt_bias", [1, NHEADS], F32, kind="ExternalInput")
    g["a_log"] = nc.dram_tensor("a_log", [1, NHEADS], F32, kind="ExternalInput")
    g["d_vec"] = nc.dram_tensor("d_vec", [1, NHEADS], F32, kind="ExternalInput")
    g["norm_w"] = nc.dram_tensor("norm_w", [32, 128], F32, kind="ExternalInput")
    g["w_qkv"] = nc.dram_tensor("w_qkv", [D, 3 * D], F32, kind="ExternalInput")
    g["w_proj"] = nc.dram_tensor("w_proj", [D, D], F32, kind="ExternalInput")
    g["gu_bytes"] = nc.dram_tensor("gu_bytes", [1024, GU], U8, kind="ExternalInput")
    g["gu_scex"] = nc.dram_tensor("gu_scex", [1024, GU], F16, kind="ExternalInput")
    g["down_bytes"] = nc.dram_tensor("down_bytes", [2816, D], U8, kind="ExternalInput")
    g["dn_scex"] = nc.dram_tensor("dn_scex", [2816, D], F16, kind="ExternalInput")
    g["ident"] = nc.dram_tensor("ident", [128, 128], F32, kind="ExternalInput")
    g["ones_row"] = nc.dram_tensor("ones_row", [1, 128], F32, kind="ExternalInput")
    g["ltri"] = nc.dram_tensor("ltri", [128, 128], F32, kind="ExternalInput")
    g["jmid"] = nc.dram_tensor("jmid", [128, 128], F32, kind="ExternalInput")
    g["dmask"] = nc.dram_tensor("dmask", [128, 128], F32, kind="ExternalInput")
    g["dmask_t"] = nc.dram_tensor("dmask_t", [128, 128], F32, kind="ExternalInput")
    g["cos_d"] = nc.dram_tensor("cos_d", [128, T], F32, kind="ExternalInput")
    g["sin_np"] = nc.dram_tensor("sin_np", [128, T], F32, kind="ExternalInput")
    g["sin_pn"] = nc.dram_tensor("sin_pn", [128, T], F32, kind="ExternalInput")
    g["z_hbm"] = nc.dram_tensor("z_hbm", [T, D_INNER], BF16, kind="Internal")
    g["masku"] = nc.dram_tensor("masku", [128, 128], F32, kind="ExternalInput")
    g["r127"] = nc.dram_tensor("r127", [128, 128], F32, kind="ExternalInput")
    g["xbcf_hbm"] = nc.dram_tensor("xbcf_hbm", [CONV_DIM, T], BF16, kind="Internal")
    g["wo_bf"] = nc.dram_tensor("wo_bf", [D_INNER, D], BF16, kind="Internal")
    g["sg_hbm"] = nc.dram_tensor("sg_hbm", [64, 128], F32, kind="Internal")
    g["dqgu_b"] = nc.dram_tensor("dqgu_b", [128, GU], U8, kind="ExternalInput")
    g["dqgu_s"] = nc.dram_tensor("dqgu_s", [128, GU], F16, kind="ExternalInput")
    g["dqdn_b"] = nc.dram_tensor("dqdn_b", [768, D], U8, kind="ExternalInput")
    g["dqdn_s"] = nc.dram_tensor("dqdn_s", [768, D], F16, kind="ExternalInput")
    g["dq_sh"] = nc.dram_tensor("dq_sh", [128, 16], I32, kind="ExternalInput")
    g["gu_loc"] = nc.dram_tensor("gu_loc", [256, GU], BF16, kind="Internal")
    g["dn_loc"] = nc.dram_tensor("dn_loc", [768, D], BF16, kind="Internal")
    g["wgu_g"] = nc.dram_tensor("wgu_g", [2048, GU], BF16, kind="Internal")
    g["wdn_g"] = nc.dram_tensor("wdn_g", [6144, D], BF16, kind="Internal")
    g["x_t1"] = nc.dram_tensor("x_t1", [T, D], F32, kind="Internal")
    g["x_t2"] = nc.dram_tensor("x_t2", [T, D], F32, kind="Internal")
    g["x_t3"] = nc.dram_tensor("x_t3", [T, D], F32, kind="Internal")
    g["wgu_bf"] = nc.dram_tensor("wgu_bf", [D, GU], BF16, kind="Internal")
    g["wdn_bf"] = nc.dram_tensor("wdn_bf", [FI, D], BF16, kind="Internal")
    return g



def alloc_consts(nc, ctx):
    cst = {}
    cst["ident_bf"] = ctx.enter_context(nc.sbuf_tensor("c_ident", [128, 128], BF16))
    cst["ident_f"] = ctx.enter_context(nc.sbuf_tensor("c_identf", [128, 128], F32))
    cst["ones_bf"] = ctx.enter_context(nc.sbuf_tensor("c_ones", [1, 128], BF16))
    cst["ones_f"] = ctx.enter_context(nc.sbuf_tensor("c_onesf", [1, 128], F32))
    cst["ones_colf"] = ctx.enter_context(nc.sbuf_tensor("c_onescf", [128, 1], F32))
    cst["half_col"] = ctx.enter_context(nc.sbuf_tensor("c_halfc", [128, 1], F32))
    cst["ltri_f"] = ctx.enter_context(nc.sbuf_tensor("c_ltri", [128, 128], F32))
    cst["jmid_f"] = ctx.enter_context(nc.sbuf_tensor("c_jmid", [128, 128], F32))
    cst["dmask_f"] = ctx.enter_context(nc.sbuf_tensor("c_dmask", [128, 128], F32))
    cst["dmask_t"] = ctx.enter_context(nc.sbuf_tensor("c_dmaskt", [128, 128], F32))
    cst["masku_bf"] = ctx.enter_context(nc.sbuf_tensor("c_masku", [128, 128], BF16))
    cst["r127_f"] = ctx.enter_context(nc.sbuf_tensor("c_r127", [128, 128], F32))
    return cst


def emit_consts(p, block, g, cst):
    @block.gpsimd
    def _(gp):
        for dst, src in [(cst["ident_bf"], g["ident"]), (cst["ones_bf"], g["ones_row"]),
                         (cst["masku_bf"], g["masku"])]:
            p.inc(gp.dma_start(dst[:, :], src[:, :]), "cstg", 16)
        p.mark_here("cstg", "consts_g")

    @block.sync
    def _(sy):
        for dst, src in [(cst["ones_f"], g["ones_row"]), (cst["ltri_f"], g["ltri"]),
                         (cst["jmid_f"], g["jmid"]), (cst["dmask_f"], g["dmask"]),
                         (cst["ident_f"], g["ident"]),
                         (cst["dmask_t"], g["dmask_t"]),
                         (cst["r127_f"], g["r127"])]:
            p.inc(sy.dma_start(dst[:, :], src[:, :]), "csts", 16)
        p.mark_here("csts", "consts_s")

    @block.vector
    def _(v):
        p.waits(v, "consts_g", "consts_s")
        v.memset(cst["ones_colf"][:, :], 1.0)
        p.inc(v.memset(cst["half_col"][:, :], 0.5), "cstv", 1, mark="consts")




def emit_ln(p, block, ctx, g, cst, nm, x_src, u_fm, rms_mode=False, xp_dst=None,
            start_mark=None, tile0=0, ntiles=NT, ucols=T, bufs=2):
    EPS = 1e-5
    nc = p.nc
    xt = ctx.enter_context(nc.sbuf_tensor(nm + "_xt", [128, bufs * D], F32))
    ut = ctx.enter_context(nc.sbuf_tensor(nm + "_ut", [128, bufs * D], BF16))
    if bufs == 2:
        dmp = ctx.enter_context(nc.sbuf_tensor(nm + "_dmp", [128, D], BF16))
    st = ctx.enter_context(nc.sbuf_tensor(nm + "_st", [128, 16], F32))
    ptr = [ctx.enter_context(nc.psum_tensor(nm + f"_ptr{i}", [128, 1024], BF16)) for i in range(2)]
    if rms_mode:
        lnt = ctx.enter_context(nc.sbuf_tensor(nm + "_lnt", [128, D], F32))
        wtile = ctx.enter_context(nc.sbuf_tensor(nm + "_wtl", [128, D], BF16))
        wrow = ctx.enter_context(nc.sbuf_tensor(nm + "_wrw", [1, D], F32))
        pwb = ctx.enter_context(nc.psum_tensor(nm + "_pwb", [128, 512], F32))

    @block.sync
    def _(sy):
        p.wait(sy, start_mark)
        if rms_mode:
            p.inc(sy.dma_start(wrow[:, :], g["rms_w"][:, :]), nm + "_ld", 16,
                  mark=nm + "_wrow")
        for k in range(tile0, tile0 + ntiles):
            sl = k % bufs
            if bufs == 2 and k >= tile0 + 2:
                p.wait(sy, f"{nm}_u{k - 2}")
            elif bufs == 1 and k >= tile0 + 1:
                p.wait(sy, f"{nm}_ev{k - 1}")
            p.inc(sy.dma_start(xt[:, ts(sl, D)], x_src[ts(k), :]),
                  nm + f"_ld{sl}", 16, mark=f"{nm}_x{k}")
            if rms_mode:
                p.wait(sy, f"{nm}_u{k}")
                p.inc(sy.dma_start(xp_dst[ts(k), :], lnt[:, 0:D]),
                      nm + f"_xp{k % 2}", 16, mark=f"{nm}_xp{k}")
        if rms_mode:
            p.mark_here(nm + "_xp0", nm + "_xpdone0")
            p.mark_here(nm + "_xp1", nm + "_xpdone1")

    @block.tensor
    def _(t):
        p.wait(t, "consts")
        if rms_mode:
            p.wait(t, nm + "_wrow")
            for j in range(4):
                if j >= 1:
                    p.wait(t, f"{nm}_wb{j - 1}")
                p.inc(t.matmul(pwb[:, :], cst["ones_f"][0:1, 0:128],
                               wrow[0:1, ts(j, 512)], start=True, stop=True),
                      nm + "_tr", 1, mark=f"{nm}_wmm{j}")
        for k in range(tile0, tile0 + ntiles):
            p.wait(t, f"{nm}_u{k}")
            for j in range(ND):
                idx = (k - tile0) * ND + j
                if idx >= 2:
                    p.wait(t, f"{nm}_ev_i{idx - 2}")
                p.inc(t.transpose(ptr[idx % 2][:, 0:128],
                                  ut[:, ts(k % bufs, D)][:, ts(j)],
                                  cst["ident_bf"][:, :]),
                      nm + "_tr", 1, mark=f"{nm}_tr_i{idx}")

    @block.scalar
    def _(s):
        if rms_mode:
            for j in range(4):
                p.wait(s, f"{nm}_wmm{j}")
                p.inc(s.copy(wtile[:, ts(j, 512)], pwb[:, :]), nm + "_wb", 1,
                      mark=f"{nm}_wb{j}")
            p.mark_here(nm + "_wb", nm + "_wtile")
        for k in range(tile0, tile0 + ntiles):
            p.wait(s, f"{nm}_x{k}")
            xs = xt[:, ts(k % bufs, D)]
            acc = st[:, 1:2] if k % 2 == 0 else st[:, 12:13]
            dmpt = dmp[:, :] if bufs == 2 else ut[:, 0:D]
            p.inc(s.activation(dmpt, xs, AF.Square, accum_out=acc),
                  nm + "_sq", 1, mark=f"{nm}_sq{k}")
            p.wait(s, f"{nm}_var{k}")
            SSA = nm + "_sch"
            p.selfsync(s, s.activation(st[:, 4:5], st[:, 3:4], AF.Ln), SSA)
            if rms_mode:
                p.selfsync(s, s.activation(st[:, 5:6], st[:, 4:5], AF.Exp, scale=-0.5), SSA)
                p.selfsync(s, s.activation(st[:, 9:10], st[:, 8:9], AF.Ln), SSA)
                p.inc(s.activation(st[:, 10:11], st[:, 9:10], AF.Exp, scale=-0.5),
                      nm + "_rst", 1, mark=f"{nm}_rst{k}")
            else:
                p.inc(s.activation(st[:, 5:6], st[:, 4:5], AF.Exp, scale=-0.5),
                      nm + "_rst", 1, mark=f"{nm}_rst{k}")
            for j in range(ND):
                idx = (k - tile0) * ND + j
                p.wait(s, f"{nm}_tr_i{idx}")
                p.inc(s.copy(u_fm[:, j * ucols + (k - tile0) * 128:
                                  j * ucols + (k - tile0 + 1) * 128],
                             ptr[idx % 2][:, 0:128]), nm + "_ev", 1,
                      mark=f"{nm}_ev_i{idx}")
            p.mark_here(nm + "_ev", f"{nm}_ev{k}")
        p.mark_here(nm + "_ev", f"{nm}_done")

    @block.vector
    def _(v):
        for k in range(tile0, tile0 + ntiles):
            p.waits(v, f"{nm}_x{k}", f"{nm}_sq{k}")
            xs = xt[:, ts(k % bufs, D)]
            us = ut[:, ts(k % bufs, D)]
            acc = st[:, 1:2] if k % 2 == 0 else st[:, 12:13]
            dmpt = dmp[:, :] if bufs == 2 else us
            p.selfsync(v, v.tensor_scalar(dmpt, xs, 0.0, 0.0, ALU.add, ALU.add,
                                          accum_out=st[:, 0:1]), nm + "_vacc")
            SS = nm + "_vch"
            p.selfsync(v, v.tensor_scalar(st[:, 2:3], st[:, 0:1], 1.0 / D, None, ALU.mult), SS)
            p.selfsync(v, v.tensor_scalar(st[:, 11:12], st[:, 2:3], st[:, 2:3], None, ALU.mult), SS)
            p.selfsync(v, v.tensor_scalar(st[:, 3:4], acc, 1.0 / D, None, ALU.mult), SS)
            p.selfsync(v, v.tensor_sub(st[:, 3:4], st[:, 3:4], st[:, 11:12]), SS)
            if rms_mode:
                p.selfsync(v, v.tensor_scalar(st[:, 7:8], st[:, 3:4], 0.0, None, ALU.add), SS)
            p.selfsync(v, v.tensor_scalar(st[:, 3:4], st[:, 3:4], EPS, None, ALU.add), SS)
            if rms_mode:
                p.selfsync(v, v.reciprocal(st[:, 11:12], st[:, 3:4]), SS)
                p.selfsync(v, v.tensor_tensor(st[:, 7:8], st[:, 7:8], st[:, 11:12], ALU.mult), SS)
                p.inc(v.tensor_scalar(st[:, 8:9], st[:, 7:8], EPS, None, ALU.add),
                      nm + "_var", 1, mark=f"{nm}_var{k}")
            else:
                p.inc(v.tensor_scalar(st[:, 8:9], st[:, 3:4], 0.0, None, ALU.add),
                      nm + "_var", 1, mark=f"{nm}_var{k}")
            p.wait(v, f"{nm}_rst{k}")
            if rms_mode:
                p.selfsync(v, v.tensor_tensor(st[:, 6:7], st[:, 5:6], st[:, 10:11], ALU.mult), SS)
                if k >= 1:
                    p.wait(v, f"{nm}_xp{k - 1}")
                lns = lnt[:, 0:D]
                p.selfsync(v, v.tensor_scalar(lns, xs, st[:, 2:3], st[:, 5:6],
                                              ALU.subtract, ALU.mult), SS)
                p.wait(v, nm + "_wtile")
                p.selfsync(v, v.tensor_scalar(us, xs, st[:, 2:3], st[:, 6:7],
                                              ALU.subtract, ALU.mult), SS)
                p.selfsync(v, v.tensor_tensor(us, us, wtile[:, :], ALU.mult), SS)
                p.inc(v.tensor_add(lns, lns, xs), nm + "_u", 1, mark=f"{nm}_u{k}")
            else:
                p.inc(v.tensor_scalar(us, xs, st[:, 2:3], st[:, 5:6],
                                      ALU.subtract, ALU.mult),
                      nm + "_u", 1, mark=f"{nm}_u{k}")



def emit_attn_b(p, block, ctx, g, cst, x_src, x_dst, start_mark=None):
    nm = "atb"
    nc = p.nc

    with contextlib.ExitStack() as c1:
        u_fm = c1.enter_context(nc.sbuf_tensor(nm + "_ufm", [128, ND * T], BF16))
        y_fm = c1.enter_context(nc.sbuf_tensor(nm + "_yfm", [128, ND * T], BF16))

        with contextlib.ExitStack() as c0:
            emit_ln(p, block, c0, g, cst, nm + "l", x_src, u_fm,
                    start_mark=start_mark)

        ch = contextlib.ExitStack()
        wstg = [ch.enter_context(nc.sbuf_tensor(nm + f"_wst{i}", [128, 16 * 128], BF16)) for i in range(2)]
        ab = ch.enter_context(nc.sbuf_tensor(nm + "_ab", [128, 2 * T], BF16))
        vf = ch.enter_context(nc.sbuf_tensor(nm + "_vf", [128, T], BF16))
        rx = ch.enter_context(nc.sbuf_tensor(nm + "_rx", [128, T], BF16))
        ry = ch.enter_context(nc.sbuf_tensor(nm + "_ry", [128, T], BF16))
        q2 = ch.enter_context(nc.sbuf_tensor(nm + "_q2", [128, T], BF16))
        k2 = ch.enter_context(nc.sbuf_tensor(nm + "_k2", [128, T], BF16))
        vt = ch.enter_context(nc.sbuf_tensor(nm + "_vt", [128, NT * 132], BF16))
        bcol = ch.enter_context(nc.sbuf_tensor(nm + "_bcl", [128, 1], F32))
        rmx = ch.enter_context(nc.sbuf_tensor(nm + "_rmx", [128, 512], BF16))
        nbb = ch.enter_context(nc.sbuf_tensor(nm + "_nbb", [1, T], BF16))
        onesr = ch.enter_context(nc.sbuf_tensor(nm + "_1r", [1, 128], BF16))
        pbuf = ch.enter_context(nc.sbuf_tensor(nm + "_pbf", [128, 2 * 512], BF16))
        rsum = ch.enter_context(nc.sbuf_tensor(nm + "_rsm", [128, 4], F32))
        ystg = ch.enter_context(nc.sbuf_tensor(nm + "_ysg", [128, 2 * 128], BF16))
        trig = ch.enter_context(nc.sbuf_tensor(nm + "_trig", [128, 3 * T], BF16))
        COS, SNP, SPN = (trig[:, 0:T], trig[:, T:2 * T], trig[:, 2 * T:3 * T])

        pp = [c1.enter_context(nc.psum_tensor(nm + f"_pp{i}", [128, 512], F32)) for i in range(6)]
        pbf = [c1.enter_context(nc.psum_tensor(nm + f"_pbf{i}", [128, 1024], BF16)) for i in range(2)]
        PAV = [pp[2], pp[3], pp[4], pp[5]]

        def qi(h, sec, c):
            return h * 12 + sec * 4 + c

        SEQ = [(c, tau) for c in range(4) for tau in range(4 * c + 4)]
        SEQI = {ct: i for i, ct in enumerate(SEQ)}

        @block.gpsimd
        def _(gp):
            p.wait(gp, nm + "l_done")
            p.inc(gp.dma_start(trig[:, 0:T], g["cos_d"][:, :]), nm + "_tg", 16)
            p.inc(gp.dma_start(trig[:, T:2 * T], g["sin_np"][:, :]), nm + "_tg", 16)
            p.inc(gp.dma_start(trig[:, 2 * T:3 * T], g["sin_pn"][:, :]), nm + "_tg", 16)
            p.mark_here(nm + "_tg", nm + "_trig")
            for h in range(BH):
                for sec in range(3):
                    gs = 3 * h + sec
                    if gs >= 2:
                        ph, psec = divmod(gs - 2, 3)
                        p.wait(gp, f"{nm}_qkv_i{ph * 12 + psec * 4 + 3}")
                    src = g["w_qkv"][:, 384 * h + 128 * sec: 384 * h + 128 * (sec + 1)]
                    srcr = src.rearrange("(a p) c -> p a c", p=128)
                    dstr = wstg[gs % 2][:, :].rearrange("p (a c) -> p a c", c=128)
                    p.inc(gp.dma_start(dstr, srcr), nm + f"_wld{gs % 2}", 16,
                          mark=f"{nm}_wsec{gs}")

        @block.vector
        def _(v):
            p.wait(v, nm + "l_done")
            v.memset(onesr[0:1, :], 1.0)
            for j in range(NT):
                p.inc(v.memset(vt[:, j * 132 + 128: j * 132 + 132], 1.0),
                      nm + "_vo", 1)
            p.mark_here(nm + "_vo", nm + "_vones")
            for h in range(BH):
                p.wait(v, f"{nm}_qe_i{qi(h, 1, 3)}")
                p.wait(v, nm + "_trig")
                A = ab[:, 0:T]
                B = ab[:, T: 2 * T]
                SSR = nm + "_vch3"
                p.selfsync(v, v.tensor_mul(rx[:, :], A, COS), SSR)
                p.selfsync(v, v.tensor_mul(q2[:, :], B, SNP), SSR)
                p.selfsync(v, v.tensor_add(rx[:, :], rx[:, :], q2[:, :]), SSR)
                p.selfsync(v, v.tensor_mul(ry[:, :], A, SPN), SSR)
                p.selfsync(v, v.tensor_mul(q2[:, :], B, COS), SSR)
                p.selfsync(v, v.tensor_add(ry[:, :], ry[:, :], q2[:, :]), SSR)
                p.selfsync(v, v.tensor_copy(k2[0:64, :], ry[64:128, :]), SSR)
                p.selfsync(v, v.tensor_copy(k2[64:128, :], rx[64:128, :]), SSR)
                p.selfsync(v, v.tensor_copy(q2[0:64, :], rx[0:64, :]), SSR)
                p.inc(v.tensor_copy(q2[64:128, :], ry[0:64, :]),
                      nm + "_rope", 1, mark=f"{nm}_rope{h}")
                for gt1 in range(NT if _stg("bound") else 0):
                    nsc = (gt1 + 4) // 4
                    for cc in range(nsc):
                        w = min(512, (gt1 + 1) * 128 - cc * 512)
                        pb = 2 + cc % 2
                        p.wait(v, f"{nm}_mxm{h}_{gt1}_{cc}")
                        if cc == nsc - 1:
                            off = gt1 * 128 - cc * 512
                            p.selfsync(v, v.tensor_tensor(pp[pb][:, off: off + 128],
                                                          pp[pb][:, off: off + 128],
                                                          cst["dmask_t"][:, :], ALU.add),
                                       nm + "_vch2")
                        if cc == 0:
                            fold = v.tensor_scalar(rmx[:, 0:w], pp[pb][:, 0:w],
                                                   0.0, None, ALU.add)
                        else:
                            fold = v.tensor_tensor(rmx[:, 0:w], rmx[:, 0:w],
                                                   pp[pb][:, 0:w], ALU.max)
                        p.inc(fold, nm + "_mxfs", 1, mark=f"{nm}_mxf{h}_{gt1}_{cc}")
                        p.wait(v, f"{nm}_mxf{h}_{gt1}_{cc}")
                    wfin = min(512, (gt1 + 1) * 128)
                    p.inc(v.reduce_max(bcol[:, 0:1], rmx[:, 0:wfin], axis=AXX),
                          nm + "_bmx", 1, mark=f"{nm}_bmax{h}_{gt1}")

                for c in range(4 if _stg("scores") else 0):
                    for tau in range(4 * c, 4 * c + 4):
                        seq = SEQI[(c, tau)]
                        mu = tau - 4 * c
                        p.wait(v, f"{nm}_smm{h}_{c}_{tau}")
                        p.inc(v.tensor_tensor(pp[seq % 2][:, ts(mu)],
                                              pp[seq % 2][:, ts(mu)],
                                              cst["dmask_f"][:, :], ALU.add),
                              nm + "_smsk", 1, mark=f"{nm}_smsk{h}_{c}_{tau}")
                    for mu in range(4 if _stg("av2") else 0):
                        gt1 = 4 * c + mu
                        p.wait(v, f"{nm}_av{h}_{gt1}")
                        p.inc(v.reciprocal(rsum[:, mu: mu + 1],
                                           PAV[mu][:, 128:129]),
                              nm + "_rcp", 1, mark=f"{nm}_rcp{h}_{gt1}")
                p.mark_here(nm + "_smsk", f"{nm}_scoresdone{h}")

        @block.tensor
        def _(t):
            for h in range(BH):
                p.wait(t, f"{nm}l_done")
                for sec in range(3):
                    gs = 3 * h + sec
                    p.wait(t, f"{nm}_wsec{gs}")
                    for c in range(4):
                        i = qi(h, sec, c)
                        pi = i % 2
                        if i >= 2:
                            p.wait(t, f"{nm}_qe_i{i - 2}")
                        last = None
                        for kk in range(ND):
                            last = t.matmul(
                                pp[pi][:, :],
                                wstg[gs % 2][:, kk * 128: (kk + 1) * 128],
                                u_fm[:, kk * T + c * 512: kk * T + (c + 1) * 512],
                                start=(kk == 0), stop=(kk == ND - 1))
                        p.inc(last, nm + "_qkv", 1, mark=f"{nm}_qkv_i{i}")
                p.mark_here(nm + "_qkv", f"{nm}_qkvdone{h}")
                for j in range(NT):
                    gidx = h * NT + j
                    p.wait(t, f"{nm}_qe_i{qi(h, 2, 3)}")
                    if j < 2:
                        if h >= 1:
                            p.wait(t, f"{nm}_yfmev{h - 1}_{14 + j}")
                    else:
                        p.wait(t, f"{nm}_vtev_i{gidx - 2}")
                    p.inc(t.transpose(pbf[j % 2][:, 0:128], vf[:, ts(j)],
                                      cst["ident_bf"][:, :]),
                          nm + "_vtr", 1, mark=f"{nm}_vtr_i{gidx}")
                for gt1 in range(NT if _stg("bound") else 0):
                    p.wait(t, f"{nm}_rope{h}")
                    nsc = (gt1 + 4) // 4
                    for cc in range(nsc):
                        w = min(512, (gt1 + 1) * 128 - cc * 512)
                        pb = 2 + cc % 2
                        if cc >= 2:
                            p.wait(t, f"{nm}_mxf{h}_{gt1}_{cc - 2}")
                        elif gt1 >= 1:
                            pn = (gt1 + 3) // 4
                            p.wait(t, f"{nm}_mxf{h}_{gt1 - 1}_{pn - 1}")
                            p.wait(t, f"{nm}_nbbev{h}_{gt1 - 1}")
                        elif h >= 1 and cc == 0:
                            pn = (NT + 3) // 4
                            p.waits(t, f"{nm}_nbbev{h - 1}_{NT - 1}",
                                    f"{nm}_mxf{h - 1}_{NT - 1}_{pn - 1}",
                                    f"{nm}_yev{h - 1}_{12}", f"{nm}_yev{h - 1}_{13}")
                        p.inc(t.matmul(pp[pb][:, 0:w], q2[:, ts(gt1)],
                                       k2[:, cc * 512: cc * 512 + w],
                                       start=True, stop=True),
                              nm + "_mxm", 1, mark=f"{nm}_mxm{h}_{gt1}_{cc}")
                    p.wait(t, f"{nm}_bmax{h}_{gt1}")
                    if gt1 >= 1:
                        p.wait(t, f"{nm}_nbbev{h}_{gt1 - 1}")
                    elif h >= 1:
                        p.wait(t, f"{nm}_nbbev{h - 1}_{NT - 1}")
                    pbt = 2 + (nsc - 1) % 2
                    p.inc(t.transpose(pp[5 - pbt][0:1, 0:128], bcol[:, 0:1],
                                      cst["ident_f"][:, :]),
                          nm + "_btr", 1, mark=f"{nm}_btr{h}_{gt1}")
                for c in range(4 if _stg("scores") else 0):
                    ntau = 4 * c + 4
                    for tau in range(-1, ntau):
                        tau1 = tau + 1
                        if tau1 < ntau:
                            seq = SEQI[(c, tau1)]
                            pi = seq % 2
                            if seq >= 2:
                                pc, pt = SEQ[seq - 2]
                                p.wait(t, f"{nm}_pev{h}_{pc}_{pt}")
                            elif h >= 1:
                                pc, pt = SEQ[len(SEQ) - 2 + seq]
                                p.wait(t, f"{nm}_pev{h - 1}_{pc}_{pt}")
                            if seq < 2:
                                p.waits(t, f"{nm}_negb{h}", f"{nm}_qe_i{qi(h, 2, 3)}")
                            t.matmul(pp[pi][:, :], k2[:, ts(tau1)],
                                     q2[:, c * 512:(c + 1) * 512],
                                     start=True, stop=False)
                            p.inc(t.matmul(pp[pi][:, :], onesr[0:1, 0:128],
                                           nbb[0:1, c * 512:(c + 1) * 512],
                                           start=False, stop=True),
                                  nm + "_smm", 1, mark=f"{nm}_smm{h}_{c}_{tau1}")
                        if tau < 0 or not _stg("av1"):
                            continue
                        p.wait(t, f"{nm}_pev{h}_{c}_{tau}")
                        lastmm = None
                        for mu in range(max(0, tau - 4 * c), 4):
                            gt1 = 4 * c + mu
                            if tau == 0 and c >= 1:
                                p.wait(t, f"{nm}_yev{h}_{4 * (c - 1) + mu}")
                            elif tau == 0 and h >= 1:
                                p.wait(t, f"{nm}_yev{h - 1}_{12 + mu}")
                            lastmm = t.matmul(
                                PAV[mu][:, 0:129],
                                pbuf[:, (SEQI[(c, tau)] % 2) * 512 + mu * 128:
                                     (SEQI[(c, tau)] % 2) * 512 + (mu + 1) * 128],
                                vt[:, tau * 132: tau * 132 + 129],
                                start=(tau == 0), stop=(tau == gt1))
                            if tau == gt1:
                                p.inc(lastmm, nm + "_av", 1, mark=f"{nm}_av{h}_{gt1}")
                                lastmm = None
                        if lastmm is not None:
                            p.inc(lastmm, nm + "_avt", 1)
                        p.mark_here(nm + "_avt" if lastmm is None else nm + "_avt",
                                    f"{nm}_avt{h}_{c}_{tau}")
                    for mu in range(4 if _stg("av") else 0):
                        gt1 = 4 * c + mu
                        gy = h * NT + gt1
                        p.wait(t, f"{nm}_yev{h}_{gt1}")
                        if gy >= 2:
                            p.wait(t, f"{nm}_yfmev{h}_{gt1 - 2}" if gt1 >= 2 else
                                   f"{nm}_yfmev{h - 1}_{14 + gt1}")
                        p.inc(t.transpose(pbf[gt1 % 2][:, 0:128],
                                          ystg[:, ts(gt1 % 2)], cst["ident_bf"][:, :]),
                              nm + "_ytr", 1, mark=f"{nm}_ytr{h}_{gt1}")

        @block.scalar
        def _(s):
            for h in range(BH):
                for sec in range(3):
                    for c in range(4):
                        i = qi(h, sec, c)
                        pi = i % 2
                        p.wait(s, f"{nm}_qkv_i{i}")
                        if h >= 1 and sec < 2 and c == 0:
                            p.wait(s, f"{nm}_rope{h - 1}")
                        ps_ = pp[pi]
                        cs = slice(c * 512, (c + 1) * 512)
                        if sec == 0:
                            s.copy(ab[0:64, 0:T][:, cs], ps_[0:64, :])
                            p.inc(s.copy(ab[0:64, T: 2 * T][:, cs], ps_[64:128, :]),
                                  nm + "_qe", 1, mark=f"{nm}_qe_i{i}")
                        elif sec == 1:
                            s.copy(ab[64:128, 0:T][:, cs], ps_[64:128, :])
                            p.inc(s.copy(ab[64:128, T: 2 * T][:, cs], ps_[0:64, :]),
                                  nm + "_qe", 1, mark=f"{nm}_qe_i{i}")
                        else:
                            p.inc(s.copy(vf[:, cs], ps_[:, :]),
                                  nm + "_qe", 1, mark=f"{nm}_qe_i{i}")
                for j in range(NT):
                    gidx = h * NT + j
                    p.wait(s, f"{nm}_vtr_i{gidx}")
                    p.inc(s.copy(vt[:, j * 132: j * 132 + 128], pbf[j % 2][:, 0:128]),
                          nm + "_vtev", 1, mark=f"{nm}_vtev_i{gidx}")
                for gt1 in range(NT if _stg("bound") else 0):
                    nsc_ = (gt1 + 4) // 4
                    tgt = 5 - (2 + (nsc_ - 1) % 2)
                    p.wait(s, f"{nm}_btr{h}_{gt1}")
                    p.inc(s.activation(nbb[0:1, ts(gt1)], pp[tgt][0:1, 0:128],
                                       AF.Identity, scale=-1.0),
                          nm + "_nbe", 1, mark=f"{nm}_nbbev{h}_{gt1}")

                p.mark_here(nm + "_nbe", f"{nm}_negb{h}")
                for c in range(4 if _stg("scores") else 0):
                    for tau in range(4 * c + 4):
                        seq = SEQI[(c, tau)]
                        pi = seq % 2
                        if tau >= 4 * c:
                            p.wait(s, f"{nm}_smsk{h}_{c}_{tau}")
                        else:
                            p.wait(s, f"{nm}_smm{h}_{c}_{tau}")
                        if seq >= 2:
                            pc, pt = SEQ[seq - 2]
                            p.wait(s, f"{nm}_avt{h}_{pc}_{pt}")
                        elif h >= 1:
                            pc, pt = SEQ[len(SEQ) - 2 + seq]
                            p.wait(s, f"{nm}_avt{h - 1}_{pc}_{pt}")
                        off = max(0, (tau - 4 * c)) * 128
                        p.inc(s.activation(pbuf[:, pi * 512 + off: (pi + 1) * 512],
                                           pp[pi][:, off:512],
                                           AF.Exp, scale=RS_D),
                              nm + "_pev", 1, mark=f"{nm}_pev{h}_{c}_{tau}")
                    for mu in range(4 if _stg("av3") else 0):
                        gt1 = 4 * c + mu
                        gy = h * NT + gt1
                        p.waits(s, f"{nm}_av{h}_{gt1}", f"{nm}_rcp{h}_{gt1}")
                        if gy >= 2:
                            p.wait(s, f"{nm}_ytr{h}_{gt1 - 2}" if gt1 >= 2 else
                                   f"{nm}_ytr{h - 1}_{14 + gt1}")
                        p.inc(s.activation(ystg[:, ts(gt1 % 2)],
                                           PAV[mu][:, 0:128], AF.Identity,
                                           scale=rsum[:, mu: mu + 1]),
                              nm + "_yev", 1, mark=f"{nm}_yev{h}_{gt1}")
                    for mu in range(4 if _stg("av") else 0):
                        gt1 = 4 * c + mu
                        p.wait(s, f"{nm}_ytr{h}_{gt1}")
                        p.inc(s.copy(y_fm[:, h * T + gt1 * 128: h * T + (gt1 + 1) * 128],
                                     pbf[gt1 % 2][:, 0:128]),
                              nm + "_yfme", 1, mark=f"{nm}_yfmev{h}_{gt1}")
            p.mark_here(nm + "_yfme", nm + "_yfmdone")

        ch.close()
        if not _stg("full"):
            @block.gpsimd
            def _(gp):
                p.wait(gp, f"{nm}_rope{BH - 1}")
                p.inc(gp.dma_start(g["x_out"][0:128, :],
                                   vf[:, :]), nm + "_dbgo", 16, mark=nm + "_DBGOUT")
            return [nm + "_DBGOUT"]
        whalf = c1.enter_context(nc.sbuf_tensor(nm + "_wph", [128, 16 * 1024], BF16))
        xres = c1.enter_context(nc.sbuf_tensor(nm + "_xr", [128, 2 * 1024], F32))
        ores = c1.enter_context(nc.sbuf_tensor(nm + "_or", [128, 2 * 1024], F32))

        @block.gpsimd
        def _(gp):
            for half in range(2):
                p.wait(gp, nm + "_yfmdone" if half == 0 else f"{nm}_omm{NT - 1}")
                src = g["w_proj"][:, half * 1024: (half + 1) * 1024]
                srcr = src.rearrange("(a p) c -> p a c", p=128)
                dstr = whalf[:, :].rearrange("p (a c) -> p a c", c=1024)
                p.inc(gp.dma_start(dstr, srcr), nm + "_wpl", 16,
                      mark=f"{nm}_wp{half}")

        @block.tensor
        def _(t):
            for half in range(2):
                p.wait(t, f"{nm}_wp{half}")
                for k in range(NT):
                    gi = half * NT + k
                    pi = gi % 2
                    if gi >= 2:
                        p.wait(t, f"{nm}_oadd{gi - 2}")
                    last = None
                    for kk in range(ND):
                        lhs = y_fm[:, kk * T + k * 128: kk * T + (k + 1) * 128]
                        t.matmul(pp[pi][:, :], lhs,
                                 whalf[:, kk * 1024: kk * 1024 + 512],
                                 start=(kk == 0), stop=(kk == ND - 1))
                        last = t.matmul(pp[2 + pi][:, :], lhs,
                                        whalf[:, kk * 1024 + 512: (kk + 1) * 1024],
                                        start=(kk == 0), stop=(kk == ND - 1))
                    p.inc(last, nm + "_omm", 1, mark=f"{nm}_omm{gi}")

        @block.sync
        def _(sy):
            for half in range(2):
                for k in range(NT):
                    gi = half * NT + k
                    if gi < 2:
                        p.wait(sy, nm + "_yfmdone")
                    else:
                        p.wait(sy, f"{nm}_oadd{gi - 2}")
                    p.inc(sy.dma_start(xres[:, ts(gi % 2, 1024)],
                                       x_src[ts(k), half * 1024:(half + 1) * 1024]),
                          nm + f"_xld{gi % 2}", 16, mark=f"{nm}_xld{gi}")
                    p.wait(sy, f"{nm}_oadd{gi}")
                    p.inc(sy.dma_start(x_dst[ts(k), half * 1024:(half + 1) * 1024],
                                       ores[:, ts(gi % 2, 1024)]),
                          nm + f"_ost{gi % 2}", 16, mark=f"{nm}_ost{gi}")

        @block.vector
        def _(v):
            for half in range(2):
                for k in range(NT):
                    gi = half * NT + k
                    pi = gi % 2
                    p.waits(v, f"{nm}_omm{gi}", f"{nm}_xld{gi}")
                    if gi >= 2:
                        p.wait(v, f"{nm}_ost{gi - 2}")
                    v.tensor_tensor(ores[:, gi % 2 * 1024: gi % 2 * 1024 + 512],
                                    pp[pi][:, :], xres[:, gi % 2 * 1024: gi % 2 * 1024 + 512],
                                    ALU.add)
                    p.inc(v.tensor_tensor(ores[:, gi % 2 * 1024 + 512: gi % 2 * 1024 + 1024],
                                          pp[2 + pi][:, :],
                                          xres[:, gi % 2 * 1024 + 512: gi % 2 * 1024 + 1024],
                                          ALU.add),
                          nm + "_oaddv", 1, mark=f"{nm}_oadd{gi}")

    return [f"{nm}_ost{2 * NT - 2}", f"{nm}_ost{2 * NT - 1}"]



def emit_dequant(p, block, ctx, g, cst, nm, start_mark=None):
    nc = p.nc
    CH = 2816
    bt = [ctx.enter_context(nc.sbuf_tensor(nm + f"_bt{i}", [128, GU], U8)) for i in range(2)]
    sct = [ctx.enter_context(nc.sbuf_tensor(nm + f"_sc{i}", [128, GU], BF16)) for i in range(2)]
    nib = ctx.enter_context(nc.sbuf_tensor(nm + "_nib", [128, CH], U8))
    lw3 = ctx.enter_context(nc.sbuf_tensor(nm + "_lw3", [128, CH], U8))
    an8 = ctx.enter_context(nc.sbuf_tensor(nm + "_an8", [128, CH], U8))
    f3 = ctx.enter_context(nc.sbuf_tensor(nm + "_f3", [128, CH], BF16))
    r1 = ctx.enter_context(nc.sbuf_tensor(nm + "_r1", [128, CH], BF16))
    r2 = ctx.enter_context(nc.sbuf_tensor(nm + "_r2", [128, CH], BF16))
    sg = ctx.enter_context(nc.sbuf_tensor(nm + "_sg", [128, CH], BF16))
    a1 = ctx.enter_context(nc.sbuf_tensor(nm + "_a1", [128, CH], BF16))
    ot = [ctx.enter_context(nc.sbuf_tensor(nm + f"_ot{i}", [128, GU], BF16)) for i in range(2)]
    cb4 = ctx.enter_context(nc.sbuf_tensor(nm + "_cb4", [128, 1], F32))
    cb12 = ctx.enter_context(nc.sbuf_tensor(nm + "_cb12", [128, 1], F32))

    jobs = []
    for jb in range(8):
        for pl in range(2):
            jobs.append(("gu", jb, pl, GU, 4))
    for jb in range(22):
        for pl in range(2):
            jobs.append(("dn", jb, pl, D, 1))
    NJ = len(jobs)

    @block.gpsimd
    def _(gp):
        for ji, (mat, jb, pl, ncol, nch) in enumerate(jobs):
            if pl != 0:
                continue
            pi = ji // 2
            sl = pi % 2
            if pi >= 2:
                p.wait(gp, f"{nm}_vdone{2 * (pi - 2) + 1}")
            else:
                p.wait(gp, start_mark)
            src = g["gu_bytes"][ts(jb), :] if mat == "gu" else g["down_bytes"][ts(jb), :]
            p.inc(gp.dma_start(bt[sl][:, 0:ncol], src), nm + f"_bl{sl}", 16,
                  mark=f"{nm}_bld{pi}")
            ssrc = g["gu_scex"][ts(jb), :] if mat == "gu" else g["dn_scex"][ts(jb), :]
            p.inc(gp.dma_start(sct[sl][:, 0:ncol], ssrc), nm + f"_sl{sl}", 16,
                  mark=f"{nm}_sld{pi}")

    @block.vector
    def _(v):
        p.wait(v, "consts")
        p.wait(v, start_mark)
        v.memset(cb4[:, :], -4.0)
        p.inc(v.memset(cb12[:, :], -12.0), nm + "_ms", 1, mark=nm + "_dqms")
        prev_act = None
        for ji, (mat, jb, pl, ncol, nch) in enumerate(jobs):
            pi = ji // 2
            sl = pi % 2
            for ci in range(nch):
                cw = ncol // nch
                cs = slice(ci * cw, (ci + 1) * cw)
                p.waits(v, f"{nm}_bld{pi}", prev_act)
                bts = bt[sl][:, cs]
                if pl == 0:
                    p.selfsync(v, v.tensor_scalar(nib[:, 0:cw], bts, 15, None,
                                                  ALU.bitwise_and), nm + "_vq")
                else:
                    p.selfsync(v, v.tensor_scalar(nib[:, 0:cw], bts, 4, None,
                                                  ALU.logical_shift_right), nm + "_vq")
                p.selfsync(v, v.tensor_scalar(lw3[:, 0:cw], nib[:, 0:cw], 7, None,
                                              ALU.bitwise_and), nm + "_vq")
                p.inc(v.tensor_scalar(an8[:, 0:cw], nib[:, 0:cw], 8, None,
                                      ALU.bitwise_and), nm + "_vq1", 1,
                      mark=f"{nm}_vnib{ji}_{ci}")
                prev_act = f"{nm}_acts{ji}_{ci}"
                p.wait(v, prev_act)
                p.selfsync(v, v.tensor_tensor(a1[:, 0:cw], f3[:, 0:cw], r1[:, 0:cw],
                                              ALU.add), nm + "_vq")
                p.selfsync(v, v.tensor_tensor(a1[:, 0:cw], a1[:, 0:cw], r2[:, 0:cw],
                                              ALU.add), nm + "_vq")
                p.selfsync(v, v.tensor_tensor(a1[:, 0:cw], a1[:, 0:cw], sg[:, 0:cw],
                                              ALU.mult), nm + "_vq")
                p.wait(v, f"{nm}_sld{pi}")
                if ji >= 2 and ci == 0:
                    p.wait(v, f"{nm}_ost{ji - 2}")
                p.inc(v.tensor_tensor(ot[ji % 2][:, cs], a1[:, 0:cw],
                                      sct[sl][:, cs], ALU.mult),
                      nm + "_vq2", 1, mark=f"{nm}_val{ji}_{ci}")
            p.mark_here(nm + "_vq2", f"{nm}_vdone{ji}")

    @block.scalar
    def _(s):
        p.wait(s, nm + "_dqms")
        for ji, (mat, jb, pl, ncol, nch) in enumerate(jobs):
            for ci in range(nch):
                cw = ncol // nch
                p.wait(s, f"{nm}_vnib{ji}_{ci}")
                s.activation(f3[:, 0:cw], lw3[:, 0:cw], AF.Identity)
                s.activation(r1[:, 0:cw], lw3[:, 0:cw], AF.Relu, bias=cb4[:, :])
                s.activation(r2[:, 0:cw], lw3[:, 0:cw], AF.Relu, bias=cb12[:, :],
                             scale=2.0)
                p.inc(s.activation(sg[:, 0:cw], an8[:, 0:cw], AF.Identity,
                                   scale=-0.125, bias=cst["half_col"][:, :]),
                      nm + "_sqa", 1, mark=f"{nm}_acts{ji}_{ci}")

    @block.sync
    def _(sy):
        for ji, (mat, jb, pl, ncol, nch) in enumerate(jobs):
            p.wait(sy, f"{nm}_vdone{ji}")
            if mat == "gu":
                dst = g["wgu_bf"][pl * 1024 + 128 * jb: pl * 1024 + 128 * (jb + 1), :]
            else:
                dst = g["wdn_bf"][pl * 2816 + 128 * jb: pl * 2816 + 128 * (jb + 1), :]
            p.inc(sy.dma_start(dst, ot[ji % 2][:, 0:ncol]), nm + f"_os{ji % 2}", 16,
                  mark=f"{nm}_ost{ji}")

    @block.gpsimd
    def _(gp):
        p.waits(gp, f"{nm}_ost{NJ - 2}", f"{nm}_ost{NJ - 1}")
        p.inc(gp.nop(), nm + "_fin", 1, mark=nm + "_ALL")

    return nm + "_ALL"


def emit_dequant8(p, block, g, cst, nm, start_mark=None, ctx=None):
    nc = p.nc
    CH = 2816
    cq = ctx if ctx is not None else contextlib.ExitStack()
    if True:
        bt = [cq.enter_context(nc.sbuf_tensor(nm + f"_bt{i}", [128, CH], U8)) for i in range(2)]
        sct = [cq.enter_context(nc.sbuf_tensor(nm + f"_sc{i}", [128, CH], F16)) for i in range(2)]
        nib = cq.enter_context(nc.sbuf_tensor(nm + "_nib", [128, CH], U8))
        an8 = cq.enter_context(nc.sbuf_tensor(nm + "_an8", [128, CH], U8))
        f3 = cq.enter_context(nc.sbuf_tensor(nm + "_f3", [128, CH], BF16))
        r1 = cq.enter_context(nc.sbuf_tensor(nm + "_r1", [128, CH], BF16))
        r2 = cq.enter_context(nc.sbuf_tensor(nm + "_r2", [128, CH], BF16))
        sg = cq.enter_context(nc.sbuf_tensor(nm + "_sg", [128, CH], BF16))
        ot = [cq.enter_context(nc.sbuf_tensor(nm + "_ot0", [128, CH], BF16))] * 2
        shc = cq.enter_context(nc.sbuf_tensor(nm + "_sh", [128, 16], I32))
        hh8 = an8
        cb4 = cq.enter_context(nc.sbuf_tensor(nm + "_cb4", [128, 1], F32))
        cb12 = cq.enter_context(nc.sbuf_tensor(nm + "_cb12", [128, 1], F32))

        units = []
        for ji in range(2):
            for ci in range(4):
                units.append((ji, ci, CH,
                              ("gu", 0, ci), ji * 128, ci * CH))
        for s in range(6):
            units.append((2 + s, 0, D, ("dn", s, 0), s * 128, 0))
        NU = len(units)

        @block.sync
        def _(gp):
            p.wait(gp, start_mark)
            p.inc(gp.dma_start(shc[:, :], g["dq_sh"][:, :]), nm + "_shl", 16,
                  mark=nm + "_shld")
            for li, (ji, ci, cw, (mat, s, cc_), dr, dc) in enumerate(units):
                sl = li % 2
                if li >= 2:
                    p.wait(gp, f"{nm}_val{li - 2}")
                if mat == "gu":
                    bs = g["dqgu_b"][:, ci * CH:(ci + 1) * CH]
                    ss = g["dqgu_s"][:, ci * CH:(ci + 1) * CH]
                else:
                    bs = g["dqdn_b"][s * 128:(s + 1) * 128, :]
                    ss = g["dqdn_s"][s * 128:(s + 1) * 128, :]
                p.inc(gp.dma_start(bt[sl][:, 0:cw], bs), nm + f"_bl{sl}", 16,
                      mark=f"{nm}_bld{li}")
                p.inc(gp.dma_start(sct[sl][:, 0:cw], ss), nm + f"_sl{sl}", 16,
                      mark=f"{nm}_sld{li}")
                if li >= 1:
                    lp = li - 1
                    jp, cp, cwp, (mp, sp_, _), drp, dcp = units[lp]
                    p.wait(gp, f"{nm}_val{lp}")
                    dstp = (g["gu_loc"] if mp == "gu" else
                            g["dn_loc"])[drp:drp + 128, dcp:dcp + cwp]
                    p.inc(gp.dma_start(dstp, ot[0][:, 0:cwp]), nm + "_os0", 16,
                          mark=f"{nm}_ost{lp}")
                if li == NU - 1:
                    p.wait(gp, f"{nm}_val{li}")
                    dstp = (g["gu_loc"] if mat == "gu" else
                            g["dn_loc"])[dr:dr + 128, dc:dc + cw]
                    p.inc(gp.dma_start(dstp, ot[0][:, 0:cw]), nm + "_os0", 16,
                          mark=f"{nm}_ost{li}")

        @block.vector
        def _(v):
            p.waits(v, "consts", start_mark, nm + "_shld")
            v.memset(cb4[:, :], -4.0)
            p.inc(v.memset(cb12[:, :], -12.0), nm + "_ms", 1, mark=nm + "_dqms")
            for li, (ji, ci, cw, (mat, s, cc_), dr, dc) in enumerate(units):
                sl = li % 2
                p.wait(v, f"{nm}_bld{li}")
                if mat == "gu" and ji == 0:
                    p.selfsync(v, v.tensor_scalar(nib[:, 0:cw], bt[sl][:, 0:cw],
                                                  15, None, ALU.bitwise_and),
                               nm + "_vq")
                elif mat == "gu":
                    p.selfsync(v, v.tensor_scalar(nib[:, 0:cw], bt[sl][:, 0:cw],
                                                  4, None,
                                                  ALU.logical_shift_right),
                               nm + "_vq")
                else:
                    p.selfsync(v, v.tensor_scalar(hh8[:, 0:cw], bt[sl][:, 0:cw],
                                                  4, None,
                                                  ALU.logical_shift_right), nm + "_vq")
                    p.selfsync(v, v.tensor_scalar(hh8[:, 0:cw], hh8[:, 0:cw],
                                                  shc[:, 8 + ji:9 + ji], None,
                                                  ALU.bitwise_and), nm + "_vq")
                    p.selfsync(v, v.tensor_scalar(nib[:, 0:cw], bt[sl][:, 0:cw],
                                                  shc[:, ji:ji + 1], None,
                                                  ALU.bitwise_and), nm + "_vq")
                    p.selfsync(v, v.tensor_tensor(nib[:, 0:cw], nib[:, 0:cw],
                                                  hh8[:, 0:cw], ALU.bitwise_or),
                               nm + "_vq")
                p.selfsync(v, v.tensor_scalar(an8[:, 0:cw], nib[:, 0:cw], 8, None,
                                              ALU.bitwise_and), nm + "_vq")
                p.inc(v.tensor_scalar(nib[:, 0:cw], nib[:, 0:cw], 7, None,
                                      ALU.bitwise_and), nm + "_vq1", 1,
                      mark=f"{nm}_vnib{li}")
                p.wait(v, f"{nm}_acts{li}")
                p.selfsync(v, v.tensor_tensor(f3[:, 0:cw], f3[:, 0:cw],
                                              r1[:, 0:cw], ALU.add), nm + "_vq")
                p.selfsync(v, v.tensor_tensor(f3[:, 0:cw], f3[:, 0:cw],
                                              r2[:, 0:cw], ALU.add), nm + "_vq")
                p.selfsync(v, v.tensor_tensor(f3[:, 0:cw], f3[:, 0:cw],
                                              sg[:, 0:cw], ALU.mult), nm + "_vq")
                p.wait(v, f"{nm}_sld{li}")
                if li >= 1:
                    p.wait(v, f"{nm}_ost{li - 1}")
                p.inc(v.tensor_tensor(ot[sl][:, 0:cw], f3[:, 0:cw],
                                      sct[sl][:, 0:cw], ALU.mult),
                      nm + "_vq2", 1, mark=f"{nm}_val{li}")

        @block.scalar
        def _(s_):
            p.wait(s_, nm + "_dqms")
            for li, (ji, ci, cw, (mat, s, cc_), dr, dc) in enumerate(units):
                p.wait(s_, f"{nm}_vnib{li}")
                s_.activation(f3[:, 0:cw], nib[:, 0:cw], AF.Identity)
                s_.activation(r1[:, 0:cw], nib[:, 0:cw], AF.Relu, bias=cb4[:, :])
                s_.activation(r2[:, 0:cw], nib[:, 0:cw], AF.Relu, bias=cb12[:, :],
                              scale=2.0)
                p.inc(s_.activation(sg[:, 0:cw], an8[:, 0:cw], AF.Identity,
                                    scale=-0.125, bias=cst["half_col"][:, :]),
                      nm + "_sqa", 1, mark=f"{nm}_acts{li}")


    free_marks = [f"{nm}_ost{NU - 2}", f"{nm}_ost{NU - 1}", nm + "_dqms"]

    def emit_gather_inline(gp):
        p.waits(gp, f"{nm}_ost{NU - 2}", f"{nm}_ost{NU - 1}")
        ci1 = gp.collective_compute(
            "AllGather", ALU.bypass, replica_groups=[list(range(NCORES))],
            ins=[g["gu_loc"].ap().opt()], outs=[g["wgu_g"].ap().opt()])
        p.inc(ci1, nm + "_ccg", 1, mark=nm + "_CCG")
        p.wait(gp, nm + "_CCG")
        ci2 = gp.collective_compute(
            "AllGather", ALU.bypass, replica_groups=[list(range(NCORES))],
            ins=[g["dn_loc"].ap().opt()], outs=[g["wdn_g"].ap().opt()])
        p.inc(ci2, nm + "_ccd", 1, mark=nm + "_CCD")
        return [nm + "_CCG", nm + "_CCD"]

    def emit_gather():
        out = {}

        @block.gpsimd
        def _(gp):
            out["m"] = emit_gather_inline(gp)
        return out["m"]

    return free_marks, (emit_gather, emit_gather_inline)


DQMAP16 = [(kk % 8) * 2 + (kk // 8) for kk in range(16)]


def emit_ffn(p, block, ctx, g, cst, nm, x_src, x_dst, start_mark=None,
             dequant=False, gathered=False, dq_gate=None, early_gate=None):
    nc = p.nc
    EPS = 1e-6
    NB = 4
    BT = T // NB
    NBT = BT // 128
    NI = 44
    NOC = 22
    NKG = 11

    with contextlib.ExitStack() as c1:
        if dequant:
            with contextlib.ExitStack() as cq:
                dq_mark = emit_dequant(p, block, cq, g, cst, nm + "q",
                                       start_mark=start_mark)
            ln0_start = dq_mark
        else:
            dq_mark = None
            ln0_start = start_mark

        wst = [c1.enter_context(nc.sbuf_tensor(nm + f"_ws{i}", [128, 16 * 512], BF16)) for i in range(2)]
        u_fm = c1.enter_context(nc.sbuf_tensor(nm + "_ufm", [128, ND * T], BF16))
        hbuf = c1.enter_context(nc.sbuf_tensor(nm + "_h", [128, NI * BT], BF16))
        htmp = c1.enter_context(nc.sbuf_tensor(nm + "_ht", [128, 2 * BT], BF16))
        wk = [c1.enter_context(nc.sbuf_tensor(nm + f"_wk{i}", [128, 4 * 512], BF16)) for i in range(2)]
        ybuf = c1.enter_context(nc.sbuf_tensor(nm + "_y", [128, NBT * D], BF16))
        xrt = c1.enter_context(nc.sbuf_tensor(nm + "_xr", [128, D], F32))
        stf = c1.enter_context(nc.sbuf_tensor(nm + "_st", [128, 8], F32))
        pga = [c1.enter_context(nc.psum_tensor(nm + "_pga0", [128, 512], F32))]
        pgb = [c1.enter_context(nc.psum_tensor(nm + "_pgb0", [128, 512], F32))]
        pdn = [c1.enter_context(nc.psum_tensor(nm + f"_pd{i}", [128, 512], F32)) for i in range(4)]
        DPS = [pdn[0][:, :], pdn[1][:, :], pdn[2][:, :], pdn[3][:, :]]

        with contextlib.ExitStack() as c0:
            emit_ln(p, block, c0, g, cst, nm + "l", x_src, u_fm,
                    start_mark=ln0_start, bufs=2)
        pga.append(c1.enter_context(nc.psum_tensor(nm + "_pga1", [128, 512], F32)))
        pgb.append(c1.enter_context(nc.psum_tensor(nm + "_pgb1", [128, 512], F32)))

        @block.gpsimd
        def _(gp):
            for blk in range(NB):
                for oc in range(NOC):
                    gi = blk * NOC + oc
                    if gi >= 2:
                        p.wait(gp, f"{nm}_gmm{gi - 2}")
                    else:
                        p.wait(gp, early_gate if early_gate else f"{nm}l_done")
                        p.wait(gp, dq_gate[0] if gathered else dq_gate)
                    src = g["wgu_g" if gathered else "wgu_bf"][:, oc * 512:(oc + 1) * 512]
                    srcr = src.rearrange("(a p) c -> p a c", p=128)
                    dstr = wst[gi % 2][:, :].rearrange("p (a c) -> p a c", c=512)
                    p.inc(gp.dma_start(dstr, srcr), nm + f"_wl{gi % 2}", 16,
                          mark=f"{nm}_wld{gi}")
                for q in range(4):
                    gq = blk * 4 + q
                    for kg in range(NKG):
                        gk = gq * NKG + kg
                        if gk >= 2:
                            p.wait(gp, f"{nm}_dmk{gk - 2}")
                        elif dequant:
                            p.wait(gp, dq_mark)
                        else:
                            p.wait(gp, f"{nm}l_done")
                            p.wait(gp, dq_gate[1] if gathered else dq_gate)
                        src = g["wdn_g" if gathered else "wdn_bf"][kg * 512:(kg + 1) * 512,
                                                                   q * 512:(q + 1) * 512]
                        srcr = src.rearrange("(a p) c -> p a c", p=128)
                        dstr = wk[gk % 2][:, :].rearrange("p (a c) -> p a c", c=512)
                        p.inc(gp.dma_start(dstr, srcr), nm + f"_dl{gk % 2}", 16,
                              mark=f"{nm}_wdl{gk}")

        @block.tensor
        def _(t):
            for blk in range(NB):
                if blk == 0:
                    p.wait(t, f"{nm}l_done")
                for oc in range(NOC):
                    gi = blk * NOC + oc
                    p.wait(t, f"{nm}_wld{gi}")
                    for j in range(2):
                        hs_ = (gi * 2 + j) % 2
                        for gu_i in range(2):
                            oti = 2 * j + gu_i
                            ps_ = pga[hs_] if gu_i == 0 else pgb[hs_]
                            if gi > 0:
                                p.wait(t, f"{nm}_hev{gi - 1}_{j}")
                            last = None
                            for kk in range(ND):
                                kb = DQMAP16[kk] if gathered else kk
                                last = t.matmul(
                                    ps_[:, 0:512],
                                    wst[gi % 2][:, kb * 512 + oti * 128: kb * 512 + (oti + 1) * 128],
                                    u_fm[:, kk * T + blk * BT: kk * T + blk * BT + 512],
                                    start=(kk == 0), stop=(kk == ND - 1))
                            p.inc(last, nm + "_gm", 1, mark=f"{nm}_gmmh{gi}_{j}_{gu_i}")
                    p.mark_here(nm + "_gm", f"{nm}_gmm{gi}")
                for q in range(4):
                    gq = blk * 4 + q
                    if q == 0:
                        p.wait(t, f"{nm}_hdone{blk}")
                    for kg in range(NKG):
                        gk = gq * NKG + kg
                        p.wait(t, f"{nm}_wdl{gk}")
                        if kg == 0:
                            for tt in range(NBT):
                                gti = gq * NBT + tt
                                if gti >= NBT:
                                    p.wait(t, f"{nm}_dev{gti - NBT}")
                        last = None
                        for k4 in range(4):
                            kk = kg * 4 + k4
                            for tt in range(NBT):
                                last = t.matmul(
                                    DPS[tt],
                                    hbuf[:, kk * BT + tt * 128: kk * BT + (tt + 1) * 128],
                                    wk[gk % 2][:, k4 * 512:(k4 + 1) * 512],
                                    start=(kk == 0), stop=(kk == NI - 1))
                        p.inc(last, nm + "_dk", 1, mark=f"{nm}_dmk{gk}")
                    p.mark_here(nm + "_dk", f"{nm}_dmmq{gq}")

        @block.scalar
        def _(s):
            for blk in range(NB):
                for oc in range(NOC):
                    gi = blk * NOC + oc
                    for j in range(2):
                        hs_ = (gi * 2 + j) % 2
                        p.wait(s, f"{nm}_gmmh{gi}_{j}_{0}")
                        if gi > 0:
                            p.wait(s, f"{nm}_hev{gi - 1}_{j}")
                        p.inc(s.activation(htmp[:, hs_ * BT:(hs_ + 1) * BT],
                                           pga[hs_][:, 0:512], AF.Silu),
                              nm + "_si", 1, mark=f"{nm}_silu{gi}_{j}")
                for q in range(4):
                    gq = blk * 4 + q
                    for tt in range(NBT):
                        gti = gq * NBT + tt
                        p.wait(s, f"{nm}_dmmq{gq}")
                        p.inc(s.copy(ybuf[:, tt * D + q * 512: tt * D + (q + 1) * 512],
                                     DPS[tt]),
                              nm + "_de", 1, mark=f"{nm}_dev{gti}")
                for tt in range(NBT):
                    SSF = nm + "_sch"
                    p.wait(s, f"{nm}_dev{(blk * 4 + 3) * NBT + tt}")
                    p.selfsync(s, s.activation(htmp[:, 0:BT],
                                               ybuf[:, tt * D: tt * D + 512],
                                               AF.Square, accum_out=stf[:, 0:1]), SSF)
                    p.selfsync(s, s.activation(htmp[:, 0:BT],
                                               ybuf[:, tt * D + 512: tt * D + 1024],
                                               AF.Square, accum_out=stf[:, 4:5]), SSF)
                    p.selfsync(s, s.activation(htmp[:, 0:BT],
                                               ybuf[:, tt * D + 1024: tt * D + 1536],
                                               AF.Square, accum_out=stf[:, 5:6]), SSF)
                    p.selfsync(s, s.activation(htmp[:, 0:BT],
                                               ybuf[:, tt * D + 1536: tt * D + 2048],
                                               AF.Square, accum_out=stf[:, 6:7]), SSF)
                    p.inc(s.nop(), nm + "_sqd", 1, mark=f"{nm}_ss{blk}_{tt}")
                    p.wait(s, f"{nm}_var{blk}_{tt}")
                    p.selfsync(s, s.activation(stf[:, 2:3], stf[:, 1:2], AF.Ln), SSF)
                    p.inc(s.activation(stf[:, 3:4], stf[:, 2:3], AF.Exp, scale=-0.5),
                          nm + "_rst", 1, mark=f"{nm}_rstd{blk}_{tt}")

        @block.vector
        def _(v):
            for blk in range(NB):
                for oc in range(NOC):
                    gi = blk * NOC + oc
                    for j in range(2):
                        p.waits(v, f"{nm}_silu{gi}_{j}", f"{nm}_gmmh{gi}_{j}_{1}")
                        itile = oc * 2 + j
                        hs_ = (gi * 2 + j) % 2
                        p.inc(v.tensor_tensor(hbuf[:, itile * BT:(itile + 1) * BT],
                                              htmp[:, hs_ * BT:(hs_ + 1) * BT],
                                              pgb[hs_][:, 0:512], ALU.mult),
                              nm + "_hm", 1, mark=f"{nm}_hev{gi}_{j}")
                p.mark_here(nm + "_hm", f"{nm}_hdone{blk}")
                for tt in range(NBT):
                    gti2 = blk * NBT + tt
                    SSV = nm + "_vch"
                    p.wait(v, f"{nm}_ss{blk}_{tt}")
                    p.selfsync(v, v.tensor_tensor(stf[:, 1:2], stf[:, 0:1],
                                                  stf[:, 4:5], ALU.add), SSV)
                    p.selfsync(v, v.tensor_tensor(stf[:, 7:8], stf[:, 5:6],
                                                  stf[:, 6:7], ALU.add), SSV)
                    p.selfsync(v, v.tensor_tensor(stf[:, 1:2], stf[:, 1:2],
                                                  stf[:, 7:8], ALU.add), SSV)
                    p.selfsync(v, v.tensor_scalar(stf[:, 1:2], stf[:, 1:2],
                                                  1.0 / D, EPS, ALU.mult, ALU.add),
                               SSV)
                    p.inc(v.tensor_scalar(stf[:, 4:5], stf[:, 1:2], 0.0, None,
                                          ALU.add), nm + "_vv", 1,
                          mark=f"{nm}_var{blk}_{tt}")
                    p.waits(v, f"{nm}_rstd{blk}_{tt}", f"{nm}_xld{gti2}")
                    if gti2 >= 1:
                        p.wait(v, f"{nm}_ost{gti2 - 1}")
                    p.inc(v.scalar_tensor_tensor(xrt[:, :],
                                                 ybuf[:, tt * D:(tt + 1) * D],
                                                 stf[:, 3:4], xrt[:, :],
                                                 ALU.mult, ALU.add),
                          nm + "_oa", 1, mark=f"{nm}_oadd{gti2}")

        @block.sync
        def _(sy):
            for blk in range(NB):
                for tt in range(NBT):
                    gti2 = blk * NBT + tt
                    if gti2 >= 1:
                        p.wait(sy, f"{nm}_ost{gti2 - 1}")
                    else:
                        p.wait(sy, f"{nm}l_done")
                    p.inc(sy.dma_start(xrt[:, :], x_src[ts(gti2), :]),
                          nm + "_xl", 16, mark=f"{nm}_xld{gti2}")
                    p.wait(sy, f"{nm}_oadd{gti2}")
                    p.inc(sy.dma_start(x_dst[ts(gti2), :], xrt[:, :]),
                          nm + "_xo", 16, mark=f"{nm}_ost{gti2}")

    return [f"{nm}_ost{NB * NBT - 1}"]



def emit_attn_a(p, block, ctx, g, cst, x_src, x_dst, start_mark=None,
                with_dq=False, dq_out=None):
    import os as _os
    astage = _os.environ.get("ATA_STAGE", "a3")
    ngg = int(_os.environ.get("ATA_NGG", "4"))
    aout = int(_os.environ.get("ATA_OUT", "1"))
    lvl = int(_os.environ.get("ATA_LVL", "9"))
    skips = set(_os.environ.get("ATA_SKIP", "").split(","))
    nm = "ata"
    nc = p.nc
    c1px = contextlib.ExitStack()
    ctx = c1px
    NBLK = [(i * 512, 512) for i in range(16)] + [(8192, 192)]
    XB = 515
    NEB = 33
    NFT = 32

    dttok = ctx.enter_context(nc.sbuf_tensor(nm + "_dtt", [128, 1024], F32))
    biasbc = ctx.enter_context(nc.sbuf_tensor(nm + "_bb", [128, 64], F32))
    Abc = ctx.enter_context(nc.sbuf_tensor(nm + "_ab", [128, 64], F32))
    Araw = ctx.enter_context(nc.sbuf_tensor(nm + "_ar", [128, 64], F32))
    Dbc = ctx.enter_context(nc.sbuf_tensor(nm + "_db", [128, 64], F32))
    Dexp = ctx.enter_context(nc.sbuf_tensor(nm + "_de", [128, 32], F32))
    cw_sb = ctx.enter_context(nc.sbuf_tensor(nm + "_cw", [128, 132], F32))
    cb_sb = ctx.enter_context(nc.sbuf_tensor(nm + "_cb", [128, 33], F32))

    @block.sync
    def _(sy):
        p.wait(sy, start_mark)
        p.inc(sy.dma_start(biasbc[0:1, :], g["dt_bias"][:, :]), nm + "_su", 16)
        p.inc(sy.dma_start(Araw[0:1, :], g["a_log"][:, :]), nm + "_su", 16)
        p.inc(sy.dma_start(Dbc[0:1, :], g["d_vec"][:, :]), nm + "_su", 16)
        p.inc(sy.dma_start(
            cw_sb[:, :].rearrange("p (a k) -> p a k", k=4),
            g["conv_w"][:, :].rearrange("(a p) k -> p a k", p=128)),
            nm + "_su", 16)
        p.inc(sy.dma_start(cb_sb[:, :], g["conv_b"][:, :]), nm + "_su", 16,
              mark=nm + "_su0")
        for step in range(7):
            lo, hi = 1 << step, 1 << (step + 1)
            p.wait(sy, nm + f"_dbl{step - 1}" if step else nm + "_su0")
            for tns in (biasbc, Araw, Dbc):
                p.inc(sy.dma_start(tns[lo:hi, :], tns[0:lo, :]), nm + "_dbs", 16)
            p.mark_here(nm + "_dbs", nm + f"_dbl{step}")

    @block.scalar
    def _(s):
        p.wait(s, nm + "_dbl6")
        p.inc(s.activation(Abc[:, :], Araw[:, :], AF.Exp), nm + "_sue", 1,
              mark=nm + "_aexp")

    @block.vector
    def _(v):
        p.wait(v, nm + "_aexp")
        p.selfsync(v, v.tensor_scalar(Abc[:, :], Abc[:, :], -1.0, None,
                                      ALU.mult), nm + "_suv")
        p.mark_here(nm + "_suv", nm + "_abc")
        p.wait(v, nm + "_dbl6")
        SSD = nm + "_suv"
        for ft in range(NFT):
            p.selfsync(v, v.tensor_scalar(Dexp[0:64, ft:ft + 1],
                                          Dbc[0:64, 2 * ft:2 * ft + 1],
                                          0.0, None, ALU.add), SSD)
            p.selfsync(v, v.tensor_scalar(Dexp[64:128, ft:ft + 1],
                                          Dbc[64:128, 2 * ft + 1:2 * ft + 2],
                                          0.0, None, ALU.add), SSD)
        p.wait(v, "consts")
        p.inc(v.tensor_scalar(Dexp[0:64, 0:1], Dexp[0:64, 0:1], 0.0, None,
                              ALU.add), nm + "_sud", 1, mark=nm + "_dexp")

    with contextlib.ExitStack() as c2:
        u_fm = c2.enter_context(nc.sbuf_tensor(nm + "_ufm", [128, ND * T], BF16))
        emit_ln(p, block, c2, g, cst, "atl", x_src, u_fm, rms_mode=True,
                xp_dst=g["x_res"], start_mark=start_mark, bufs=2)
        wbuf = [c2.enter_context(nc.sbuf_tensor(nm + f"_wb{i}", [128, 16 * 512], BF16)) for i in range(2)]
        dq_gate = None
        gath_inline = None
        if with_dq:
            with contextlib.ExitStack() as cdq:
                dq_free, (emit_gather, gath_inline) = emit_dequant8(
                    p, block, g, cst, "dq8", start_mark=start_mark, ctx=cdq)
            dq_out["gather"] = emit_gather
            dq_out["free"] = dq_free
            dq_out["gmarks"] = ["dq8_CCG", "dq8_CCD"]
            dq_gate = dq_free

        if astage == "a1":
            c1px.close()
            return ["atl_done", "atl_xpdone0", "atl_xpdone1"]
        xibt = c2.enter_context(nc.sbuf_tensor(nm + "_xi", [128, 16 * 512], BF16))
        ztmp = [c2.enter_context(nc.sbuf_tensor(nm + f"_zt{i}", [128, 512], BF16)) for i in range(2)]
        wot = [c2.enter_context(nc.sbuf_tensor(nm + f"_wo{i}", [128, D], F32)) for i in range(2)]
        nw_col = c2.enter_context(nc.sbuf_tensor(nm + "_nw", [128, 32], F32))
        nwt = c2.enter_context(nc.sbuf_tensor(nm + "_nwt", [32, 128], F32))
        pp = [c2.enter_context(nc.psum_tensor(nm + f"_pp{i}", [128, 512], F32)) for i in range(2)]
        pt = [c2.enter_context(nc.psum_tensor(nm + f"_pt{i}", [128, 1024], BF16)) for i in range(2)]
        ppc = c2.enter_context(nc.psum_tensor(nm + "_ppc", [128, 512], F32))

        @block.sync
        def _(sy):
            p.wait(sy, start_mark)
            p.inc(sy.dma_start(nwt[:, :], g["norm_w"][:, :]),
                  nm + "_pcn", 16, mark=nm + "_nwr")

        @block.sync
        def _(sy):
            for ib in range(8):
                n0 = ib * 512
                for cc in range(16):
                    zi = ib * 16 + cc
                    p.wait(sy, nm + f"_aze{zi}")
                    p.inc(sy.dma_start(g["z_hbm"][cc * 128:(cc + 1) * 128, n0:n0 + 512],
                                       ztmp[cc % 2][:, :]),
                          nm + f"_azq{cc % 2}", 16, mark=nm + f"_azs{zi}")
            for ib in range(8, 17):
                n0, nw = NBLK[ib]
                nft = 4 if ib < 16 else 1
                f0 = n0 - 4096
                p.wait(sy, nm + f"_atvd{ib}")
                for j in range(nft):
                    srcj = xibt[:, 0:16 * 512].rearrange(
                        "p (cc r) -> p cc r", r=512)[:, :, j * 128:(j + 1) * 128]
                    dstj = g["xbcf_hbm"][f0 + j * 128: f0 + (j + 1) * 128, :].rearrange(
                        "(o p) (cc t) -> p (o cc) t", p=128, t=128)
                    p.inc(sy.dma_start(dstj, srcj), nm + "_axq", 16)
                p.mark_here(nm + "_axq", nm + f"_axo{ib}")
            p.mark_here(nm + "_axq", nm + "_a2done")

        @block.sync
        def _(sy):
            p.waits(sy, start_mark, dq_gate)
            for et in range(32):
                if et >= 2:
                    p.wait(sy, nm + f"_pco{et - 2}")
                p.inc(sy.dma_start(wot[et % 2][:, :],
                                   g["w_outproj"][et * 128:(et + 1) * 128, :]),
                      nm + f"_pcl{et % 2}", 16, mark=nm + f"_pcl{et}")

        @block.tensor
        def _(t):
            p.waits(t, nm + "_nwr", "consts_s")
            p.inc(t.transpose(ppc[:, 0:32], nwt[:, :], cst["ident_f"][0:32, 0:32]),
                  nm + "_pct", 1, mark=nm + "_nwt")

        @block.scalar
        def _(s):
            p.wait(s, nm + "_nwt")
            p.inc(s.activation(nw_col[:, :], ppc[:, 0:32], AF.Identity),
                  nm + "_pce", 1, mark=nm + "_nwc")

        @block.vector
        def _(v):
            p.wait(v, nm + "_nwc")
            for et in range(32):
                p.wait(v, nm + f"_pcl{et}")
                if et >= 2:
                    p.wait(v, nm + f"_pco{et - 2}")
                p.inc(v.tensor_scalar(wot[et % 2][:, :], wot[et % 2][:, :],
                                      nw_col[:, et:et + 1],
                                      None, ALU.mult), nm + "_pcm", 1,
                      mark=nm + f"_pcm{et}")

        @block.gpsimd
        def _(gp):
            for ib, (n0, nw) in enumerate(NBLK):
                if ib >= 2:
                    p.wait(gp, nm + f"_aev{ib * 16 - 17}")
                else:
                    p.wait(gp, start_mark)
                src = g["w_inproj"][:, n0:n0 + nw].rearrange("(a p) c -> p a c", p=128)
                dst = wbuf[ib % 2][:, :].rearrange("p (a c) -> p a c", c=512)[:, :, 0:nw]
                p.inc(gp.dma_start(dst, src), nm + f"_awl{ib % 2}", 16,
                      mark=nm + f"_awld{ib}")

        @block.gpsimd
        def _(gp):
            for et in range(32):
                p.wait(gp, nm + f"_pcm{et}")
                p.inc(gp.dma_start(g["wo_bf"][et * 128:(et + 1) * 128, :],
                                   wot[et % 2][:, :]),
                      nm + f"_pco{et % 2}", 16, mark=nm + f"_pco{et}")
            p.mark_here(nm + "_pco0", nm + "_pcdone0")
            p.mark_here(nm + "_pco1", nm + "_pcdone")

        @block.tensor
        def _(t):
            for ib, (n0, nw) in enumerate(NBLK):
                p.wait(t, nm + f"_awld{ib}")
                for cc in range(16):
                    gi = ib * 16 + cc
                    if gi >= 2:
                        p.wait(t, nm + f"_aev{gi - 2}")
                    if ib == 0:
                        p.wait(t, f"atl_ev{cc}")
                    mm = None
                    for dt_ in range(16):
                        mm = t.matmul(pp[gi % 2][:, 0:nw],
                                      u_fm[:, dt_ * T + cc * 128: dt_ * T + (cc + 1) * 128],
                                      wbuf[ib % 2][:, dt_ * 512: dt_ * 512 + nw],
                                      start=(dt_ == 0), stop=(dt_ == 15))
                    p.inc(mm, nm + "_amm", 1, mark=nm + f"_amm{gi}")
                if ib >= 8:
                    nft = 4 if ib < 16 else 1
                    for cc in range(16):
                        for j in range(nft):
                            idx = cc * nft + j
                            p.wait(t, nm + f"_axv{ib}_{cc}")
                            if idx >= 2:
                                p.wait(t, nm + f"_atv{ib}_{idx - 2}")
                            p.inc(t.transpose(pt[idx % 2][:, 0:128],
                                              xibt[:, cc * 512 + j * 128: cc * 512 + (j + 1) * 128],
                                              cst["ident_bf"][:, :]),
                                  nm + "_atr", 1, mark=nm + f"_atr{ib}_{idx}")

        @block.scalar
        def _(s):
            for ib, (n0, nw) in enumerate(NBLK):
                nft = 4 if ib < 16 else 1
                for cc in range(16):
                    gi = ib * 16 + cc
                    p.wait(s, nm + f"_amm{gi}")
                    if ib < 8:
                        zi = ib * 16 + cc
                        if zi >= 2:
                            p.wait(s, nm + f"_azs{zi - 2}")
                        else:
                            p.wait(s, dq_gate)
                        p.inc(s.activation(ztmp[cc % 2][:, :], pp[gi % 2][:, 0:512],
                                           AF.Identity), nm + "_aze", 1,
                              mark=nm + f"_aze{zi}")
                        p.mark_here(nm + "_aze", nm + f"_aev{gi}")
                    else:
                        if ib > 8:
                            pnf = 4 if ib - 1 < 16 else 1
                            p.wait(s, nm + f"_atr{ib - 1}_{cc * pnf + pnf - 1}")
                            if cc == 0:
                                p.wait(s, nm + f"_axo{ib - 1}")
                        elif cc == 0:
                            p.wait(s, dq_gate)
                        p.inc(s.activation(xibt[:, cc * 512: cc * 512 + nft * 128],
                                           pp[gi % 2][:, 0:nft * 128], AF.Identity),
                              nm + "_axv", 1, mark=nm + f"_axv{ib}_{cc}")
                        if ib == 16:
                            p.inc(s.activation(dttok[:, cc * 64:(cc + 1) * 64],
                                               pp[gi % 2][:, 128:192], AF.Identity),
                                  nm + "_adt", 1, mark=nm + f"_adt{cc}")
                            p.mark_here(nm + "_adt", nm + f"_aev{gi}")
                        else:
                            p.mark_here(nm + "_axv", nm + f"_aev{gi}")
                if ib >= 8:
                    for cc in range(16):
                        for j in range(nft):
                            idx = cc * nft + j
                            p.wait(s, nm + f"_atr{ib}_{idx}")
                            p.inc(s.activation(xibt[:, cc * 512 + j * 128: cc * 512 + (j + 1) * 128],
                                               pt[idx % 2][:, 0:128], AF.Identity),
                                  nm + "_atv", 1, mark=nm + f"_atv{ib}_{idx}")
                    p.mark_here(nm + "_atv", nm + f"_atvd{ib}")

    if astage == "a2":
        c1px.close()
        return [nm + "_a2done", nm + "_azs126", nm + "_azs127",
                nm + "_pcdone", nm + "_pcdone0", "atl_xpdone0", "atl_xpdone1"]
    with contextlib.ExitStack() as c3:
        xbcbuf = c3.enter_context(nc.sbuf_tensor(nm + "_xc", [128, NEB * XB], BF16))
        St = c3.enter_context(nc.sbuf_tensor(nm + "_st", [64, D_INNER], F32))
        St_bf = c3.enter_context(nc.sbuf_tensor(nm + "_stb", [64, D_INNER], BF16))
        dDiag = c3.enter_context(nc.sbuf_tensor(nm + "_dd", [128, NFT * 128], BF16))
        x_tok = c3.enter_context(nc.sbuf_tensor(nm + "_xt", [128, D_INNER], BF16))
        xsc = c3.enter_context(nc.sbuf_tensor(nm + "_xs", [128, D_INNER], BF16))
        y_sb = c3.enter_context(nc.sbuf_tensor(nm + "_ys", [128, D_INNER], BF16))
        yn_fm = c3.enter_context(nc.sbuf_tensor(nm + "_yf", [128, NFT * 512], BF16))
        zbuf = c3.enter_context(nc.sbuf_tensor(nm + "_zb", [128, D_INNER], BF16))
        isb = c3.enter_context(nc.sbuf_tensor(nm + "_is", [128, 512], BF16))
        M0 = c3.enter_context(nc.sbuf_tensor(nm + "_m0", [128, 256], BF16))
        Mh = c3.enter_context(nc.sbuf_tensor(nm + "_mh", [128, 256], BF16))
        G_sb = c3.enter_context(nc.sbuf_tensor(nm + "_gs", [128, 128], BF16))
        btok = c3.enter_context(nc.sbuf_tensor(nm + "_bt", [128, 64], BF16))
        C0 = c3.enter_context(nc.sbuf_tensor(nm + "_c0", [64, 128], BF16))
        sgt2 = c3.enter_context(nc.sbuf_tensor(nm + "_sg", [128, 128], F32))
        sgn2 = c3.enter_context(nc.sbuf_tensor(nm + "_sn", [128, 128], F32))
        esg2 = c3.enter_context(nc.sbuf_tensor(nm + "_es", [128, 128], F32))
        esl2 = c3.enter_context(nc.sbuf_tensor(nm + "_el", [128, 128], F32))
        T2p2 = c3.enter_context(nc.sbuf_tensor(nm + "_t2", [128, 128], F32))
        dtA2 = c3.enter_context(nc.sbuf_tensor(nm + "_da", [128, 128], F32))
        dtv2 = c3.enter_context(nc.sbuf_tensor(nm + "_dv", [128, 128], F32))
        sp1 = c3.enter_context(nc.sbuf_tensor(nm + "_s1", [128, 64], F32))
        sp2 = c3.enter_context(nc.sbuf_tensor(nm + "_s2", [128, 64], F32))
        spm = c3.enter_context(nc.sbuf_tensor(nm + "_s3", [128, 64], F32))
        spe = c3.enter_context(nc.sbuf_tensor(nm + "_s4", [128, 64], F32))
        spl = c3.enter_context(nc.sbuf_tensor(nm + "_s5", [128, 64], F32))
        tm1 = c3.enter_context(nc.sbuf_tensor(nm + "_t1", [128, 64], F32))
        tm2 = c3.enter_context(nc.sbuf_tensor(nm + "_t3", [128, 64], F32))
        sT2 = c3.enter_context(nc.sbuf_tensor(nm + "_sT", [64, 256], F32))
        wos = [c3.enter_context(nc.sbuf_tensor(nm + f"_ws{i}", [128, 32 * 128], BF16)) for i in range(2)]
        otok = [c3.enter_context(nc.sbuf_tensor(nm + f"_ot{i}", [128, 1024], F32)) for i in range(4)]
        xresh = c3.enter_context(nc.sbuf_tensor(nm + "_xr", [128, 1024], F32))
        ofm = c3.enter_context(nc.sbuf_tensor(nm + "_of", [128, 512], BF16))
        sgt, sgn, esg, esl = (sgt2[:, 0:64], sgn2[:, 0:64], esg2[:, 0:64],
                              esl2[:, 0:64])
        T2p, dtA, dtv = T2p2[:, 0:64], dtA2[:, 0:64], dtv2[:, 0:64]
        sT = sT2[:, 0:128]
        sR = c3.enter_context(nc.sbuf_tensor(nm + "_sR", [1, 4096], F32))
        stf = c3.enter_context(nc.sbuf_tensor(nm + "_sf", [128, 8], F32))
        dcv = [c3.enter_context(nc.sbuf_tensor(nm + f"_dc{i}", [128, 512], BF16)) for i in range(2)]
        pb0 = c3.enter_context(nc.psum_tensor(nm + "_p0", [128, 512], F32))
        pb1 = c3.enter_context(nc.psum_tensor(nm + "_p1", [128, 512], F32))
        pb2 = c3.enter_context(nc.psum_tensor(nm + "_p2", [128, 1024], BF16))
        pb3 = c3.enter_context(nc.psum_tensor(nm + "_p3", [128, 1024], BF16))
        pb4 = c3.enter_context(nc.psum_tensor(nm + "_p4", [128, 512], F32))
        pb5 = c3.enter_context(nc.psum_tensor(nm + "_p5", [128, 512], F32))
        pb6 = c3.enter_context(nc.psum_tensor(nm + "_p6", [128, 512], F32))
        pb7 = c3.enter_context(nc.psum_tensor(nm + "_p7", [128, 512], F32))

        def ebcol(eb, k):
            return xbcbuf[:, eb * XB + k: eb * XB + k + 512]

        def ebtile(eb, lc, rows=slice(0, 128)):
            c0 = eb * XB + 3 + lc * 128
            return xbcbuf[rows, c0:c0 + 128]

        @block.gpsimd
        def _(gp):
            for gg in range(ngg):
                if gg == 0:
                    p.waits(gp, nm + "_a2done", nm + "_hst",
                            "atl_xpdone0", "atl_xpdone1", nm + "_pcdone", nm + "_pcdone0",
                            nm + "_azs126", nm + "_azs127")
                else:
                    p.wait(gp, nm + f"_yev{gg * 4 - 1}")
                c0 = gg * 512 - 3
                off = 0 if gg else 3
                ncols = 515 if gg else 512
                srcd = g["xbcf_hbm"][:, max(c0, 0):gg * 512 + 512].rearrange(
                    "(a p) c -> p a c", p=128)
                dst = xbcbuf[:, :].rearrange("p (a c) -> p a c", c=XB)[:, :, off:off + ncols]
                p.inc(gp.dma_start(dst, srcd), nm + "_gld", 16, mark=nm + f"_gld{gg}")
                if gg == 0 and gath_inline is not None:
                    gath_inline(gp)
                for dh in range(2 if aout else 0):
                    for db in range(8):
                        gwo = gg * 16 + dh * 8 + db
                        if gwo < 2:
                            p.waits(gp, nm + "_pcdone", nm + "_pcdone0", nm + "_a2done",
                                    "atl_xpdone0", "atl_xpdone1")
                        else:
                            p.wait(gp, nm + f"_opm{gwo - 2}")
                        dcol = (dh * 8 + db) * 128
                        srcw = g["wo_bf"][:, dcol:dcol + 128].rearrange(
                            "(a p) c -> p a c", p=128)
                        dstw = wos[gwo % 2][:, :].rearrange("p (a c) -> p a c", c=128)
                        p.inc(gp.dma_start(dstw, srcw), nm + f"_wol{gwo % 2}", 16,
                              mark=nm + f"_wol{gwo}")

        @block.sync
        def _(sy):
            for gg in range(ngg):
                for lc in range(4):
                    cc = gg * 4 + lc
                    p.wait(sy, nm + f"_azs{7 * 16 + cc}")
                    if cc:
                        p.wait(sy, nm + f"_zsl{cc - 1}")
                    else:
                        p.waits(sy, nm + "_a2done", "atl_xpdone0",
                                "atl_xpdone1", nm + "_pcdone", nm + "_pcdone0",
                                nm + "_azs126", nm + "_azs127")
                    if lvl >= 5:
                        p.inc(sy.dma_start(zbuf[:, :], g["z_hbm"][cc * 128:(cc + 1) * 128, :]),
                              nm + "_zld", 16, mark=nm + f"_zld{cc}")
                    if cc == 0:
                        p.waits(sy, nm + "_sTe0", dq_gate)
                        p.inc(sy.dma_start(g["sg_hbm"][:, :],
                                           sT2[:, 0:128]),
                              nm + "_sgw", 16, mark=nm + "_sgw0")
                        p.wait(sy, nm + "_sgw0")
                        p.inc(sy.dma_start(
                            sR[0:1, :],
                            g["sg_hbm"][0:32, :].rearrange("(o h) t -> o (h t)", o=1)),
                            nm + "_sRl", 16, mark=nm + "_sR0_0")
                    p.wait(sy, nm + f"_bcm{cc * 16 + 7}")
                    p.inc(sy.dma_start(
                        sR[0:1, :],
                        g["sg_hbm"][32:64, :].rearrange("(o h) t -> o (h t)", o=1)),
                        nm + "_sRl", 16, mark=nm + f"_sR{cc}_1")
                    if cc + 1 < 16:
                        nxp = ((cc + 1) % 2) * 128
                        p.waits(sy, nm + f"_sTe{cc + 1}", nm + f"_sR{cc}_1")
                        p.inc(sy.dma_start(g["sg_hbm"][:, :],
                                           sT2[:, nxp:nxp + 128]),
                              nm + "_sgw", 16, mark=nm + f"_sgw{cc + 1}")
                        p.waits(sy, nm + f"_sgw{cc + 1}", nm + f"_bcm{cc * 16 + 15}")
                        p.inc(sy.dma_start(
                            sR[0:1, :],
                            g["sg_hbm"][0:32, :].rearrange("(o h) t -> o (h t)", o=1)),
                            nm + "_sRl", 16, mark=nm + f"_sR{cc + 1}_0")
                for dh in range(2 if aout else 0):
                    for lc in range(4):
                        cc = gg * 4 + lc
                        oi = (gg * 2 + dh) * 4 + lc
                        p.wait(sy, f"atl_xp{cc}")
                        if oi >= 1:
                            p.wait(sy, nm + f"_ora{oi - 1}")
                        else:
                            p.waits(sy, nm + "_a2done", "atl_xpdone0",
                                    "atl_xpdone1", nm + "_pcdone", nm + "_pcdone0",
                                    nm + "_azs126", nm + "_azs127")
                        p.inc(sy.dma_start(xresh[:, :],
                                           g["x_res"][cc * 128:(cc + 1) * 128,
                                                      dh * 1024:(dh + 1) * 1024]),
                              nm + "_xrl", 16, mark=nm + f"_xrl{oi}")
                        p.wait(sy, nm + f"_ora{oi}")
                        p.inc(sy.dma_start(x_dst[cc * 128:(cc + 1) * 128,
                                                 dh * 1024:(dh + 1) * 1024],
                                           otok[lc][:, :]),
                              nm + f"_osl{lc}", 16, mark=nm + f"_ost{oi}")

        def SL(n):
            po = (n % 2) * 64
            return (sgt2[:, po:po + 64], sgn2[:, po:po + 64],
                    esg2[:, po:po + 64], esl2[:, po:po + 64],
                    T2p2[:, po:po + 64], dtA2[:, po:po + 64],
                    dtv2[:, po:po + 64],
                    sT2[:, (n % 2) * 128:(n % 2) * 128 + 128])

        SS = nm + "_vch"
        SSA = nm + "_sch"

        def dve_sp1(v, n):
            p.waits(v, nm + f"_adt{n}", nm + "_dbl6")
            p.selfsync(v, v.tensor_tensor(sp1[:, :], dttok[:, n * 64:(n + 1) * 64],
                                          biasbc[:, :], ALU.add), SS)
            p.selfsync(v, v.tensor_scalar(sp2[:, :], sp1[:, :], -1.0, None,
                                          ALU.mult), SS)
            p.inc(v.tensor_tensor(sp2[:, :], sp1[:, :], sp2[:, :], ALU.min),
                  nm + "_vsp", 1, mark=nm + f"_vsp{n}")

        def dve_dtv(v, n):
            _, _, _, _, _, dtA_, dtv_, _ = SL(n)
            p.waits(v, nm + f"_aspl{n}", nm + f"_aspm{n}")
            p.selfsync(v, v.tensor_tensor(dtv_, spm[:, :], spl[:, :],
                                          ALU.add), SS)
            p.wait(v, nm + "_abc")
            if n >= 2:
                p.wait(v, nm + f"_sgm{n - 2}")
            p.inc(v.tensor_tensor(dtA_, dtv_, Abc[:, :], ALU.mult),
                  nm + "_vda", 1, mark=nm + f"_vda{n}")

        def dve_t2(v, n):
            sgt_, _, _, _, T2p_, _, dtv_, _ = SL(n)
            p.waits(v, nm + f"_slm{n}", nm + f"_ael{n}")
            p.inc(v.tensor_tensor(tm1[:, :], pb1[:, 64:128], sgt_,
                                  ALU.subtract), nm + "_vt2", 1,
                  mark=nm + f"_vt2a{n}")
            p.wait(v, nm + f"_at2{n}")
            p.inc(v.tensor_tensor(T2p_, tm2[:, :], dtv_, ALU.mult),
                  nm + "_vt2", 1, mark=nm + f"_vt2{n}")

        def pe_sigma(t, n):
            sgt_, _, _, _, _, dtA_, _, _ = SL(n)
            p.wait(t, nm + f"_vda{n}")
            if n >= 1:
                p.wait(t, nm + f"_agp{n - 1}")
            if n >= 2:
                p.wait(t, nm + f"_vye{n - 2}_7")
            p.inc(t.matmul(pb1[:, 0:64], cst["ltri_f"][:, :], dtA_,
                           start=True, stop=True),
                  nm + "_sgm", 1, mark=nm + f"_sgm{n}")
            p.waits(t, nm + f"_asg{n}", nm + f"_aes{n}", nm + f"_asn{n}")
            p.inc(t.transpose(pb1[0:64, 256:384], sgt_, cst["ident_f"][:, :]),
                  nm + "_sTm", 1, mark=nm + f"_sTm{n}")
            p.wait(t, nm + f"_sTe{n}")
            p.inc(t.matmul(pb1[:, 64:128], cst["r127_f"][:, :], sgt_,
                           start=True, stop=True),
                  nm + "_slm", 1, mark=nm + f"_slm{n}")

        def act_sp(s, n):
            p.wait(s, nm + f"_vsp{n}")
            p.selfsync(s, s.activation(spm[:, :], sp1[:, :], AF.Relu), SSA)
            p.mark_here(SSA, nm + f"_aspm{n}")
            p.selfsync(s, s.activation(spe[:, :], sp2[:, :], AF.Exp), SSA)
            p.inc(s.activation(spl[:, :], spe[:, :], AF.Ln,
                               bias=cst["ones_colf"][:, 0:1]),
                  nm + "_asl", 1, mark=nm + f"_aspl{n}")

        def act_sgev(s, n):
            sgt_, sgn_, esg_, esl_, _, _, _, sT_ = SL(n)
            p.wait(s, nm + f"_sgm{n}")
            p.inc(s.activation(sgt_, pb1[:, 0:64], AF.Identity),
                  nm + "_asg", 1, mark=nm + f"_asg{n}")
            p.inc(s.activation(sgn_, pb1[:, 0:64], AF.Identity, scale=-1.0),
                  nm + "_asn", 1, mark=nm + f"_asn{n}")
            p.inc(s.activation(esg_, pb1[:, 0:64], AF.Exp),
                  nm + "_aes", 1, mark=nm + f"_aes{n}")
            p.wait(s, nm + f"_sTm{n}")
            if n >= 2:
                p.wait(s, nm + f"_sgw{n - 2}")
            p.inc(s.activation(sT_, pb1[0:64, 256:384], AF.Identity),
                  nm + "_sTe", 1, mark=nm + f"_sTe{n}")
            p.wait(s, nm + f"_slm{n}")
            if n >= 2:
                p.wait(s, nm + f"_vsb{n - 2}_7")
            p.inc(s.activation(esl_, pb1[:, 64:128], AF.Exp),
                  nm + "_ael", 1, mark=nm + f"_ael{n}")

        def act_at2(s, n):
            p.wait(s, nm + f"_vt2a{n}")
            p.inc(s.activation(tm2[:, :], tm1[:, :], AF.Exp),
                  nm + "_at2", 1, mark=nm + f"_at2{n}")

        @block.vector
        def _(v):
            p.waits(v, nm + "_a2done", "atl_xpdone0", "atl_xpdone1",
                    nm + "_pcdone", nm + "_pcdone0", nm + "_azs126", nm + "_azs127")
            for eb in range(NEB):
                p.inc(v.memset(xbcbuf[:, eb * XB: eb * XB + 3], 0.0), nm + "_hs", 1)
            p.mark_here(nm + "_hs", nm + "_hst")
            v.memset(St[:, :], 0.0)
            p.inc(v.memset(St_bf[:, :], 0.0), nm + "_suv", 1, mark=nm + "_st0")
            p.wait(v, nm + "_dexp")
            for ft in range(NFT):
                p.inc(v.tensor_scalar(dDiag[:, ft * 128:(ft + 1) * 128],
                                      cst["ident_bf"][:, :], Dexp[:, ft:ft + 1],
                                      None, ALU.mult), nm + "_sud", 1)
            p.mark_here(nm + "_sud", nm + "_ddg")
            dve_sp1(v, 0)
            dve_dtv(v, 0)
            for gg in range(ngg):
                for eb in range(NEB):
                    ci = gg * NEB + eb
                    if ci >= 2:
                        p.wait(v, nm + f"_cvm{ci - 2}")
                    elif ci == 0:
                        p.wait(v, nm + "_su0")
                    for k in range(4):
                        ins = v.tensor_scalar(dcv[eb % 2][:, k * 128:(k + 1) * 128],
                                              cst["ident_bf"][:, :],
                                              cw_sb[:, eb * 4 + k: eb * 4 + k + 1],
                                              None, ALU.mult)
                    p.inc(ins, nm + "_dcb", 1, mark=nm + f"_dcb{gg * NEB + eb}")
                for lc in range(4):
                    cc = gg * 4 + lc
                    _, _, esg_, esl_, T2p_, _, dtv_, _ = SL(cc)
                    if cc + 1 < 16:
                        dve_sp1(v, cc + 1)
                    if cc == 0:
                        dve_t2(v, 0)
                    p.wait(v, nm + f"_vt2{cc}")
                    for h in range(64):
                        if h == 0:
                            for ft in range(NFT):
                                p.wait(v, nm + f"_xtk{cc}_{ft}")
                            if cc:
                                p.waits(v, nm + f"_ctm{cc - 1}_7",
                                        nm + f"_ynt{cc - 1}_{NFT - 1}")
                        ins = v.tensor_scalar(xsc[:, h * 64:(h + 1) * 64],
                                              x_tok[:, h * 64:(h + 1) * 64],
                                              T2p_[:, h:h + 1], None, ALU.mult)
                    p.inc(ins, nm + "_vxs", 1, mark=nm + f"_vxs{cc}")
                    if cc + 1 < 16:
                        dve_dtv(v, cc + 1)
                    for g8 in range(8):
                        for hh in range(8):
                            h = g8 * 8 + hh
                            p.waits(v, nm + f"_am0{cc}_{h}", nm + f"_agp{cc}")
                            if h >= 2:
                                p.wait(v, nm + f"_inm{cc}_{h - 2}")
                            p.inc(v.scalar_tensor_tensor(Mh[:, (h % 2) * 128:(h % 2 + 1) * 128],
                                                         M0[:, (h % 2) * 128:(h % 2 + 1) * 128],
                                                         dtv_[:, h:h + 1], G_sb[:, :],
                                                         ALU.mult, ALU.mult),
                                  nm + "_vmh", 1, mark=nm + f"_vmh{cc}_{h}")
                        p.waits(v, nm + f"_iev{cc}_{g8}", nm + f"_inm{cc}_{g8 * 8 + 7}")
                        p.wait(v, nm + f"_aes{cc}")
                        ins = None
                        for hh in range(8):
                            h = g8 * 8 + hh
                            ins = v.scalar_tensor_tensor(y_sb[:, h * 64:(h + 1) * 64],
                                                         isb[:, hh * 64:(hh + 1) * 64],
                                                         esg_[:, h:h + 1],
                                                         pb7[:, hh * 64:(hh + 1) * 64],
                                                         ALU.mult, ALU.add)
                        p.inc(ins, nm + "_vye", 1, mark=nm + f"_vye{cc}_{g8}")
                        p.waits(v, nm + f"_ctm{cc}_{g8}", nm + f"_ael{cc}")
                        ins = None
                        for hh in range(8):
                            h = g8 * 8 + hh
                            ins = v.scalar_tensor_tensor(St[0:64, h * 64:(h + 1) * 64],
                                                         St[0:64, h * 64:(h + 1) * 64],
                                                         esl_[0:64, h:h + 1],
                                                         pb0[0:64, hh * 64:(hh + 1) * 64],
                                                         ALU.mult, ALU.add)
                        p.selfsync(v, ins, SS)
                        p.inc(v.tensor_scalar(St_bf[0:64, g8 * 512:(g8 + 1) * 512],
                                              St[0:64, g8 * 512:(g8 + 1) * 512],
                                              0.0, None, ALU.add),
                              nm + "_vsb", 1, mark=nm + f"_vsb{cc}_{g8}")
                    p.mark_here(nm + "_vye", nm + f"_yev{cc}")
                    p.waits(v, nm + f"_zsl{cc}", nm + f"_yev{cc}")
                    p.selfsync(v, v.tensor_tensor(x_tok[:, :], x_tok[:, :],
                                                  y_sb[:, :], ALU.mult), SS)
                    p.mark_here(nm + "_vch", nm + f"_vyg{cc}")
                    p.wait(v, nm + f"_asq{cc}")
                    p.selfsync(v, v.tensor_scalar(stf[:, 1:2], stf[:, 0:1],
                                                  1.0 / D_INNER, 1e-5, ALU.mult,
                                                  ALU.add), SS)
                    p.inc(v.tensor_scalar(stf[:, 6:7], stf[:, 1:2], 0.0, None,
                                          ALU.add), nm + "_vrv", 1,
                          mark=nm + f"_vrv{cc}")
                    p.wait(v, nm + f"_ars{cc}")
                    p.inc(v.tensor_scalar(xsc[:, :], x_tok[:, :], stf[:, 3:4],
                                          None, ALU.mult), nm + "_vyn", 1,
                          mark=nm + f"_vyn{cc}")
                    if cc + 1 < 16:
                        dve_t2(v, cc + 1)
                for dh in range(2 if aout else 0):
                    for lc in range(4):
                        oi = (gg * 2 + dh) * 4 + lc
                        p.waits(v, nm + f"_xrl{oi}",
                                nm + f"_oev{gg * 16 + dh * 8 + 7}_{lc}")
                        p.inc(v.tensor_tensor(otok[lc][:, :], otok[lc][:, :],
                                              xresh[:, :], ALU.add),
                              nm + "_ora", 1, mark=nm + f"_ora{oi}")

        @block.tensor
        def _(t):
            p.wait(t, nm + "_a2done")
            pe_sigma(t, 0)
            for gg in range(ngg):
                for eb in range(NEB):
                    ci = gg * NEB + eb
                    p.waits(t, nm + f"_gld{gg}", nm + f"_dcb{ci}")
                    if ci >= 2:
                        p.wait(t, nm + f"_cve{ci - 2}")
                    if gg and eb == 0:
                        p.wait(t, nm + f"_vsb{gg * 4 - 1}_7")
                    cvp = [pb0, pb4][ci % 2]
                    mm = None
                    for k in range(4):
                        mm = t.matmul(cvp[:, 0:512], dcv[eb % 2][:, k * 128:(k + 1) * 128],
                                      ebcol(eb, k), start=(k == 0), stop=(k == 3))
                    p.inc(mm, nm + "_cvm", 1, mark=nm + f"_cvm{ci}")
                for lc in range(4):
                    cc = gg * 4 + lc
                    for ft in range(NFT):
                        p.wait(t, nm + f"_cve{gg * NEB + ft}")
                        if ft >= 2:
                            p.wait(t, nm + f"_xtk{cc}_{ft - 2}")
                        elif cc:
                            p.wait(t, nm + f"_xtk{cc - 1}_{NFT - 2 + ft}")
                            if ft == 0:
                                p.waits(t, nm + f"_bte{cc - 1}",
                                        nm + f"_yfe{cc - 1}_{NFT - 2}")
                            else:
                                p.wait(t, nm + f"_yfe{cc - 1}_{NFT - 1}")
                        p.inc(t.transpose([pb2, pb3][ft % 2][:, 0:128],
                                          ebtile(ft, lc), cst["ident_bf"][:, :]),
                              nm + "_xtr", 1, mark=nm + f"_xtr{cc}_{ft}")
                    p.waits(t, nm + f"_cve{gg * NEB + 32}", nm + f"_xtk{cc}_{30}")
                    if cc:
                        p.wait(t, nm + f"_ctm{cc - 1}_7")
                    p.inc(t.transpose(pb2[:, 128:192], ebtile(32, lc, slice(0, 64)),
                                      cst["ident_bf"][0:64, 0:64]),
                          nm + "_btr", 1, mark=nm + f"_btr{cc}")
                    p.waits(t, nm + f"_vt2a{cc}", nm + f"_ael{cc}")
                    p.wait(t, nm + f"_ccp{cc}")
                    if cc:
                        p.wait(t, nm + f"_agp{cc - 1}")
                    p.inc(t.matmul(pb1[:, 128:256], ebtile(32, lc, slice(0, 64)),
                                   C0[:, :], start=True, stop=True),
                          nm + "_gpm", 1, mark=nm + f"_gpm{cc}")
                    if cc + 1 < 16:
                        pe_sigma(t, cc + 1)
                    for g8 in range(8):
                        for half in range(2):
                            hg4 = g8 * 2 + half
                            bi = cc * 16 + hg4
                            p.wait(t, nm + f"_sR{cc}_{hg4 // 8}")
                            if lc == 0 and hg4 == 0:
                                p.wait(t, nm + f"_cve{gg * NEB + NEB - 1}")
                            if bi:
                                p.wait(t, nm + f"_am0{cc}_{hg4 * 4 - 1}"
                                       if hg4 else nm + f"_am0{cc - 1}_{63}")
                            mm = t.matmul(pb4[:, 0:512], cst["ones_f"][0:1, 0:128],
                                          sR[0:1, (hg4 % 8) * 512:
                                             (hg4 % 8 + 1) * 512],
                                          start=True, stop=False)
                            for j in range(4):
                                mm = t.matmul(pb4[:, j * 128:(j + 1) * 128],
                                              cst["masku_bf"][:, :],
                                              cst["ident_bf"][:, :],
                                              start=False, stop=(j == 3))
                            p.inc(mm, nm + "_bcm", 1, mark=nm + f"_bcm{cc * 16 + hg4}")
                        p.wait(t, nm + f"_vsb{cc - 1}_{g8}" if cc else nm + "_st0")
                        if g8:
                            p.wait(t, nm + f"_iev{cc}_{g8 - 1}")
                        elif cc:
                            p.wait(t, nm + f"_iev{cc - 1}_{7}")
                        p.wait(t, nm + f"_ccp{cc}")
                        p.inc(t.matmul(pb6[:, 0:512], C0[:, :],
                                       St_bf[0:64, g8 * 512:(g8 + 1) * 512],
                                       start=True, stop=True),
                              nm + "_itm", 1, mark=nm + f"_itm{cc}_{g8}")
                        if g8:
                            p.wait(t, nm + f"_vye{cc}_{g8 - 1}")
                        elif cc:
                            p.wait(t, nm + f"_vye{cc - 1}_{7}")
                        p.wait(t, nm + "_ddg")
                        p.wait(t, nm + f"_cve{gg * NEB + g8 * 4 + 3}")
                        mm = None
                        for q in range(4):
                            ft = g8 * 4 + q
                            mm = t.matmul(pb7[:, q * 128:(q + 1) * 128],
                                          ebtile(ft, lc),
                                          dDiag[:, ft * 128:(ft + 1) * 128],
                                          start=(q == 0), stop=False)
                        p.inc(mm, nm + "_ddm", 1)
                        for hh in range(8):
                            h = g8 * 8 + hh
                            p.waits(t, nm + f"_vmh{cc}_{h}", nm + f"_xtk{cc}_{h // 2}")
                            p.inc(t.matmul(pb7[:, hh * 64:(hh + 1) * 64],
                                           Mh[:, (h % 2) * 128:(h % 2 + 1) * 128],
                                           x_tok[:, h * 64:(h + 1) * 64],
                                           start=False, stop=(hh == 7)),
                                  nm + "_inm", 1, mark=nm + f"_inm{cc}_{h}")
                        p.waits(t, nm + f"_vxs{cc}", nm + f"_bte{cc}")
                        if g8:
                            p.wait(t, nm + f"_vsb{cc}_{g8 - 1}")
                        else:
                            p.wait(t, nm + f"_cve{gg * NEB + NEB - 1}")
                        p.inc(t.matmul(pb0[0:64, 0:512], btok[:, :],
                                       xsc[:, g8 * 512:(g8 + 1) * 512],
                                       start=True, stop=True),
                              nm + "_ctm", 1, mark=nm + f"_ctm{cc}_{g8}")
                    for ft in range(NFT):
                        p.wait(t, nm + f"_vyn{cc}")
                        if ft >= 2:
                            p.wait(t, nm + f"_yfe{cc}_{ft - 2}")
                        elif cc:
                            p.wait(t, nm + f"_yfe{cc - 1}_{NFT - 2 + ft}")
                        if cc >= 4 and ft < 2 and aout:
                            p.wait(t, nm + f"_opm{(cc // 4 - 1) * 16 + 15}")
                        p.inc(t.transpose([pb2, pb3][ft % 2][:, 0:128],
                                          xsc[:, ft * 128:(ft + 1) * 128],
                                          cst["ident_bf"][:, :]),
                              nm + "_ynt", 1, mark=nm + f"_ynt{cc}_{ft}")
                for dh in range(2 if aout else 0):
                    for db in range(8):
                        gwo = gg * 16 + dh * 8 + db
                        p.wait(t, nm + f"_wol{gwo}")
                        if db == 0 and dh == 0:
                            for ft in range(NFT):
                                p.wait(t, nm + f"_yfe{gg * 4 + 3}_{ft}")
                        if gwo:
                            p.wait(t, nm + f"_oev{gwo - 1}_{3}")
                        mm = None
                        for et in range(NFT):
                            mm = t.matmul(pb5[:, 0:512],
                                          wos[gwo % 2][:, et * 128:(et + 1) * 128],
                                          yn_fm[:, et * 512:(et + 1) * 512],
                                          start=(et == 0), stop=(et == NFT - 1))
                        p.inc(mm, nm + "_opm", 1, mark=nm + f"_opm{gwo}")
                        p.wait(t, nm + f"_ofe{gwo}")
                        for lc in range(4):
                            if lc:
                                p.wait(t, nm + f"_oev{gwo}_{lc - 1}")
                            mm = t.matmul(pb5[:, lc * 128:(lc + 1) * 128],
                                          ofm[:, lc * 128:(lc + 1) * 128],
                                          cst["ident_bf"][:, :],
                                          start=True, stop=True)
                            p.inc(mm, nm + "_obt", 1, mark=nm + f"_obt{gwo}_{lc}")

        @block.scalar
        def _(s):
            act_sp(s, 0)
            act_sgev(s, 0)
            for gg in range(ngg):
                for eb in range(NEB):
                    ci = gg * NEB + eb
                    p.waits(s, nm + f"_cvm{ci}", nm + "_su0")
                    p.inc(s.activation(xbcbuf[:, eb * XB + 3: eb * XB + 515],
                                       [pb0, pb4][ci % 2][:, 0:512], AF.Silu,
                                       bias=cb_sb[:, eb:eb + 1]),
                          nm + "_cve", 1, mark=nm + f"_cve{ci}")
                for lc in range(4):
                    cc = gg * 4 + lc
                    sgt_, sgn_, _, _, _, _, _, _ = SL(cc)
                    if cc == 0:
                        act_at2(s, 0)
                    for ft in range(NFT):
                        p.wait(s, nm + f"_xtr{cc}_{ft}")
                        if ft == 0 and cc:
                            p.wait(s, nm + f"_vyn{cc - 1}")
                        p.inc(s.activation(x_tok[:, ft * 128:(ft + 1) * 128],
                                           [pb2, pb3][ft % 2][:, 0:128], AF.Identity),
                              nm + "_xtk", 1, mark=nm + f"_xtk{cc}_{ft}")
                    p.wait(s, nm + f"_btr{cc}")
                    p.inc(s.activation(btok[:, :], pb2[:, 128:192], AF.Identity),
                          nm + "_bte", 1, mark=nm + f"_bte{cc}")
                    if cc:
                        p.wait(s, nm + f"_itm{cc - 1}_7")
                    p.inc(s.copy(C0[:, :], ebtile(32, lc, slice(64, 128))),
                          nm + "_ccp", 1, mark=nm + f"_ccp{cc}")
                    p.wait(s, nm + f"_gpm{cc}")
                    p.inc(s.activation(G_sb[:, :], pb1[:, 128:256], AF.Identity),
                          nm + "_agp", 1, mark=nm + f"_agp{cc}")
                    if cc + 1 < 16:
                        act_sp(s, cc + 1)
                        act_sgev(s, cc + 1)
                    for g8 in range(8):
                        for half in range(2):
                            hg4 = g8 * 2 + half
                            p.wait(s, nm + f"_bcm{cc * 16 + hg4}")
                            if hg4 == 0:
                                p.wait(s, nm + f"_asn{cc}")
                            for hh in range(4):
                                h = hg4 * 4 + hh
                                if h >= 2:
                                    p.wait(s, nm + f"_vmh{cc}_{h - 2}")
                                elif cc:
                                    p.wait(s, nm + f"_vmh{cc - 1}_{62 + h}")
                                p.inc(s.activation(M0[:, (h % 2) * 128:(h % 2 + 1) * 128],
                                                   pb4[:, hh * 128:(hh + 1) * 128],
                                                   AF.Exp, bias=sgn_[:, h:h + 1]),
                                      nm + "_am0", 1, mark=nm + f"_am0{cc}_{h}")
                        p.wait(s, nm + f"_itm{cc}_{g8}")
                        if g8:
                            p.wait(s, nm + f"_vye{cc}_{g8 - 1}")
                        elif cc:
                            p.wait(s, nm + f"_vye{cc - 1}_{7}")
                        p.inc(s.activation(isb[:, :], pb6[:, 0:512], AF.Identity),
                              nm + "_iev", 1, mark=nm + f"_iev{cc}_{g8}")
                    p.waits(s, nm + f"_zld{cc}", nm + f"_inm{cc}_{63}",
                            nm + f"_vxs{cc}")
                    p.wait(s, nm + f"_ctm{cc}_{7}")
                    p.inc(s.activation(x_tok[:, :], zbuf[:, :], AF.Silu),
                          nm + "_zsl", 1, mark=nm + f"_zsl{cc}")
                    p.wait(s, nm + f"_vyg{cc}")
                    p.selfsync(s, s.activation(xsc[:, :], x_tok[:, :], AF.Square,
                                               accum_out=stf[:, 0:1]), SSA)
                    p.mark_here(SSA, nm + f"_asq{cc}")
                    p.wait(s, nm + f"_vrv{cc}")
                    p.selfsync(s, s.activation(stf[:, 2:3], stf[:, 6:7], AF.Ln), SSA)
                    p.inc(s.activation(stf[:, 3:4], stf[:, 2:3], AF.Exp, scale=-0.5),
                          nm + "_ars", 1, mark=nm + f"_ars{cc}")
                    for ft in range(NFT):
                        p.wait(s, nm + f"_ynt{cc}_{ft}")
                        p.inc(s.activation(yn_fm[:, ft * 512 + lc * 128: ft * 512 + (lc + 1) * 128],
                                           [pb2, pb3][ft % 2][:, 0:128],
                                           AF.Identity),
                              nm + "_yfe", 1, mark=nm + f"_yfe{cc}_{ft}")
                    if cc + 1 < 16:
                        act_at2(s, cc + 1)
                for dh in range(2 if aout else 0):
                    for db in range(8):
                        gwo = gg * 16 + dh * 8 + db
                        p.wait(s, nm + f"_opm{gwo}")
                        p.inc(s.activation(ofm[:, :], pb5[:, 0:512], AF.Identity),
                              nm + "_ofe", 1, mark=nm + f"_ofe{gwo}")
                        for lc in range(4):
                            oi = (gg * 2 + dh) * 4 + lc
                            p.wait(s, nm + f"_obt{gwo}_{lc}")
                            if db == 0 and oi >= 4:
                                p.wait(s, nm + f"_ost{oi - 4}")
                            p.inc(s.activation(otok[lc][:, db * 128:(db + 1) * 128],
                                               pb5[:, lc * 128:(lc + 1) * 128],
                                               AF.Identity),
                                  nm + "_oev", 1, mark=nm + f"_oev{gwo}_{lc}")

    c1px.close()
    if not aout:
        lcc = ngg * 4 - 1
        fin = {0: [nm + f"_cve{(ngg - 1) * NEB + NEB - 1}"],
               1: [nm + f"_xtk{lcc}_{31}", nm + f"_ccp{lcc}"],
               2: [nm + f"_agp{lcc}", nm + f"_vt2{lcc}", nm + f"_sR{lcc}_1"],
               3: [nm + f"_vmh{lcc}_{63}"],
               4: [nm + f"_yev{lcc}", nm + f"_vsb{lcc}_{7}"],
               5: [nm + f"_yev{lcc}", nm + f"_vyn{lcc}"]}
        return fin[min(lvl, 5)]
    return [nm + f"_ost{(ngg * 2 - 1) * 4 + lc}" for lc in range(4)]


def build_program(prev_marks, phases=("attn_b",), io_src="x_in", detect_races=False):
    import os as _os
    nc = bass.Bass("TRN2", target_bir_lowering=False,
                   debug=bool(_os.environ.get("DUMPB")),
                   detect_race_conditions=detect_races)
    g = declare_io(nc)
    ctx = nc.ctx
    with contextlib.ExitStack() as top:
        p = Prog(nc, top, prev_marks)
        cst = alloc_consts(nc, top)
        with nc.Block() as block:
            emit_consts(p, block, g, cst)
            src = g[io_src]
            last = None
            for ph in phases:
                if ph == "ln_dbg":
                    import contextlib as _cl
                    with _cl.ExitStack() as dbgs:
                        u_dbg = dbgs.enter_context(nc.sbuf_tensor("dbg_ufm", [128, ND * T], BF16))
                        emit_ln(p, block, dbgs, g, cst, "dbg", src, u_dbg)
                        @block.gpsimd
                        def _(gp):
                            p.wait(gp, "dbg_done")
                            dstr = g["x_out"][:, :].rearrange("(a p) c -> p a c", p=128)
                            srcr = u_dbg[:, :].rearrange("p (a c) -> p a c", c=T)
                            p.inc(gp.dma_start(dstr, srcr), "dbgo", 16, mark="dbg_OUT")
                    last = ["dbg_OUT"]
                elif ph == "attn_a":
                    last = emit_attn_a(p, block, top, g, cst, src, g["x_out"],
                                       start_mark=None)
                elif ph == "attn_b":
                    last = emit_attn_b(p, block, top, g, cst, src, g["x_out"])
                elif ph == "dq8":
                    with contextlib.ExitStack() as cdq8:
                        dq_free, (emit_gather, _gi) = emit_dequant8(
                            p, block, g, cst, "dq8", start_mark=None,
                            ctx=cdq8)
                    last = dq_free + emit_gather()
                elif ph == "full":
                    dqh = {}
                    last = emit_attn_a(p, block, top, g, cst, g["x_in"],
                                       g["x_t1"], start_mark=None,
                                       with_dq=True, dq_out=dqh)
                    last = emit_ffn(p, block, top, g, cst, "ff1", g["x_t1"],
                                    g["x_t2"], start_mark=last, dequant=False,
                                    gathered=True, dq_gate=dqh["gmarks"],
                                    early_gate="ata_yev15")
                    last = emit_attn_b(p, block, top, g, cst, g["x_t2"],
                                       g["x_t3"], start_mark=last)
                    last = emit_ffn(p, block, top, g, cst, "ff2", g["x_t3"],
                                    g["x_out"], start_mark=last, dequant=False,
                                    gathered=True, dq_gate=dqh["gmarks"],
                                    early_gate="atb_qkvdone15")
                elif ph == "ffn":
                    last = emit_ffn(p, block, top, g, cst, "ff1", src,
                                    g["x_out"], dequant=True)
                else:
                    raise ValueError(ph)
            @block.gpsimd
            def _(gp):
                for m in last:
                    p.wait(gp, m)
        marks = p.marks
    return nc, marks


def build_twice(phases, io_src="x_in"):
    _, marks = build_program(None, phases, io_src)
    nc, marks2 = build_program(marks, phases, io_src)
    assert marks == marks2, "nondeterministic emission"
    return nc


def kernel(**inputs):
    w = host_prep({k: np.asarray(v) for k, v in inputs.items()})
    hs = np.asarray(inputs["L0_attn_a_hidden_states"])
    nc = build_twice(("full",))
    in_maps = []
    for b in range(NCORES):
        m = dict(w)
        m["x_in"] = np.ascontiguousarray(hs[b][:, PERM])
        m["dqgu_b"] = np.ascontiguousarray(w["gu_bytes"][b * 128:(b + 1) * 128])
        m["dqgu_s"] = np.ascontiguousarray(w["gu_scex"][b * 128:(b + 1) * 128])
        dnb, dns, pls = [], [], [0, 1]
        for s in range(6):
            blk = 6 * b + s
            if blk < 44:
                jb = blk % 22
                dnb.append(w["down_bytes"][jb * 128:(jb + 1) * 128])
                dns.append(w["dn_scex"][jb * 128:(jb + 1) * 128])
                pls.append(blk // 22)
            else:
                dnb.append(np.zeros((128, D), np.uint8))
                dns.append(np.zeros((128, D), np.float16))
                pls.append(0)
        m["dqdn_b"] = np.ascontiguousarray(np.concatenate(dnb, 0))
        m["dqdn_s"] = np.ascontiguousarray(np.concatenate(dns, 0))
        pls = np.asarray(pls, np.int32)
        msk = np.concatenate([15 * (1 - pls), 15 * pls])
        m["dq_sh"] = np.broadcast_to(msk[None, :], (128, 16)).copy()
        in_maps.append(m)
    res = run_bass_kernel_spmd(nc, in_maps, core_ids=list(range(NCORES)))
    outs = [res.results[b]["x_out"][:, INVPERM] for b in range(NCORES)]
    return np.stack(outs, axis=0)


# version 22 (final)
# speedup vs baseline: 1.0256x; 1.0256x over previous
import contextlib
import numpy as np
import ml_dtypes
import concourse.bass as bass
import concourse.mybir as mybir
from concourse.bass_utils import run_bass_kernel_spmd

F32 = mybir.dt.float32
BF16 = mybir.dt.bfloat16
F16 = mybir.dt.float16
U8 = mybir.dt.uint8
I32 = mybir.dt.int32
ALU = mybir.AluOpType
AF = mybir.ActivationFunctionType
AXX = mybir.AxisListType.X

T = 2048
D = 2048
NCORES = 8
NT = T // 128
ND = D // 128

D_INNER = 4096
D_STATE = 64
NHEADS = 64
CONV_DIM = D_INNER + 2 * D_STATE
NCT = CONV_DIM // 128
ZXDT = 2 * D_INNER + 2 * D_STATE + NHEADS

BH = 16
BD = 128

FI = 5632
GU = 2 * FI

PERM = np.concatenate([np.arange(0, D, 2), np.arange(1, D, 2)])
INVPERM = np.argsort(PERM)

RS_D = float(1.0 / np.sqrt(128.0))


DBG_STAGE = "full"
_STAGES = ["rope", "bound", "scores", "av1", "av2", "av3", "av", "full"]


def _stg(s):
    return _STAGES.index(DBG_STAGE) >= _STAGES.index(s)


def ts(i, size=128):
    return slice(i * size, (i + 1) * size)



def _ffn_col_order():
    cols = []
    for p2 in range(2):
        for tau in range(22):
            base = 2 * (128 * tau + np.arange(128)) + p2
            cols.append(base)
            cols.append(FI + base)
    return np.concatenate(cols)


FFN_COL_ORDER = _ffn_col_order()


def _qkv_row_order():
    order = []
    for h in range(BH):
        for sec in range(3):
            base = sec * D + h * BD
            if sec < 2:
                order.append(base + np.arange(0, BD, 2))
                order.append(base + np.arange(1, BD, 2))
            else:
                order.append(base + np.arange(BD))
    return np.concatenate(order)


QKV_ROW_ORDER = _qkv_row_order()


def host_prep(inputs):
    f = np.ascontiguousarray
    out = {}
    out["w_inproj"] = f(inputs["L0_attn_a_in_proj_w"].T[PERM, :])
    out["w_outproj"] = f(inputs["L0_attn_a_out_proj_w"][PERM, :].T)
    out["rms_w"] = f(inputs["L0_attn_a_rms_weight"][PERM][None, :])
    out["conv_w"] = f(inputs["L0_attn_a_conv_w"])
    out["conv_b"] = f(inputs["L0_attn_a_conv_b"].reshape(NCT, 128).T)
    out["dt_bias"] = f(inputs["L0_attn_a_dt_bias"][None, :])
    out["a_log"] = f(inputs["L0_attn_a_A_log"][None, :])
    out["d_vec"] = f(inputs["L0_attn_a_D"][None, :])
    out["norm_w"] = f(inputs["L0_attn_a_norm_w"].reshape(32, 128))

    out["w_qkv"] = f(inputs["L0_attn_b_W_qkv"][QKV_ROW_ORDER, :].T[PERM, :])
    out["w_proj"] = f(inputs["L0_attn_b_W_proj"][PERM, :].T)

    gub = inputs["L0_ffn_gate_up_packed"].view(np.uint8).reshape(GU, 1024)
    out["gu_bytes"] = f(gub.T[:, FFN_COL_ORDER])
    gus = inputs["L0_ffn_gate_up_scales"][:, FFN_COL_ORDER]
    out["gu_scex"] = f(gus[np.arange(1024) // 64])
    dpb = inputs["L0_ffn_down_packed"].view(np.uint8).reshape(D, 2816)
    out["down_bytes"] = f(dpb.T[:, PERM])
    dns = inputs["L0_ffn_down_scales"][:, PERM]
    out["dn_scex"] = f(dns[np.arange(2816) // 64])

    out["ident"] = np.eye(128, dtype=np.float32)
    ij = np.arange(128)
    out["masku"] = f(np.where(ij[:, None] < ij[None, :], np.float32(-1.0e30),
                              np.float32(0.0)).astype(np.float32))
    r127 = np.zeros((128, 128), np.float32)
    r127[127, :] = 1.0
    out["r127"] = r127
    out["ones_row"] = np.ones((1, 128), dtype=np.float32)
    ii = np.arange(128)
    out["ltri"] = f((ii[:, None] <= ii[None, :]).astype(np.float32))
    jm = np.zeros((128, 128), np.float32)
    jm[:64, :] = 1.0
    out["jmid"] = jm
    out["dmask"] = f(np.where(ii[:, None] > ii[None, :], np.float32(-3.0e38),
                              np.float32(0.0)).astype(np.float32))
    out["dmask_t"] = f(out["dmask"].T)
    pos = np.arange(T, dtype=np.float64)
    invf = 1.0 / 10000.0 ** (2.0 * np.arange(64, dtype=np.float64) / 128.0)
    ang = invf[:, None] * pos[None, :]
    cos_t = np.cos(ang).astype(np.float32)
    sin_t = np.sin(ang).astype(np.float32)
    out["cos_d"] = f(np.concatenate([cos_t, cos_t], 0))
    out["sin_np"] = f(np.concatenate([-sin_t, sin_t], 0))
    out["sin_pn"] = f(np.concatenate([sin_t, -sin_t], 0))
    return out



class Prog:
    def __init__(self, nc, ctx, prev_marks):
        self.nc = nc
        self.ctx = ctx
        self.prev = prev_marks
        self.marks = {}
        self.sems = {}
        self.counts = {}
        self.dma_sems = set()
        self.logmap = {}

    def _sem(self, name):
        if name not in self.sems:
            self.sems[name] = self.ctx.enter_context(self.nc.semaphore(name))
            self.counts[name] = 0
        return self.sems[name]

    def inc(self, instr, semname, k=1, mark=None):
        c = instr.concise()
        toks = c.split()
        logical = semname
        if "DMACopy" not in toks[1] and "Collective" not in toks[1]:
            semname = "X" + toks[0]
        else:
            self.dma_sems.add(semname)
        self.logmap[logical] = semname
        h = self._sem(semname)
        instr.then_inc(h, k)
        self.counts[semname] += k
        if mark is not None:
            assert mark not in self.marks, f"duplicate mark {mark}"
            self.marks[mark] = (semname, self.counts[semname])
        return instr

    def mark_here(self, semname, mark):
        semname = self.logmap.get(semname, semname)
        assert mark not in self.marks, mark
        self.marks[mark] = (semname, self.counts.get(semname, 0))

    def wait(self, eng, mark):
        if mark is None:
            return
        if isinstance(mark, (list, tuple)):
            for m in mark:
                self.wait(eng, m)
            return
        if self.prev is None:
            if mark in self.marks:
                sn, v = self.marks[mark]
                eng.wait_ge(self._sem(sn), v)
            else:
                eng.wait_ge(self._sem("_p1dummy"), 0)
        else:
            if mark not in self.prev:
                assert DBG_STAGE != "full", f"unknown mark {mark}"
                eng.wait_ge(self._sem("_p1dummy"), 0)
                return
            sn, v = self.prev[mark]
            eng.wait_ge(self._sem(sn), v)

    def waits(self, eng, *ms):
        for m in ms:
            self.wait(eng, m)

    def selfsync(self, eng, instr, semname):
        toks = instr.concise().split()
        logical = semname
        if "DMACopy" not in toks[1]:
            semname = "X" + toks[0]
        self.logmap[logical] = semname
        h = self._sem(semname)
        instr.then_inc(h, 1)
        self.counts[semname] += 1
        eng.wait_ge(h, self.counts[semname])



def declare_io(nc):
    g = {}
    g["x_in"] = nc.dram_tensor("x_in", [T, D], F32, kind="ExternalInput")
    g["x_out"] = nc.dram_tensor("x_out", [T, D], F32, kind="ExternalOutput")
    g["x_res"] = nc.dram_tensor("x_res", [T, D], F32, kind="Internal")
    g["w_inproj"] = nc.dram_tensor("w_inproj", [D, ZXDT], F32, kind="ExternalInput")
    g["w_outproj"] = nc.dram_tensor("w_outproj", [D_INNER, D], F32, kind="ExternalInput")
    g["rms_w"] = nc.dram_tensor("rms_w", [1, D], F32, kind="ExternalInput")
    g["conv_w"] = nc.dram_tensor("conv_w", [CONV_DIM, 4], F32, kind="ExternalInput")
    g["conv_b"] = nc.dram_tensor("conv_b", [128, NCT], F32, kind="ExternalInput")
    g["dt_bias"] = nc.dram_tensor("dt_bias", [1, NHEADS], F32, kind="ExternalInput")
    g["a_log"] = nc.dram_tensor("a_log", [1, NHEADS], F32, kind="ExternalInput")
    g["d_vec"] = nc.dram_tensor("d_vec", [1, NHEADS], F32, kind="ExternalInput")
    g["norm_w"] = nc.dram_tensor("norm_w", [32, 128], F32, kind="ExternalInput")
    g["w_qkv"] = nc.dram_tensor("w_qkv", [D, 3 * D], F32, kind="ExternalInput")
    g["w_proj"] = nc.dram_tensor("w_proj", [D, D], F32, kind="ExternalInput")
    g["gu_bytes"] = nc.dram_tensor("gu_bytes", [1024, GU], U8, kind="ExternalInput")
    g["gu_scex"] = nc.dram_tensor("gu_scex", [1024, GU], F16, kind="ExternalInput")
    g["down_bytes"] = nc.dram_tensor("down_bytes", [2816, D], U8, kind="ExternalInput")
    g["dn_scex"] = nc.dram_tensor("dn_scex", [2816, D], F16, kind="ExternalInput")
    g["ident"] = nc.dram_tensor("ident", [128, 128], F32, kind="ExternalInput")
    g["ones_row"] = nc.dram_tensor("ones_row", [1, 128], F32, kind="ExternalInput")
    g["ltri"] = nc.dram_tensor("ltri", [128, 128], F32, kind="ExternalInput")
    g["jmid"] = nc.dram_tensor("jmid", [128, 128], F32, kind="ExternalInput")
    g["dmask"] = nc.dram_tensor("dmask", [128, 128], F32, kind="ExternalInput")
    g["dmask_t"] = nc.dram_tensor("dmask_t", [128, 128], F32, kind="ExternalInput")
    g["cos_d"] = nc.dram_tensor("cos_d", [128, T], F32, kind="ExternalInput")
    g["sin_np"] = nc.dram_tensor("sin_np", [128, T], F32, kind="ExternalInput")
    g["sin_pn"] = nc.dram_tensor("sin_pn", [128, T], F32, kind="ExternalInput")
    g["z_hbm"] = nc.dram_tensor("z_hbm", [T, D_INNER], BF16, kind="Internal")
    g["masku"] = nc.dram_tensor("masku", [128, 128], F32, kind="ExternalInput")
    g["r127"] = nc.dram_tensor("r127", [128, 128], F32, kind="ExternalInput")
    g["xbcf_hbm"] = nc.dram_tensor("xbcf_hbm", [CONV_DIM, T], BF16, kind="Internal")
    g["wo_bf"] = nc.dram_tensor("wo_bf", [D_INNER, D], BF16, kind="Internal")
    g["sg_hbm"] = nc.dram_tensor("sg_hbm", [64, 128], F32, kind="Internal")
    g["dqgu_b"] = nc.dram_tensor("dqgu_b", [128, GU], U8, kind="ExternalInput")
    g["dqgu_s"] = nc.dram_tensor("dqgu_s", [128, GU], F16, kind="ExternalInput")
    g["dqdn_b"] = nc.dram_tensor("dqdn_b", [768, D], U8, kind="ExternalInput")
    g["dqdn_s"] = nc.dram_tensor("dqdn_s", [768, D], F16, kind="ExternalInput")
    g["dq_sh"] = nc.dram_tensor("dq_sh", [128, 16], I32, kind="ExternalInput")
    g["gu_loc"] = nc.dram_tensor("gu_loc", [256, GU], BF16, kind="Internal")
    g["dn_loc"] = nc.dram_tensor("dn_loc", [768, D], BF16, kind="Internal")
    g["wgu_g"] = nc.dram_tensor("wgu_g", [2048, GU], BF16, kind="Internal")
    g["wdn_g"] = nc.dram_tensor("wdn_g", [6144, D], BF16, kind="Internal")
    g["x_t1"] = nc.dram_tensor("x_t1", [T, D], F32, kind="Internal")
    g["x_t2"] = nc.dram_tensor("x_t2", [T, D], F32, kind="Internal")
    g["x_t3"] = nc.dram_tensor("x_t3", [T, D], F32, kind="Internal")
    g["wgu_bf"] = nc.dram_tensor("wgu_bf", [D, GU], BF16, kind="Internal")
    g["wdn_bf"] = nc.dram_tensor("wdn_bf", [FI, D], BF16, kind="Internal")
    return g



def alloc_consts(nc, ctx):
    cst = {}
    cst["ident_bf"] = ctx.enter_context(nc.sbuf_tensor("c_ident", [128, 128], BF16))
    cst["ident_f"] = ctx.enter_context(nc.sbuf_tensor("c_identf", [128, 128], F32))
    cst["ones_bf"] = ctx.enter_context(nc.sbuf_tensor("c_ones", [1, 128], BF16))
    cst["ones_f"] = ctx.enter_context(nc.sbuf_tensor("c_onesf", [33, 128], F32))
    cst["ones_colf"] = ctx.enter_context(nc.sbuf_tensor("c_onescf", [128, 1], F32))
    cst["half_col"] = ctx.enter_context(nc.sbuf_tensor("c_halfc", [128, 1], F32))
    cst["ltri_f"] = ctx.enter_context(nc.sbuf_tensor("c_ltri", [128, 128], F32))
    cst["jmid_f"] = ctx.enter_context(nc.sbuf_tensor("c_jmid", [128, 128], F32))
    cst["dmask_f"] = ctx.enter_context(nc.sbuf_tensor("c_dmask", [128, 128], F32))
    cst["dmask_t"] = ctx.enter_context(nc.sbuf_tensor("c_dmaskt", [128, 128], F32))
    cst["masku_bf"] = ctx.enter_context(nc.sbuf_tensor("c_masku", [128, 128], BF16))
    cst["r127_f"] = ctx.enter_context(nc.sbuf_tensor("c_r127", [128, 128], F32))
    return cst


def emit_consts(p, block, g, cst):
    @block.gpsimd
    def _(gp):
        for dst, src in [(cst["ident_bf"], g["ident"]), (cst["ones_bf"], g["ones_row"]),
                         (cst["masku_bf"], g["masku"])]:
            p.inc(gp.dma_start(dst[:, :], src[:, :]), "cstg", 16)
        p.mark_here("cstg", "consts_g")

    @block.sync
    def _(sy):
        p.inc(sy.dma_start(cst["ones_f"][32:33, :], g["ones_row"][:, :]), "csts", 16)
        for dst, src in [(cst["ones_f"][0:1, :], g["ones_row"]), (cst["ltri_f"], g["ltri"]),
                         (cst["jmid_f"], g["jmid"]), (cst["dmask_f"], g["dmask"]),
                         (cst["ident_f"], g["ident"]),
                         (cst["dmask_t"], g["dmask_t"]),
                         (cst["r127_f"], g["r127"])]:
            p.inc(sy.dma_start(dst[:, :], src[:, :]), "csts", 16)
        p.mark_here("csts", "consts_s")

    @block.vector
    def _(v):
        p.waits(v, "consts_g", "consts_s")
        v.memset(cst["ones_colf"][:, :], 1.0)
        p.inc(v.memset(cst["half_col"][:, :], 0.5), "cstv", 1, mark="consts")




def emit_ln(p, block, ctx, g, cst, nm, x_src, u_fm, rms_mode=False, xp_dst=None,
            start_mark=None, tile0=0, ntiles=NT, ucols=T, bufs=2):
    EPS = 1e-5
    nc = p.nc
    xt = ctx.enter_context(nc.sbuf_tensor(nm + "_xt", [128, bufs * D], F32))
    ut = ctx.enter_context(nc.sbuf_tensor(nm + "_ut", [128, bufs * D], BF16))
    if bufs == 2:
        dmp = ctx.enter_context(nc.sbuf_tensor(nm + "_dmp", [128, D], BF16))
    st = ctx.enter_context(nc.sbuf_tensor(nm + "_st", [128, 16], F32))
    ptr = [ctx.enter_context(nc.psum_tensor(nm + f"_ptr{i}", [128, 1024], BF16)) for i in range(2)]
    if rms_mode:
        lnt = ctx.enter_context(nc.sbuf_tensor(nm + "_lnt", [128, D], F32))
        wtile = ctx.enter_context(nc.sbuf_tensor(nm + "_wtl", [128, D], BF16))
        wrow = ctx.enter_context(nc.sbuf_tensor(nm + "_wrw", [1, D], F32))
        pwb = ctx.enter_context(nc.psum_tensor(nm + "_pwb", [128, 512], F32))

    @block.sync
    def _(sy):
        p.wait(sy, start_mark)
        if rms_mode:
            p.inc(sy.dma_start(wrow[:, :], g["rms_w"][:, :]), nm + "_ld", 16,
                  mark=nm + "_wrow")
        for k in range(tile0, tile0 + ntiles):
            sl = k % bufs
            if bufs == 2 and k >= tile0 + 2:
                p.wait(sy, f"{nm}_u{k - 2}")
            elif bufs == 1 and k >= tile0 + 1:
                p.wait(sy, f"{nm}_ev{k - 1}")
            p.inc(sy.dma_start(xt[:, ts(sl, D)], x_src[ts(k), :]),
                  nm + f"_ld{sl}", 16, mark=f"{nm}_x{k}")
            if rms_mode:
                p.wait(sy, f"{nm}_u{k}")
                p.inc(sy.dma_start(xp_dst[ts(k), :], lnt[:, 0:D]),
                      nm + f"_xp{k % 2}", 16, mark=f"{nm}_xp{k}")
        if rms_mode:
            p.mark_here(nm + "_xp0", nm + "_xpdone0")
            p.mark_here(nm + "_xp1", nm + "_xpdone1")

    @block.tensor
    def _(t):
        p.wait(t, "consts")
        if rms_mode:
            p.wait(t, nm + "_wrow")
            for j in range(4):
                if j >= 1:
                    p.wait(t, f"{nm}_wb{j - 1}")
                p.inc(t.matmul(pwb[:, :], cst["ones_f"][0:1, 0:128],
                               wrow[0:1, ts(j, 512)], start=True, stop=True),
                      nm + "_tr", 1, mark=f"{nm}_wmm{j}")
        for k in range(tile0, tile0 + ntiles):
            p.wait(t, f"{nm}_u{k}")
            for j in range(ND):
                idx = (k - tile0) * ND + j
                if idx >= 2:
                    p.wait(t, f"{nm}_ev_i{idx - 2}")
                p.inc(t.transpose(ptr[idx % 2][:, 0:128],
                                  ut[:, ts(k % bufs, D)][:, ts(j)],
                                  cst["ident_bf"][:, :]),
                      nm + "_tr", 1, mark=f"{nm}_tr_i{idx}")

    @block.scalar
    def _(s):
        if rms_mode:
            for j in range(4):
                p.wait(s, f"{nm}_wmm{j}")
                p.inc(s.copy(wtile[:, ts(j, 512)], pwb[:, :]), nm + "_wb", 1,
                      mark=f"{nm}_wb{j}")
            p.mark_here(nm + "_wb", nm + "_wtile")
        for k in range(tile0, tile0 + ntiles):
            p.wait(s, f"{nm}_x{k}")
            xs = xt[:, ts(k % bufs, D)]
            acc = st[:, 1:2] if k % 2 == 0 else st[:, 12:13]
            dmpt = dmp[:, :] if bufs == 2 else ut[:, 0:D]
            p.inc(s.activation(dmpt, xs, AF.Square, accum_out=acc),
                  nm + "_sq", 1, mark=f"{nm}_sq{k}")
            p.wait(s, f"{nm}_var{k}")
            SSA = nm + "_sch"
            p.selfsync(s, s.activation(st[:, 4:5], st[:, 3:4], AF.Ln), SSA)
            if rms_mode:
                p.selfsync(s, s.activation(st[:, 5:6], st[:, 4:5], AF.Exp, scale=-0.5), SSA)
                p.selfsync(s, s.activation(st[:, 9:10], st[:, 8:9], AF.Ln), SSA)
                p.inc(s.activation(st[:, 10:11], st[:, 9:10], AF.Exp, scale=-0.5),
                      nm + "_rst", 1, mark=f"{nm}_rst{k}")
            else:
                p.inc(s.activation(st[:, 5:6], st[:, 4:5], AF.Exp, scale=-0.5),
                      nm + "_rst", 1, mark=f"{nm}_rst{k}")
            for j in range(ND):
                idx = (k - tile0) * ND + j
                p.wait(s, f"{nm}_tr_i{idx}")
                p.inc(s.copy(u_fm[:, j * ucols + (k - tile0) * 128:
                                  j * ucols + (k - tile0 + 1) * 128],
                             ptr[idx % 2][:, 0:128]), nm + "_ev", 1,
                      mark=f"{nm}_ev_i{idx}")
            p.mark_here(nm + "_ev", f"{nm}_ev{k}")
        p.mark_here(nm + "_ev", f"{nm}_done")

    @block.vector
    def _(v):
        for k in range(tile0, tile0 + ntiles):
            p.waits(v, f"{nm}_x{k}", f"{nm}_sq{k}")
            xs = xt[:, ts(k % bufs, D)]
            us = ut[:, ts(k % bufs, D)]
            acc = st[:, 1:2] if k % 2 == 0 else st[:, 12:13]
            dmpt = dmp[:, :] if bufs == 2 else us
            p.selfsync(v, v.tensor_scalar(dmpt, xs, 0.0, 0.0, ALU.add, ALU.add,
                                          accum_out=st[:, 0:1]), nm + "_vacc")
            SS = nm + "_vch"
            p.selfsync(v, v.tensor_scalar(st[:, 2:3], st[:, 0:1], 1.0 / D, None, ALU.mult), SS)
            p.selfsync(v, v.tensor_scalar(st[:, 11:12], st[:, 2:3], st[:, 2:3], None, ALU.mult), SS)
            p.selfsync(v, v.tensor_scalar(st[:, 3:4], acc, 1.0 / D, None, ALU.mult), SS)
            p.selfsync(v, v.tensor_sub(st[:, 3:4], st[:, 3:4], st[:, 11:12]), SS)
            if rms_mode:
                p.selfsync(v, v.tensor_scalar(st[:, 7:8], st[:, 3:4], 0.0, None, ALU.add), SS)
            p.selfsync(v, v.tensor_scalar(st[:, 3:4], st[:, 3:4], EPS, None, ALU.add), SS)
            if rms_mode:
                p.selfsync(v, v.reciprocal(st[:, 11:12], st[:, 3:4]), SS)
                p.selfsync(v, v.tensor_tensor(st[:, 7:8], st[:, 7:8], st[:, 11:12], ALU.mult), SS)
                p.inc(v.tensor_scalar(st[:, 8:9], st[:, 7:8], EPS, None, ALU.add),
                      nm + "_var", 1, mark=f"{nm}_var{k}")
            else:
                p.inc(v.tensor_scalar(st[:, 8:9], st[:, 3:4], 0.0, None, ALU.add),
                      nm + "_var", 1, mark=f"{nm}_var{k}")
            p.wait(v, f"{nm}_rst{k}")
            if rms_mode:
                p.selfsync(v, v.tensor_tensor(st[:, 6:7], st[:, 5:6], st[:, 10:11], ALU.mult), SS)
                if k >= 1:
                    p.wait(v, f"{nm}_xp{k - 1}")
                lns = lnt[:, 0:D]
                p.selfsync(v, v.tensor_scalar(lns, xs, st[:, 2:3], st[:, 5:6],
                                              ALU.subtract, ALU.mult), SS)
                p.wait(v, nm + "_wtile")
                p.selfsync(v, v.tensor_scalar(us, xs, st[:, 2:3], st[:, 6:7],
                                              ALU.subtract, ALU.mult), SS)
                p.selfsync(v, v.tensor_tensor(us, us, wtile[:, :], ALU.mult), SS)
                p.inc(v.tensor_add(lns, lns, xs), nm + "_u", 1, mark=f"{nm}_u{k}")
            else:
                p.inc(v.tensor_scalar(us, xs, st[:, 2:3], st[:, 5:6],
                                      ALU.subtract, ALU.mult),
                      nm + "_u", 1, mark=f"{nm}_u{k}")



def emit_attn_b(p, block, ctx, g, cst, x_src, x_dst, start_mark=None):
    nm = "atb"
    nc = p.nc

    with contextlib.ExitStack() as c1:
        u_fm = c1.enter_context(nc.sbuf_tensor(nm + "_ufm", [128, ND * T], BF16))
        y_fm = c1.enter_context(nc.sbuf_tensor(nm + "_yfm", [128, ND * T], BF16))

        with contextlib.ExitStack() as c0:
            emit_ln(p, block, c0, g, cst, nm + "l", x_src, u_fm,
                    start_mark=start_mark)

        ch = contextlib.ExitStack()
        wstg = [ch.enter_context(nc.sbuf_tensor(nm + f"_wst{i}", [128, 16 * 128], BF16)) for i in range(2)]
        ab = ch.enter_context(nc.sbuf_tensor(nm + "_ab", [128, 2 * T], BF16))
        vf = ch.enter_context(nc.sbuf_tensor(nm + "_vf", [128, T], BF16))
        rx = ch.enter_context(nc.sbuf_tensor(nm + "_rx", [128, T], BF16))
        ry = ch.enter_context(nc.sbuf_tensor(nm + "_ry", [128, T], BF16))
        q2 = ch.enter_context(nc.sbuf_tensor(nm + "_q2", [128, T], BF16))
        k2 = ch.enter_context(nc.sbuf_tensor(nm + "_k2", [128, T], BF16))
        vt = ch.enter_context(nc.sbuf_tensor(nm + "_vt", [128, NT * 132], BF16))
        bcol = ch.enter_context(nc.sbuf_tensor(nm + "_bcl", [128, 1], F32))
        rmx = ch.enter_context(nc.sbuf_tensor(nm + "_rmx", [128, 512], BF16))
        nbb = ch.enter_context(nc.sbuf_tensor(nm + "_nbb", [1, T], BF16))
        onesr = ch.enter_context(nc.sbuf_tensor(nm + "_1r", [1, 128], BF16))
        pbuf = ch.enter_context(nc.sbuf_tensor(nm + "_pbf", [128, 2 * 512], BF16))
        rsum = ch.enter_context(nc.sbuf_tensor(nm + "_rsm", [128, 4], F32))
        ystg = ch.enter_context(nc.sbuf_tensor(nm + "_ysg", [128, 2 * 128], BF16))
        trig = ch.enter_context(nc.sbuf_tensor(nm + "_trig", [128, 3 * T], BF16))
        COS, SNP, SPN = (trig[:, 0:T], trig[:, T:2 * T], trig[:, 2 * T:3 * T])

        pp = [c1.enter_context(nc.psum_tensor(nm + f"_pp{i}", [128, 512], F32)) for i in range(6)]
        pbf = [c1.enter_context(nc.psum_tensor(nm + f"_pbf{i}", [128, 1024], BF16)) for i in range(2)]
        PAV = [pp[2], pp[3], pp[4], pp[5]]

        def qi(h, sec, c):
            return h * 12 + sec * 4 + c

        SEQ = [(c, tau) for c in range(4) for tau in range(4 * c + 4)]
        SEQI = {ct: i for i, ct in enumerate(SEQ)}

        @block.gpsimd
        def _(gp):
            p.wait(gp, nm + "l_done")
            p.inc(gp.dma_start(trig[:, 0:T], g["cos_d"][:, :]), nm + "_tg", 16)
            p.inc(gp.dma_start(trig[:, T:2 * T], g["sin_np"][:, :]), nm + "_tg", 16)
            p.inc(gp.dma_start(trig[:, 2 * T:3 * T], g["sin_pn"][:, :]), nm + "_tg", 16)
            p.mark_here(nm + "_tg", nm + "_trig")
            for h in range(BH):
                for sec in range(3):
                    gs = 3 * h + sec
                    if gs >= 2:
                        ph, psec = divmod(gs - 2, 3)
                        p.wait(gp, f"{nm}_qkv_i{ph * 12 + psec * 4 + 3}")
                    src = g["w_qkv"][:, 384 * h + 128 * sec: 384 * h + 128 * (sec + 1)]
                    srcr = src.rearrange("(a p) c -> p a c", p=128)
                    dstr = wstg[gs % 2][:, :].rearrange("p (a c) -> p a c", c=128)
                    p.inc(gp.dma_start(dstr, srcr), nm + f"_wld{gs % 2}", 16,
                          mark=f"{nm}_wsec{gs}")

        @block.vector
        def _(v):
            p.wait(v, nm + "l_done")
            v.memset(onesr[0:1, :], 1.0)
            for j in range(NT):
                p.inc(v.memset(vt[:, j * 132 + 128: j * 132 + 132], 1.0),
                      nm + "_vo", 1)
            p.mark_here(nm + "_vo", nm + "_vones")
            for h in range(BH):
                p.wait(v, f"{nm}_qe_i{qi(h, 1, 3)}")
                p.wait(v, nm + "_trig")
                A = ab[:, 0:T]
                B = ab[:, T: 2 * T]
                SSR = nm + "_vch3"
                p.selfsync(v, v.tensor_mul(rx[:, :], A, COS), SSR)
                p.selfsync(v, v.tensor_mul(q2[:, :], B, SNP), SSR)
                p.selfsync(v, v.tensor_add(rx[:, :], rx[:, :], q2[:, :]), SSR)
                p.selfsync(v, v.tensor_mul(ry[:, :], A, SPN), SSR)
                p.selfsync(v, v.tensor_mul(q2[:, :], B, COS), SSR)
                p.selfsync(v, v.tensor_add(ry[:, :], ry[:, :], q2[:, :]), SSR)
                p.selfsync(v, v.tensor_copy(k2[0:64, :], ry[64:128, :]), SSR)
                p.selfsync(v, v.tensor_copy(k2[64:128, :], rx[64:128, :]), SSR)
                p.selfsync(v, v.tensor_copy(q2[0:64, :], rx[0:64, :]), SSR)
                p.inc(v.tensor_copy(q2[64:128, :], ry[0:64, :]),
                      nm + "_rope", 1, mark=f"{nm}_rope{h}")
                for gt1 in range(NT if _stg("bound") else 0):
                    nsc = (gt1 + 4) // 4
                    for cc in range(nsc):
                        w = min(512, (gt1 + 1) * 128 - cc * 512)
                        pb = 2 + cc % 2
                        p.wait(v, f"{nm}_mxm{h}_{gt1}_{cc}")
                        if cc == nsc - 1:
                            off = gt1 * 128 - cc * 512
                            p.selfsync(v, v.tensor_tensor(pp[pb][:, off: off + 128],
                                                          pp[pb][:, off: off + 128],
                                                          cst["dmask_t"][:, :], ALU.add),
                                       nm + "_vch2")
                        if cc == 0:
                            fold = v.tensor_scalar(rmx[:, 0:w], pp[pb][:, 0:w],
                                                   0.0, None, ALU.add)
                        else:
                            fold = v.tensor_tensor(rmx[:, 0:w], rmx[:, 0:w],
                                                   pp[pb][:, 0:w], ALU.max)
                        p.inc(fold, nm + "_mxfs", 1, mark=f"{nm}_mxf{h}_{gt1}_{cc}")
                        p.wait(v, f"{nm}_mxf{h}_{gt1}_{cc}")
                    wfin = min(512, (gt1 + 1) * 128)
                    p.inc(v.reduce_max(bcol[:, 0:1], rmx[:, 0:wfin], axis=AXX),
                          nm + "_bmx", 1, mark=f"{nm}_bmax{h}_{gt1}")

                for c in range(4 if _stg("scores") else 0):
                    for tau in range(4 * c, 4 * c + 4):
                        seq = SEQI[(c, tau)]
                        mu = tau - 4 * c
                        p.wait(v, f"{nm}_smm{h}_{c}_{tau}")
                        p.inc(v.tensor_tensor(pp[seq % 2][:, ts(mu)],
                                              pp[seq % 2][:, ts(mu)],
                                              cst["dmask_f"][:, :], ALU.add),
                              nm + "_smsk", 1, mark=f"{nm}_smsk{h}_{c}_{tau}")
                    for mu in range(4 if _stg("av2") else 0):
                        gt1 = 4 * c + mu
                        p.wait(v, f"{nm}_av{h}_{gt1}")
                        p.inc(v.reciprocal(rsum[:, mu: mu + 1],
                                           PAV[mu][:, 128:129]),
                              nm + "_rcp", 1, mark=f"{nm}_rcp{h}_{gt1}")
                p.mark_here(nm + "_smsk", f"{nm}_scoresdone{h}")

        @block.tensor
        def _(t):
            for h in range(BH):
                p.wait(t, f"{nm}l_done")
                for sec in range(3):
                    gs = 3 * h + sec
                    p.wait(t, f"{nm}_wsec{gs}")
                    for c in range(4):
                        i = qi(h, sec, c)
                        pi = i % 2
                        if i >= 2:
                            p.wait(t, f"{nm}_qe_i{i - 2}")
                        last = None
                        for kk in range(ND):
                            last = t.matmul(
                                pp[pi][:, :],
                                wstg[gs % 2][:, kk * 128: (kk + 1) * 128],
                                u_fm[:, kk * T + c * 512: kk * T + (c + 1) * 512],
                                start=(kk == 0), stop=(kk == ND - 1))
                        p.inc(last, nm + "_qkv", 1, mark=f"{nm}_qkv_i{i}")
                p.mark_here(nm + "_qkv", f"{nm}_qkvdone{h}")
                for j in range(NT):
                    gidx = h * NT + j
                    p.wait(t, f"{nm}_qe_i{qi(h, 2, 3)}")
                    if j < 2:
                        if h >= 1:
                            p.wait(t, f"{nm}_yfmev{h - 1}_{14 + j}")
                    else:
                        p.wait(t, f"{nm}_vtev_i{gidx - 2}")
                    p.inc(t.transpose(pbf[j % 2][:, 0:128], vf[:, ts(j)],
                                      cst["ident_bf"][:, :]),
                          nm + "_vtr", 1, mark=f"{nm}_vtr_i{gidx}")
                for gt1 in range(NT if _stg("bound") else 0):
                    p.wait(t, f"{nm}_rope{h}")
                    nsc = (gt1 + 4) // 4
                    for cc in range(nsc):
                        w = min(512, (gt1 + 1) * 128 - cc * 512)
                        pb = 2 + cc % 2
                        if cc >= 2:
                            p.wait(t, f"{nm}_mxf{h}_{gt1}_{cc - 2}")
                        elif gt1 >= 1:
                            pn = (gt1 + 3) // 4
                            p.wait(t, f"{nm}_mxf{h}_{gt1 - 1}_{pn - 1}")
                            p.wait(t, f"{nm}_nbbev{h}_{gt1 - 1}")
                        elif h >= 1 and cc == 0:
                            pn = (NT + 3) // 4
                            p.waits(t, f"{nm}_nbbev{h - 1}_{NT - 1}",
                                    f"{nm}_mxf{h - 1}_{NT - 1}_{pn - 1}",
                                    f"{nm}_yev{h - 1}_{12}", f"{nm}_yev{h - 1}_{13}")
                        p.inc(t.matmul(pp[pb][:, 0:w], q2[:, ts(gt1)],
                                       k2[:, cc * 512: cc * 512 + w],
                                       start=True, stop=True),
                              nm + "_mxm", 1, mark=f"{nm}_mxm{h}_{gt1}_{cc}")
                    p.wait(t, f"{nm}_bmax{h}_{gt1}")
                    if gt1 >= 1:
                        p.wait(t, f"{nm}_nbbev{h}_{gt1 - 1}")
                    elif h >= 1:
                        p.wait(t, f"{nm}_nbbev{h - 1}_{NT - 1}")
                    pbt = 2 + (nsc - 1) % 2
                    p.inc(t.transpose(pp[5 - pbt][0:1, 0:128], bcol[:, 0:1],
                                      cst["ident_f"][:, :]),
                          nm + "_btr", 1, mark=f"{nm}_btr{h}_{gt1}")
                for c in range(4 if _stg("scores") else 0):
                    ntau = 4 * c + 4
                    for tau in range(-1, ntau):
                        tau1 = tau + 1
                        if tau1 < ntau:
                            seq = SEQI[(c, tau1)]
                            pi = seq % 2
                            if seq >= 2:
                                pc, pt = SEQ[seq - 2]
                                p.wait(t, f"{nm}_pev{h}_{pc}_{pt}")
                            elif h >= 1:
                                pc, pt = SEQ[len(SEQ) - 2 + seq]
                                p.wait(t, f"{nm}_pev{h - 1}_{pc}_{pt}")
                            if seq < 2:
                                p.waits(t, f"{nm}_negb{h}", f"{nm}_qe_i{qi(h, 2, 3)}")
                            t.matmul(pp[pi][:, :], k2[:, ts(tau1)],
                                     q2[:, c * 512:(c + 1) * 512],
                                     start=True, stop=False)
                            p.inc(t.matmul(pp[pi][:, :], onesr[0:1, 0:128],
                                           nbb[0:1, c * 512:(c + 1) * 512],
                                           start=False, stop=True),
                                  nm + "_smm", 1, mark=f"{nm}_smm{h}_{c}_{tau1}")
                        if tau < 0 or not _stg("av1"):
                            continue
                        p.wait(t, f"{nm}_pev{h}_{c}_{tau}")
                        lastmm = None
                        for mu in range(max(0, tau - 4 * c), 4):
                            gt1 = 4 * c + mu
                            if tau == 0 and c >= 1:
                                p.wait(t, f"{nm}_yev{h}_{4 * (c - 1) + mu}")
                            elif tau == 0 and h >= 1:
                                p.wait(t, f"{nm}_yev{h - 1}_{12 + mu}")
                            lastmm = t.matmul(
                                PAV[mu][:, 0:129],
                                pbuf[:, (SEQI[(c, tau)] % 2) * 512 + mu * 128:
                                     (SEQI[(c, tau)] % 2) * 512 + (mu + 1) * 128],
                                vt[:, tau * 132: tau * 132 + 129],
                                start=(tau == 0), stop=(tau == gt1))
                            if tau == gt1:
                                p.inc(lastmm, nm + "_av", 1, mark=f"{nm}_av{h}_{gt1}")
                                lastmm = None
                        if lastmm is not None:
                            p.inc(lastmm, nm + "_avt", 1)
                        p.mark_here(nm + "_avt" if lastmm is None else nm + "_avt",
                                    f"{nm}_avt{h}_{c}_{tau}")
                    for mu in range(4 if _stg("av") else 0):
                        gt1 = 4 * c + mu
                        gy = h * NT + gt1
                        p.wait(t, f"{nm}_yev{h}_{gt1}")
                        if gy >= 2:
                            p.wait(t, f"{nm}_yfmev{h}_{gt1 - 2}" if gt1 >= 2 else
                                   f"{nm}_yfmev{h - 1}_{14 + gt1}")
                        p.inc(t.transpose(pbf[gt1 % 2][:, 0:128],
                                          ystg[:, ts(gt1 % 2)], cst["ident_bf"][:, :]),
                              nm + "_ytr", 1, mark=f"{nm}_ytr{h}_{gt1}")

        @block.scalar
        def _(s):
            for h in range(BH):
                for sec in range(3):
                    for c in range(4):
                        i = qi(h, sec, c)
                        pi = i % 2
                        p.wait(s, f"{nm}_qkv_i{i}")
                        if h >= 1 and sec < 2 and c == 0:
                            p.wait(s, f"{nm}_rope{h - 1}")
                        ps_ = pp[pi]
                        cs = slice(c * 512, (c + 1) * 512)
                        if sec == 0:
                            s.copy(ab[0:64, 0:T][:, cs], ps_[0:64, :])
                            p.inc(s.copy(ab[0:64, T: 2 * T][:, cs], ps_[64:128, :]),
                                  nm + "_qe", 1, mark=f"{nm}_qe_i{i}")
                        elif sec == 1:
                            s.copy(ab[64:128, 0:T][:, cs], ps_[64:128, :])
                            p.inc(s.copy(ab[64:128, T: 2 * T][:, cs], ps_[0:64, :]),
                                  nm + "_qe", 1, mark=f"{nm}_qe_i{i}")
                        else:
                            p.inc(s.copy(vf[:, cs], ps_[:, :]),
                                  nm + "_qe", 1, mark=f"{nm}_qe_i{i}")
                for j in range(NT):
                    gidx = h * NT + j
                    p.wait(s, f"{nm}_vtr_i{gidx}")
                    p.inc(s.copy(vt[:, j * 132: j * 132 + 128], pbf[j % 2][:, 0:128]),
                          nm + "_vtev", 1, mark=f"{nm}_vtev_i{gidx}")
                for gt1 in range(NT if _stg("bound") else 0):
                    nsc_ = (gt1 + 4) // 4
                    tgt = 5 - (2 + (nsc_ - 1) % 2)
                    p.wait(s, f"{nm}_btr{h}_{gt1}")
                    p.inc(s.activation(nbb[0:1, ts(gt1)], pp[tgt][0:1, 0:128],
                                       AF.Identity, scale=-1.0),
                          nm + "_nbe", 1, mark=f"{nm}_nbbev{h}_{gt1}")

                p.mark_here(nm + "_nbe", f"{nm}_negb{h}")
                for c in range(4 if _stg("scores") else 0):
                    for tau in range(4 * c + 4):
                        seq = SEQI[(c, tau)]
                        pi = seq % 2
                        if tau >= 4 * c:
                            p.wait(s, f"{nm}_smsk{h}_{c}_{tau}")
                        else:
                            p.wait(s, f"{nm}_smm{h}_{c}_{tau}")
                        if seq >= 2:
                            pc, pt = SEQ[seq - 2]
                            p.wait(s, f"{nm}_avt{h}_{pc}_{pt}")
                        elif h >= 1:
                            pc, pt = SEQ[len(SEQ) - 2 + seq]
                            p.wait(s, f"{nm}_avt{h - 1}_{pc}_{pt}")
                        off = max(0, (tau - 4 * c)) * 128
                        p.inc(s.activation(pbuf[:, pi * 512 + off: (pi + 1) * 512],
                                           pp[pi][:, off:512],
                                           AF.Exp, scale=RS_D),
                              nm + "_pev", 1, mark=f"{nm}_pev{h}_{c}_{tau}")
                    for mu in range(4 if _stg("av3") else 0):
                        gt1 = 4 * c + mu
                        gy = h * NT + gt1
                        p.waits(s, f"{nm}_av{h}_{gt1}", f"{nm}_rcp{h}_{gt1}")
                        if gy >= 2:
                            p.wait(s, f"{nm}_ytr{h}_{gt1 - 2}" if gt1 >= 2 else
                                   f"{nm}_ytr{h - 1}_{14 + gt1}")
                        p.inc(s.activation(ystg[:, ts(gt1 % 2)],
                                           PAV[mu][:, 0:128], AF.Identity,
                                           scale=rsum[:, mu: mu + 1]),
                              nm + "_yev", 1, mark=f"{nm}_yev{h}_{gt1}")
                    for mu in range(4 if _stg("av") else 0):
                        gt1 = 4 * c + mu
                        p.wait(s, f"{nm}_ytr{h}_{gt1}")
                        p.inc(s.copy(y_fm[:, h * T + gt1 * 128: h * T + (gt1 + 1) * 128],
                                     pbf[gt1 % 2][:, 0:128]),
                              nm + "_yfme", 1, mark=f"{nm}_yfmev{h}_{gt1}")
            p.mark_here(nm + "_yfme", nm + "_yfmdone")

        ch.close()
        if not _stg("full"):
            @block.gpsimd
            def _(gp):
                p.wait(gp, f"{nm}_rope{BH - 1}")
                p.inc(gp.dma_start(g["x_out"][0:128, :],
                                   vf[:, :]), nm + "_dbgo", 16, mark=nm + "_DBGOUT")
            return [nm + "_DBGOUT"]
        whalf = c1.enter_context(nc.sbuf_tensor(nm + "_wph", [128, 16 * 1024], BF16))
        xres = c1.enter_context(nc.sbuf_tensor(nm + "_xr", [128, 2 * 1024], F32))
        ores = c1.enter_context(nc.sbuf_tensor(nm + "_or", [128, 2 * 1024], F32))

        @block.gpsimd
        def _(gp):
            for half in range(2):
                p.wait(gp, nm + "_yfmdone" if half == 0 else f"{nm}_omm{NT - 1}")
                src = g["w_proj"][:, half * 1024: (half + 1) * 1024]
                srcr = src.rearrange("(a p) c -> p a c", p=128)
                dstr = whalf[:, :].rearrange("p (a c) -> p a c", c=1024)
                p.inc(gp.dma_start(dstr, srcr), nm + "_wpl", 16,
                      mark=f"{nm}_wp{half}")

        @block.tensor
        def _(t):
            for half in range(2):
                p.wait(t, f"{nm}_wp{half}")
                for k in range(NT):
                    gi = half * NT + k
                    pi = gi % 2
                    if gi >= 2:
                        p.wait(t, f"{nm}_oadd{gi - 2}")
                    last = None
                    for kk in range(ND):
                        lhs = y_fm[:, kk * T + k * 128: kk * T + (k + 1) * 128]
                        t.matmul(pp[pi][:, :], lhs,
                                 whalf[:, kk * 1024: kk * 1024 + 512],
                                 start=(kk == 0), stop=(kk == ND - 1))
                        last = t.matmul(pp[2 + pi][:, :], lhs,
                                        whalf[:, kk * 1024 + 512: (kk + 1) * 1024],
                                        start=(kk == 0), stop=(kk == ND - 1))
                    p.inc(last, nm + "_omm", 1, mark=f"{nm}_omm{gi}")

        @block.sync
        def _(sy):
            for half in range(2):
                for k in range(NT):
                    gi = half * NT + k
                    if gi < 2:
                        p.wait(sy, nm + "_yfmdone")
                    else:
                        p.wait(sy, f"{nm}_oadd{gi - 2}")
                    p.inc(sy.dma_start(xres[:, ts(gi % 2, 1024)],
                                       x_src[ts(k), half * 1024:(half + 1) * 1024]),
                          nm + f"_xld{gi % 2}", 16, mark=f"{nm}_xld{gi}")
                    p.wait(sy, f"{nm}_oadd{gi}")
                    p.inc(sy.dma_start(x_dst[ts(k), half * 1024:(half + 1) * 1024],
                                       ores[:, ts(gi % 2, 1024)]),
                          nm + f"_ost{gi % 2}", 16, mark=f"{nm}_ost{gi}")

        @block.vector
        def _(v):
            for half in range(2):
                for k in range(NT):
                    gi = half * NT + k
                    pi = gi % 2
                    p.waits(v, f"{nm}_omm{gi}", f"{nm}_xld{gi}")
                    if gi >= 2:
                        p.wait(v, f"{nm}_ost{gi - 2}")
                    v.tensor_tensor(ores[:, gi % 2 * 1024: gi % 2 * 1024 + 512],
                                    pp[pi][:, :], xres[:, gi % 2 * 1024: gi % 2 * 1024 + 512],
                                    ALU.add)
                    p.inc(v.tensor_tensor(ores[:, gi % 2 * 1024 + 512: gi % 2 * 1024 + 1024],
                                          pp[2 + pi][:, :],
                                          xres[:, gi % 2 * 1024 + 512: gi % 2 * 1024 + 1024],
                                          ALU.add),
                          nm + "_oaddv", 1, mark=f"{nm}_oadd{gi}")

    return [f"{nm}_ost{2 * NT - 2}", f"{nm}_ost{2 * NT - 1}"]



def emit_dequant(p, block, ctx, g, cst, nm, start_mark=None):
    nc = p.nc
    CH = 2816
    bt = [ctx.enter_context(nc.sbuf_tensor(nm + f"_bt{i}", [128, GU], U8)) for i in range(2)]
    sct = [ctx.enter_context(nc.sbuf_tensor(nm + f"_sc{i}", [128, GU], BF16)) for i in range(2)]
    nib = ctx.enter_context(nc.sbuf_tensor(nm + "_nib", [128, CH], U8))
    lw3 = ctx.enter_context(nc.sbuf_tensor(nm + "_lw3", [128, CH], U8))
    an8 = ctx.enter_context(nc.sbuf_tensor(nm + "_an8", [128, CH], U8))
    f3 = ctx.enter_context(nc.sbuf_tensor(nm + "_f3", [128, CH], BF16))
    r1 = ctx.enter_context(nc.sbuf_tensor(nm + "_r1", [128, CH], BF16))
    r2 = ctx.enter_context(nc.sbuf_tensor(nm + "_r2", [128, CH], BF16))
    sg = ctx.enter_context(nc.sbuf_tensor(nm + "_sg", [128, CH], BF16))
    a1 = ctx.enter_context(nc.sbuf_tensor(nm + "_a1", [128, CH], BF16))
    ot = [ctx.enter_context(nc.sbuf_tensor(nm + f"_ot{i}", [128, GU], BF16)) for i in range(2)]
    cb4 = ctx.enter_context(nc.sbuf_tensor(nm + "_cb4", [128, 1], F32))
    cb12 = ctx.enter_context(nc.sbuf_tensor(nm + "_cb12", [128, 1], F32))

    jobs = []
    for jb in range(8):
        for pl in range(2):
            jobs.append(("gu", jb, pl, GU, 4))
    for jb in range(22):
        for pl in range(2):
            jobs.append(("dn", jb, pl, D, 1))
    NJ = len(jobs)

    @block.gpsimd
    def _(gp):
        for ji, (mat, jb, pl, ncol, nch) in enumerate(jobs):
            if pl != 0:
                continue
            pi = ji // 2
            sl = pi % 2
            if pi >= 2:
                p.wait(gp, f"{nm}_vdone{2 * (pi - 2) + 1}")
            else:
                p.wait(gp, start_mark)
            src = g["gu_bytes"][ts(jb), :] if mat == "gu" else g["down_bytes"][ts(jb), :]
            p.inc(gp.dma_start(bt[sl][:, 0:ncol], src), nm + f"_bl{sl}", 16,
                  mark=f"{nm}_bld{pi}")
            ssrc = g["gu_scex"][ts(jb), :] if mat == "gu" else g["dn_scex"][ts(jb), :]
            p.inc(gp.dma_start(sct[sl][:, 0:ncol], ssrc), nm + f"_sl{sl}", 16,
                  mark=f"{nm}_sld{pi}")

    @block.vector
    def _(v):
        p.wait(v, "consts")
        p.wait(v, start_mark)
        v.memset(cb4[:, :], -4.0)
        p.inc(v.memset(cb12[:, :], -12.0), nm + "_ms", 1, mark=nm + "_dqms")
        prev_act = None
        for ji, (mat, jb, pl, ncol, nch) in enumerate(jobs):
            pi = ji // 2
            sl = pi % 2
            for ci in range(nch):
                cw = ncol // nch
                cs = slice(ci * cw, (ci + 1) * cw)
                p.waits(v, f"{nm}_bld{pi}", prev_act)
                bts = bt[sl][:, cs]
                if pl == 0:
                    p.selfsync(v, v.tensor_scalar(nib[:, 0:cw], bts, 15, None,
                                                  ALU.bitwise_and), nm + "_vq")
                else:
                    p.selfsync(v, v.tensor_scalar(nib[:, 0:cw], bts, 4, None,
                                                  ALU.logical_shift_right), nm + "_vq")
                p.selfsync(v, v.tensor_scalar(lw3[:, 0:cw], nib[:, 0:cw], 7, None,
                                              ALU.bitwise_and), nm + "_vq")
                p.inc(v.tensor_scalar(an8[:, 0:cw], nib[:, 0:cw], 8, None,
                                      ALU.bitwise_and), nm + "_vq1", 1,
                      mark=f"{nm}_vnib{ji}_{ci}")
                prev_act = f"{nm}_acts{ji}_{ci}"
                p.wait(v, prev_act)
                p.selfsync(v, v.tensor_tensor(a1[:, 0:cw], f3[:, 0:cw], r1[:, 0:cw],
                                              ALU.add), nm + "_vq")
                p.selfsync(v, v.tensor_tensor(a1[:, 0:cw], a1[:, 0:cw], r2[:, 0:cw],
                                              ALU.add), nm + "_vq")
                p.selfsync(v, v.tensor_tensor(a1[:, 0:cw], a1[:, 0:cw], sg[:, 0:cw],
                                              ALU.mult), nm + "_vq")
                p.wait(v, f"{nm}_sld{pi}")
                if ji >= 2 and ci == 0:
                    p.wait(v, f"{nm}_ost{ji - 2}")
                p.inc(v.tensor_tensor(ot[ji % 2][:, cs], a1[:, 0:cw],
                                      sct[sl][:, cs], ALU.mult),
                      nm + "_vq2", 1, mark=f"{nm}_val{ji}_{ci}")
            p.mark_here(nm + "_vq2", f"{nm}_vdone{ji}")

    @block.scalar
    def _(s):
        p.wait(s, nm + "_dqms")
        for ji, (mat, jb, pl, ncol, nch) in enumerate(jobs):
            for ci in range(nch):
                cw = ncol // nch
                p.wait(s, f"{nm}_vnib{ji}_{ci}")
                s.activation(f3[:, 0:cw], lw3[:, 0:cw], AF.Identity)
                s.activation(r1[:, 0:cw], lw3[:, 0:cw], AF.Relu, bias=cb4[:, :])
                s.activation(r2[:, 0:cw], lw3[:, 0:cw], AF.Relu, bias=cb12[:, :],
                             scale=2.0)
                p.inc(s.activation(sg[:, 0:cw], an8[:, 0:cw], AF.Identity,
                                   scale=-0.125, bias=cst["half_col"][:, :]),
                      nm + "_sqa", 1, mark=f"{nm}_acts{ji}_{ci}")

    @block.sync
    def _(sy):
        for ji, (mat, jb, pl, ncol, nch) in enumerate(jobs):
            p.wait(sy, f"{nm}_vdone{ji}")
            if mat == "gu":
                dst = g["wgu_bf"][pl * 1024 + 128 * jb: pl * 1024 + 128 * (jb + 1), :]
            else:
                dst = g["wdn_bf"][pl * 2816 + 128 * jb: pl * 2816 + 128 * (jb + 1), :]
            p.inc(sy.dma_start(dst, ot[ji % 2][:, 0:ncol]), nm + f"_os{ji % 2}", 16,
                  mark=f"{nm}_ost{ji}")

    @block.gpsimd
    def _(gp):
        p.waits(gp, f"{nm}_ost{NJ - 2}", f"{nm}_ost{NJ - 1}")
        p.inc(gp.nop(), nm + "_fin", 1, mark=nm + "_ALL")

    return nm + "_ALL"


def emit_dequant8(p, block, g, cst, nm, start_mark=None, ctx=None):
    nc = p.nc
    CH = 2816
    cq = ctx if ctx is not None else contextlib.ExitStack()
    if True:
        bt = [cq.enter_context(nc.sbuf_tensor(nm + f"_bt{i}", [128, CH], U8)) for i in range(2)]
        sct = [cq.enter_context(nc.sbuf_tensor(nm + f"_sc{i}", [128, CH], F16)) for i in range(2)]
        nib = cq.enter_context(nc.sbuf_tensor(nm + "_nib", [128, CH], U8))
        an8 = cq.enter_context(nc.sbuf_tensor(nm + "_an8", [128, CH], U8))
        f3 = cq.enter_context(nc.sbuf_tensor(nm + "_f3", [128, CH], BF16))
        r1 = cq.enter_context(nc.sbuf_tensor(nm + "_r1", [128, CH], BF16))
        r2 = cq.enter_context(nc.sbuf_tensor(nm + "_r2", [128, CH], BF16))
        sg = cq.enter_context(nc.sbuf_tensor(nm + "_sg", [128, CH], BF16))
        ot = [cq.enter_context(nc.sbuf_tensor(nm + "_ot0", [128, CH], BF16))] * 2
        shc = cq.enter_context(nc.sbuf_tensor(nm + "_sh", [128, 16], I32))
        hh8 = an8
        cb4 = cq.enter_context(nc.sbuf_tensor(nm + "_cb4", [128, 1], F32))
        cb12 = cq.enter_context(nc.sbuf_tensor(nm + "_cb12", [128, 1], F32))

        units = []
        for ji in range(2):
            for ci in range(4):
                units.append((ji, ci, CH,
                              ("gu", 0, ci), ji * 128, ci * CH))
        for s in range(6):
            units.append((2 + s, 0, D, ("dn", s, 0), s * 128, 0))
        NU = len(units)

        @block.sync
        def _(gp):
            p.wait(gp, start_mark)
            p.inc(gp.dma_start(shc[:, :], g["dq_sh"][:, :]), nm + "_shl", 16,
                  mark=nm + "_shld")
            for li, (ji, ci, cw, (mat, s, cc_), dr, dc) in enumerate(units):
                sl = li % 2
                if li >= 2:
                    p.wait(gp, f"{nm}_val{li - 2}")
                if mat == "gu":
                    bs = g["dqgu_b"][:, ci * CH:(ci + 1) * CH]
                    ss = g["dqgu_s"][:, ci * CH:(ci + 1) * CH]
                else:
                    bs = g["dqdn_b"][s * 128:(s + 1) * 128, :]
                    ss = g["dqdn_s"][s * 128:(s + 1) * 128, :]
                p.inc(gp.dma_start(bt[sl][:, 0:cw], bs), nm + f"_bl{sl}", 16,
                      mark=f"{nm}_bld{li}")
                p.inc(gp.dma_start(sct[sl][:, 0:cw], ss), nm + f"_sl{sl}", 16,
                      mark=f"{nm}_sld{li}")
                if li >= 1:
                    lp = li - 1
                    jp, cp, cwp, (mp, sp_, _), drp, dcp = units[lp]
                    p.wait(gp, f"{nm}_val{lp}")
                    dstp = (g["gu_loc"] if mp == "gu" else
                            g["dn_loc"])[drp:drp + 128, dcp:dcp + cwp]
                    p.inc(gp.dma_start(dstp, ot[0][:, 0:cwp]), nm + "_os0", 16,
                          mark=f"{nm}_ost{lp}")
                if li == NU - 1:
                    p.wait(gp, f"{nm}_val{li}")
                    dstp = (g["gu_loc"] if mat == "gu" else
                            g["dn_loc"])[dr:dr + 128, dc:dc + cw]
                    p.inc(gp.dma_start(dstp, ot[0][:, 0:cw]), nm + "_os0", 16,
                          mark=f"{nm}_ost{li}")

        @block.vector
        def _(v):
            p.waits(v, "consts", start_mark, nm + "_shld")
            v.memset(cb4[:, :], -4.0)
            p.inc(v.memset(cb12[:, :], -12.0), nm + "_ms", 1, mark=nm + "_dqms")
            for li, (ji, ci, cw, (mat, s, cc_), dr, dc) in enumerate(units):
                sl = li % 2
                p.wait(v, f"{nm}_bld{li}")
                if mat == "gu" and ji == 0:
                    p.selfsync(v, v.tensor_scalar(nib[:, 0:cw], bt[sl][:, 0:cw],
                                                  15, None, ALU.bitwise_and),
                               nm + "_vq")
                elif mat == "gu":
                    p.selfsync(v, v.tensor_scalar(nib[:, 0:cw], bt[sl][:, 0:cw],
                                                  4, None,
                                                  ALU.logical_shift_right),
                               nm + "_vq")
                else:
                    p.selfsync(v, v.tensor_scalar(hh8[:, 0:cw], bt[sl][:, 0:cw],
                                                  4, None,
                                                  ALU.logical_shift_right), nm + "_vq")
                    p.selfsync(v, v.tensor_scalar(hh8[:, 0:cw], hh8[:, 0:cw],
                                                  shc[:, 8 + ji:9 + ji], None,
                                                  ALU.bitwise_and), nm + "_vq")
                    p.selfsync(v, v.tensor_scalar(nib[:, 0:cw], bt[sl][:, 0:cw],
                                                  shc[:, ji:ji + 1], None,
                                                  ALU.bitwise_and), nm + "_vq")
                    p.selfsync(v, v.tensor_tensor(nib[:, 0:cw], nib[:, 0:cw],
                                                  hh8[:, 0:cw], ALU.bitwise_or),
                               nm + "_vq")
                p.selfsync(v, v.tensor_scalar(an8[:, 0:cw], nib[:, 0:cw], 8, None,
                                              ALU.bitwise_and), nm + "_vq")
                p.inc(v.tensor_scalar(nib[:, 0:cw], nib[:, 0:cw], 7, None,
                                      ALU.bitwise_and), nm + "_vq1", 1,
                      mark=f"{nm}_vnib{li}")
                p.wait(v, f"{nm}_acts{li}")
                p.selfsync(v, v.tensor_tensor(f3[:, 0:cw], f3[:, 0:cw],
                                              r1[:, 0:cw], ALU.add), nm + "_vq")
                p.selfsync(v, v.tensor_tensor(f3[:, 0:cw], f3[:, 0:cw],
                                              r2[:, 0:cw], ALU.add), nm + "_vq")
                p.selfsync(v, v.tensor_tensor(f3[:, 0:cw], f3[:, 0:cw],
                                              sg[:, 0:cw], ALU.mult), nm + "_vq")
                p.wait(v, f"{nm}_sld{li}")
                if li >= 1:
                    p.wait(v, f"{nm}_ost{li - 1}")
                p.inc(v.tensor_tensor(ot[sl][:, 0:cw], f3[:, 0:cw],
                                      sct[sl][:, 0:cw], ALU.mult),
                      nm + "_vq2", 1, mark=f"{nm}_val{li}")

        @block.scalar
        def _(s_):
            p.wait(s_, nm + "_dqms")
            for li, (ji, ci, cw, (mat, s, cc_), dr, dc) in enumerate(units):
                p.wait(s_, f"{nm}_vnib{li}")
                s_.activation(f3[:, 0:cw], nib[:, 0:cw], AF.Identity)
                s_.activation(r1[:, 0:cw], nib[:, 0:cw], AF.Relu, bias=cb4[:, :])
                s_.activation(r2[:, 0:cw], nib[:, 0:cw], AF.Relu, bias=cb12[:, :],
                              scale=2.0)
                p.inc(s_.activation(sg[:, 0:cw], an8[:, 0:cw], AF.Identity,
                                    scale=-0.125, bias=cst["half_col"][:, :]),
                      nm + "_sqa", 1, mark=f"{nm}_acts{li}")


    free_marks = [f"{nm}_ost{NU - 2}", f"{nm}_ost{NU - 1}", nm + "_dqms"]

    def emit_gather_inline(gp):
        p.waits(gp, f"{nm}_ost{NU - 2}", f"{nm}_ost{NU - 1}")
        ci1 = gp.collective_compute(
            "AllGather", ALU.bypass, replica_groups=[list(range(NCORES))],
            ins=[g["gu_loc"].ap().opt()], outs=[g["wgu_g"].ap().opt()])
        p.inc(ci1, nm + "_ccg", 1, mark=nm + "_CCG")
        p.wait(gp, nm + "_CCG")
        ci2 = gp.collective_compute(
            "AllGather", ALU.bypass, replica_groups=[list(range(NCORES))],
            ins=[g["dn_loc"].ap().opt()], outs=[g["wdn_g"].ap().opt()])
        p.inc(ci2, nm + "_ccd", 1, mark=nm + "_CCD")
        return [nm + "_CCG", nm + "_CCD"]

    def emit_gather():
        out = {}

        @block.gpsimd
        def _(gp):
            out["m"] = emit_gather_inline(gp)
        return out["m"]

    return free_marks, (emit_gather, emit_gather_inline)


DQMAP16 = [(kk % 8) * 2 + (kk // 8) for kk in range(16)]


def emit_ffn(p, block, ctx, g, cst, nm, x_src, x_dst, start_mark=None,
             dequant=False, gathered=False, dq_gate=None, early_gate=None):
    nc = p.nc
    EPS = 1e-6
    NB = 4
    BT = T // NB
    NBT = BT // 128
    NI = 44
    NOC = 22
    NKG = 11

    with contextlib.ExitStack() as c1:
        if dequant:
            with contextlib.ExitStack() as cq:
                dq_mark = emit_dequant(p, block, cq, g, cst, nm + "q",
                                       start_mark=start_mark)
            ln0_start = dq_mark
        else:
            dq_mark = None
            ln0_start = start_mark

        wst = [c1.enter_context(nc.sbuf_tensor(nm + f"_ws{i}", [128, 16 * 512], BF16)) for i in range(2)]
        u_fm = c1.enter_context(nc.sbuf_tensor(nm + "_ufm", [128, ND * T], BF16))
        hbuf = c1.enter_context(nc.sbuf_tensor(nm + "_h", [128, NI * BT], BF16))
        htmp = c1.enter_context(nc.sbuf_tensor(nm + "_ht", [128, 2 * BT], BF16))
        wk = [c1.enter_context(nc.sbuf_tensor(nm + f"_wk{i}", [128, 4 * 512], BF16)) for i in range(2)]
        ybuf = c1.enter_context(nc.sbuf_tensor(nm + "_y", [128, NBT * D], BF16))
        xrt = c1.enter_context(nc.sbuf_tensor(nm + "_xr", [128, D], F32))
        stf = c1.enter_context(nc.sbuf_tensor(nm + "_st", [128, 8], F32))
        pga = [c1.enter_context(nc.psum_tensor(nm + "_pga0", [128, 512], F32))]
        pgb = [c1.enter_context(nc.psum_tensor(nm + "_pgb0", [128, 512], F32))]
        pdn = [c1.enter_context(nc.psum_tensor(nm + f"_pd{i}", [128, 512], F32)) for i in range(4)]
        DPS = [pdn[0][:, :], pdn[1][:, :], pdn[2][:, :], pdn[3][:, :]]

        with contextlib.ExitStack() as c0:
            emit_ln(p, block, c0, g, cst, nm + "l", x_src, u_fm,
                    start_mark=ln0_start, bufs=2)
        pga.append(c1.enter_context(nc.psum_tensor(nm + "_pga1", [128, 512], F32)))
        pgb.append(c1.enter_context(nc.psum_tensor(nm + "_pgb1", [128, 512], F32)))

        @block.gpsimd
        def _(gp):
            for blk in range(NB):
                for oc in range(NOC):
                    gi = blk * NOC + oc
                    if gi >= 2:
                        p.wait(gp, f"{nm}_gmm{gi - 2}")
                    else:
                        p.wait(gp, early_gate if early_gate else f"{nm}l_done")
                        p.wait(gp, dq_gate[0] if gathered else dq_gate)
                    src = g["wgu_g" if gathered else "wgu_bf"][:, oc * 512:(oc + 1) * 512]
                    srcr = src.rearrange("(a p) c -> p a c", p=128)
                    dstr = wst[gi % 2][:, :].rearrange("p (a c) -> p a c", c=512)
                    p.inc(gp.dma_start(dstr, srcr), nm + f"_wl{gi % 2}", 16,
                          mark=f"{nm}_wld{gi}")
                for q in range(4):
                    gq = blk * 4 + q
                    for kg in range(NKG):
                        gk = gq * NKG + kg
                        if gk >= 2:
                            p.wait(gp, f"{nm}_dmk{gk - 2}")
                        elif dequant:
                            p.wait(gp, dq_mark)
                        else:
                            p.wait(gp, f"{nm}l_done")
                            p.wait(gp, dq_gate[1] if gathered else dq_gate)
                        src = g["wdn_g" if gathered else "wdn_bf"][kg * 512:(kg + 1) * 512,
                                                                   q * 512:(q + 1) * 512]
                        srcr = src.rearrange("(a p) c -> p a c", p=128)
                        dstr = wk[gk % 2][:, :].rearrange("p (a c) -> p a c", c=512)
                        p.inc(gp.dma_start(dstr, srcr), nm + f"_dl{gk % 2}", 16,
                              mark=f"{nm}_wdl{gk}")

        @block.tensor
        def _(t):
            for blk in range(NB):
                if blk == 0:
                    p.wait(t, f"{nm}l_done")
                for oc in range(NOC):
                    gi = blk * NOC + oc
                    p.wait(t, f"{nm}_wld{gi}")
                    for j in range(2):
                        hs_ = (gi * 2 + j) % 2
                        for gu_i in range(2):
                            oti = 2 * j + gu_i
                            ps_ = pga[hs_] if gu_i == 0 else pgb[hs_]
                            if gi > 0:
                                p.wait(t, f"{nm}_hev{gi - 1}_{j}")
                            last = None
                            for kk in range(ND):
                                kb = DQMAP16[kk] if gathered else kk
                                last = t.matmul(
                                    ps_[:, 0:512],
                                    wst[gi % 2][:, kb * 512 + oti * 128: kb * 512 + (oti + 1) * 128],
                                    u_fm[:, kk * T + blk * BT: kk * T + blk * BT + 512],
                                    start=(kk == 0), stop=(kk == ND - 1))
                            p.inc(last, nm + "_gm", 1, mark=f"{nm}_gmmh{gi}_{j}_{gu_i}")
                    p.mark_here(nm + "_gm", f"{nm}_gmm{gi}")
                for q in range(4):
                    gq = blk * 4 + q
                    if q == 0:
                        p.wait(t, f"{nm}_hdone{blk}")
                    for kg in range(NKG):
                        gk = gq * NKG + kg
                        p.wait(t, f"{nm}_wdl{gk}")
                        if kg == 0:
                            for tt in range(NBT):
                                gti = gq * NBT + tt
                                if gti >= NBT:
                                    p.wait(t, f"{nm}_dev{gti - NBT}")
                        last = None
                        for k4 in range(4):
                            kk = kg * 4 + k4
                            for tt in range(NBT):
                                last = t.matmul(
                                    DPS[tt],
                                    hbuf[:, kk * BT + tt * 128: kk * BT + (tt + 1) * 128],
                                    wk[gk % 2][:, k4 * 512:(k4 + 1) * 512],
                                    start=(kk == 0), stop=(kk == NI - 1))
                        p.inc(last, nm + "_dk", 1, mark=f"{nm}_dmk{gk}")
                    p.mark_here(nm + "_dk", f"{nm}_dmmq{gq}")

        @block.scalar
        def _(s):
            for blk in range(NB):
                for oc in range(NOC):
                    gi = blk * NOC + oc
                    for j in range(2):
                        hs_ = (gi * 2 + j) % 2
                        p.wait(s, f"{nm}_gmmh{gi}_{j}_{0}")
                        if gi > 0:
                            p.wait(s, f"{nm}_hev{gi - 1}_{j}")
                        p.inc(s.activation(htmp[:, hs_ * BT:(hs_ + 1) * BT],
                                           pga[hs_][:, 0:512], AF.Silu),
                              nm + "_si", 1, mark=f"{nm}_silu{gi}_{j}")
                for q in range(4):
                    gq = blk * 4 + q
                    for tt in range(NBT):
                        gti = gq * NBT + tt
                        p.wait(s, f"{nm}_dmmq{gq}")
                        p.inc(s.copy(ybuf[:, tt * D + q * 512: tt * D + (q + 1) * 512],
                                     DPS[tt]),
                              nm + "_de", 1, mark=f"{nm}_dev{gti}")
                for tt in range(NBT):
                    SSF = nm + "_sch"
                    p.wait(s, f"{nm}_dev{(blk * 4 + 3) * NBT + tt}")
                    p.selfsync(s, s.activation(htmp[:, 0:BT],
                                               ybuf[:, tt * D: tt * D + 512],
                                               AF.Square, accum_out=stf[:, 0:1]), SSF)
                    p.selfsync(s, s.activation(htmp[:, 0:BT],
                                               ybuf[:, tt * D + 512: tt * D + 1024],
                                               AF.Square, accum_out=stf[:, 4:5]), SSF)
                    p.selfsync(s, s.activation(htmp[:, 0:BT],
                                               ybuf[:, tt * D + 1024: tt * D + 1536],
                                               AF.Square, accum_out=stf[:, 5:6]), SSF)
                    p.selfsync(s, s.activation(htmp[:, 0:BT],
                                               ybuf[:, tt * D + 1536: tt * D + 2048],
                                               AF.Square, accum_out=stf[:, 6:7]), SSF)
                    p.inc(s.nop(), nm + "_sqd", 1, mark=f"{nm}_ss{blk}_{tt}")
                    p.wait(s, f"{nm}_var{blk}_{tt}")
                    p.selfsync(s, s.activation(stf[:, 2:3], stf[:, 1:2], AF.Ln), SSF)
                    p.inc(s.activation(stf[:, 3:4], stf[:, 2:3], AF.Exp, scale=-0.5),
                          nm + "_rst", 1, mark=f"{nm}_rstd{blk}_{tt}")

        @block.vector
        def _(v):
            for blk in range(NB):
                for oc in range(NOC):
                    gi = blk * NOC + oc
                    for j in range(2):
                        p.waits(v, f"{nm}_silu{gi}_{j}", f"{nm}_gmmh{gi}_{j}_{1}")
                        itile = oc * 2 + j
                        hs_ = (gi * 2 + j) % 2
                        p.inc(v.tensor_tensor(hbuf[:, itile * BT:(itile + 1) * BT],
                                              htmp[:, hs_ * BT:(hs_ + 1) * BT],
                                              pgb[hs_][:, 0:512], ALU.mult),
                              nm + "_hm", 1, mark=f"{nm}_hev{gi}_{j}")
                p.mark_here(nm + "_hm", f"{nm}_hdone{blk}")
                for tt in range(NBT):
                    gti2 = blk * NBT + tt
                    SSV = nm + "_vch"
                    p.wait(v, f"{nm}_ss{blk}_{tt}")
                    p.selfsync(v, v.tensor_tensor(stf[:, 1:2], stf[:, 0:1],
                                                  stf[:, 4:5], ALU.add), SSV)
                    p.selfsync(v, v.tensor_tensor(stf[:, 7:8], stf[:, 5:6],
                                                  stf[:, 6:7], ALU.add), SSV)
                    p.selfsync(v, v.tensor_tensor(stf[:, 1:2], stf[:, 1:2],
                                                  stf[:, 7:8], ALU.add), SSV)
                    p.selfsync(v, v.tensor_scalar(stf[:, 1:2], stf[:, 1:2],
                                                  1.0 / D, EPS, ALU.mult, ALU.add),
                               SSV)
                    p.inc(v.tensor_scalar(stf[:, 4:5], stf[:, 1:2], 0.0, None,
                                          ALU.add), nm + "_vv", 1,
                          mark=f"{nm}_var{blk}_{tt}")
                    p.waits(v, f"{nm}_rstd{blk}_{tt}", f"{nm}_xld{gti2}")
                    if gti2 >= 1:
                        p.wait(v, f"{nm}_ost{gti2 - 1}")
                    p.inc(v.scalar_tensor_tensor(xrt[:, :],
                                                 ybuf[:, tt * D:(tt + 1) * D],
                                                 stf[:, 3:4], xrt[:, :],
                                                 ALU.mult, ALU.add),
                          nm + "_oa", 1, mark=f"{nm}_oadd{gti2}")

        @block.sync
        def _(sy):
            for blk in range(NB):
                for tt in range(NBT):
                    gti2 = blk * NBT + tt
                    if gti2 >= 1:
                        p.wait(sy, f"{nm}_ost{gti2 - 1}")
                    else:
                        p.wait(sy, f"{nm}l_done")
                    p.inc(sy.dma_start(xrt[:, :], x_src[ts(gti2), :]),
                          nm + "_xl", 16, mark=f"{nm}_xld{gti2}")
                    p.wait(sy, f"{nm}_oadd{gti2}")
                    p.inc(sy.dma_start(x_dst[ts(gti2), :], xrt[:, :]),
                          nm + "_xo", 16, mark=f"{nm}_ost{gti2}")

    return [f"{nm}_ost{NB * NBT - 1}"]



def emit_attn_a(p, block, ctx, g, cst, x_src, x_dst, start_mark=None,
                with_dq=False, dq_out=None):
    import os as _os
    astage = _os.environ.get("ATA_STAGE", "a3")
    ngg = int(_os.environ.get("ATA_NGG", "4"))
    aout = int(_os.environ.get("ATA_OUT", "1"))
    lvl = int(_os.environ.get("ATA_LVL", "9"))
    skips = set(_os.environ.get("ATA_SKIP", "").split(","))
    nm = "ata"
    nc = p.nc
    c1px = contextlib.ExitStack()
    ctx = c1px
    NBLK = [(i * 512, 512) for i in range(16)] + [(8192, 192)]
    XB = 515
    NEB = 33
    NFT = 32

    dttok = ctx.enter_context(nc.sbuf_tensor(nm + "_dtt", [128, 1024], F32))
    biasbc = ctx.enter_context(nc.sbuf_tensor(nm + "_bb", [128, 64], F32))
    Abc = ctx.enter_context(nc.sbuf_tensor(nm + "_ab", [128, 64], F32))
    Araw = ctx.enter_context(nc.sbuf_tensor(nm + "_ar", [128, 64], F32))
    Dbc = ctx.enter_context(nc.sbuf_tensor(nm + "_db", [128, 64], F32))
    Dexp = ctx.enter_context(nc.sbuf_tensor(nm + "_de", [128, 32], F32))
    cw_sb = ctx.enter_context(nc.sbuf_tensor(nm + "_cw", [128, 132], F32))
    cb_sb = ctx.enter_context(nc.sbuf_tensor(nm + "_cb", [128, 33], F32))

    @block.sync
    def _(sy):
        p.wait(sy, start_mark)
        p.inc(sy.dma_start(biasbc[0:1, :], g["dt_bias"][:, :]), nm + "_su", 16)
        p.inc(sy.dma_start(Araw[0:1, :], g["a_log"][:, :]), nm + "_su", 16)
        p.inc(sy.dma_start(Dbc[0:1, :], g["d_vec"][:, :]), nm + "_su", 16)
        p.inc(sy.dma_start(
            cw_sb[:, :].rearrange("p (a k) -> p a k", k=4),
            g["conv_w"][:, :].rearrange("(a p) k -> p a k", p=128)),
            nm + "_su", 16)
        p.inc(sy.dma_start(cb_sb[:, :], g["conv_b"][:, :]), nm + "_su", 16,
              mark=nm + "_su0")
        for step in range(7):
            lo, hi = 1 << step, 1 << (step + 1)
            p.wait(sy, nm + f"_dbl{step - 1}" if step else nm + "_su0")
            for tns in (biasbc, Araw, Dbc):
                p.inc(sy.dma_start(tns[lo:hi, :], tns[0:lo, :]), nm + "_dbs", 16)
            p.mark_here(nm + "_dbs", nm + f"_dbl{step}")

    @block.scalar
    def _(s):
        p.wait(s, nm + "_dbl6")
        p.inc(s.activation(Abc[:, :], Araw[:, :], AF.Exp), nm + "_sue", 1,
              mark=nm + "_aexp")

    @block.vector
    def _(v):
        p.wait(v, nm + "_aexp")
        p.selfsync(v, v.tensor_scalar(Abc[:, :], Abc[:, :], -1.0, None,
                                      ALU.mult), nm + "_suv")
        p.mark_here(nm + "_suv", nm + "_abc")
        p.wait(v, nm + "_dbl6")
        SSD = nm + "_suv"
        for ft in range(NFT):
            p.selfsync(v, v.tensor_scalar(Dexp[0:64, ft:ft + 1],
                                          Dbc[0:64, 2 * ft:2 * ft + 1],
                                          0.0, None, ALU.add), SSD)
            p.selfsync(v, v.tensor_scalar(Dexp[64:128, ft:ft + 1],
                                          Dbc[64:128, 2 * ft + 1:2 * ft + 2],
                                          0.0, None, ALU.add), SSD)
        p.wait(v, "consts")
        p.inc(v.tensor_scalar(Dexp[0:64, 0:1], Dexp[0:64, 0:1], 0.0, None,
                              ALU.add), nm + "_sud", 1, mark=nm + "_dexp")

    with contextlib.ExitStack() as c2:
        u_fm = c2.enter_context(nc.sbuf_tensor(nm + "_ufm", [128, ND * T], BF16))
        emit_ln(p, block, c2, g, cst, "atl", x_src, u_fm, rms_mode=True,
                xp_dst=g["x_res"], start_mark=start_mark, bufs=2)
        wbuf = [c2.enter_context(nc.sbuf_tensor(nm + f"_wb{i}", [128, 16 * 512], BF16)) for i in range(2)]
        dq_gate = None
        gath_inline = None
        if with_dq:
            with contextlib.ExitStack() as cdq:
                dq_free, (emit_gather, gath_inline) = emit_dequant8(
                    p, block, g, cst, "dq8", start_mark=start_mark, ctx=cdq)
            dq_out["gather"] = emit_gather
            dq_out["free"] = dq_free
            dq_out["gmarks"] = ["dq8_CCG", "dq8_CCD"]
            dq_gate = dq_free

        if astage == "a1":
            c1px.close()
            return ["atl_done", "atl_xpdone0", "atl_xpdone1"]
        xibt = c2.enter_context(nc.sbuf_tensor(nm + "_xi", [128, 16 * 512], BF16))
        ztmp = [c2.enter_context(nc.sbuf_tensor(nm + f"_zt{i}", [128, 512], BF16)) for i in range(2)]
        wot = [c2.enter_context(nc.sbuf_tensor(nm + f"_wo{i}", [128, D], F32)) for i in range(2)]
        nw_col = c2.enter_context(nc.sbuf_tensor(nm + "_nw", [128, 32], F32))
        nwt = c2.enter_context(nc.sbuf_tensor(nm + "_nwt", [32, 128], F32))
        pp = [c2.enter_context(nc.psum_tensor(nm + f"_pp{i}", [128, 512], F32)) for i in range(2)]
        pt = [c2.enter_context(nc.psum_tensor(nm + f"_pt{i}", [128, 1024], BF16)) for i in range(2)]
        ppc = c2.enter_context(nc.psum_tensor(nm + "_ppc", [128, 512], F32))

        @block.sync
        def _(sy):
            p.wait(sy, start_mark)
            p.inc(sy.dma_start(nwt[:, :], g["norm_w"][:, :]),
                  nm + "_pcn", 16, mark=nm + "_nwr")

        @block.sync
        def _(sy):
            for ib in range(8):
                n0 = ib * 512
                for cc in range(16):
                    zi = ib * 16 + cc
                    p.wait(sy, nm + f"_aze{zi}")
                    p.inc(sy.dma_start(g["z_hbm"][cc * 128:(cc + 1) * 128, n0:n0 + 512],
                                       ztmp[cc % 2][:, :]),
                          nm + f"_azq{cc % 2}", 16, mark=nm + f"_azs{zi}")
            for ib in range(8, 17):
                n0, nw = NBLK[ib]
                nft = 4 if ib < 16 else 1
                f0 = n0 - 4096
                p.wait(sy, nm + f"_atvd{ib}")
                for j in range(nft):
                    srcj = xibt[:, 0:16 * 512].rearrange(
                        "p (cc r) -> p cc r", r=512)[:, :, j * 128:(j + 1) * 128]
                    dstj = g["xbcf_hbm"][f0 + j * 128: f0 + (j + 1) * 128, :].rearrange(
                        "(o p) (cc t) -> p (o cc) t", p=128, t=128)
                    p.inc(sy.dma_start(dstj, srcj), nm + "_axq", 16)
                p.mark_here(nm + "_axq", nm + f"_axo{ib}")
            p.mark_here(nm + "_axq", nm + "_a2done")

        @block.sync
        def _(sy):
            p.waits(sy, start_mark, dq_gate)
            for et in range(32):
                if et >= 2:
                    p.wait(sy, nm + f"_pco{et - 2}")
                p.inc(sy.dma_start(wot[et % 2][:, :],
                                   g["w_outproj"][et * 128:(et + 1) * 128, :]),
                      nm + f"_pcl{et % 2}", 16, mark=nm + f"_pcl{et}")

        @block.tensor
        def _(t):
            p.waits(t, nm + "_nwr", "consts_s")
            p.inc(t.transpose(ppc[:, 0:32], nwt[:, :], cst["ident_f"][0:32, 0:32]),
                  nm + "_pct", 1, mark=nm + "_nwt")

        @block.scalar
        def _(s):
            p.wait(s, nm + "_nwt")
            p.inc(s.activation(nw_col[:, :], ppc[:, 0:32], AF.Identity),
                  nm + "_pce", 1, mark=nm + "_nwc")

        @block.vector
        def _(v):
            p.wait(v, nm + "_nwc")
            for et in range(32):
                p.wait(v, nm + f"_pcl{et}")
                if et >= 2:
                    p.wait(v, nm + f"_pco{et - 2}")
                p.inc(v.tensor_scalar(wot[et % 2][:, :], wot[et % 2][:, :],
                                      nw_col[:, et:et + 1],
                                      None, ALU.mult), nm + "_pcm", 1,
                      mark=nm + f"_pcm{et}")

        @block.gpsimd
        def _(gp):
            for ib, (n0, nw) in enumerate(NBLK):
                if ib >= 2:
                    p.wait(gp, nm + f"_aev{ib * 16 - 17}")
                else:
                    p.wait(gp, start_mark)
                src = g["w_inproj"][:, n0:n0 + nw].rearrange("(a p) c -> p a c", p=128)
                dst = wbuf[ib % 2][:, :].rearrange("p (a c) -> p a c", c=512)[:, :, 0:nw]
                p.inc(gp.dma_start(dst, src), nm + f"_awl{ib % 2}", 16,
                      mark=nm + f"_awld{ib}")

        @block.gpsimd
        def _(gp):
            for et in range(32):
                p.wait(gp, nm + f"_pcm{et}")
                p.inc(gp.dma_start(g["wo_bf"][et * 128:(et + 1) * 128, :],
                                   wot[et % 2][:, :]),
                      nm + f"_pco{et % 2}", 16, mark=nm + f"_pco{et}")
            p.mark_here(nm + "_pco0", nm + "_pcdone0")
            p.mark_here(nm + "_pco1", nm + "_pcdone")

        @block.tensor
        def _(t):
            for ib, (n0, nw) in enumerate(NBLK):
                p.wait(t, nm + f"_awld{ib}")
                for cc in range(16):
                    gi = ib * 16 + cc
                    if gi >= 2:
                        p.wait(t, nm + f"_aev{gi - 2}")
                    if ib == 0:
                        p.wait(t, f"atl_ev{cc}")
                    mm = None
                    for dt_ in range(16):
                        mm = t.matmul(pp[gi % 2][:, 0:nw],
                                      u_fm[:, dt_ * T + cc * 128: dt_ * T + (cc + 1) * 128],
                                      wbuf[ib % 2][:, dt_ * 512: dt_ * 512 + nw],
                                      start=(dt_ == 0), stop=(dt_ == 15))
                    p.inc(mm, nm + "_amm", 1, mark=nm + f"_amm{gi}")
                if ib >= 8:
                    nft = 4 if ib < 16 else 1
                    for cc in range(16):
                        for j in range(nft):
                            idx = cc * nft + j
                            p.wait(t, nm + f"_axv{ib}_{cc}")
                            if idx >= 2:
                                p.wait(t, nm + f"_atv{ib}_{idx - 2}")
                            p.inc(t.transpose(pt[idx % 2][:, 0:128],
                                              xibt[:, cc * 512 + j * 128: cc * 512 + (j + 1) * 128],
                                              cst["ident_bf"][:, :]),
                                  nm + "_atr", 1, mark=nm + f"_atr{ib}_{idx}")

        @block.scalar
        def _(s):
            for ib, (n0, nw) in enumerate(NBLK):
                nft = 4 if ib < 16 else 1
                for cc in range(16):
                    gi = ib * 16 + cc
                    p.wait(s, nm + f"_amm{gi}")
                    if ib < 8:
                        zi = ib * 16 + cc
                        if zi >= 2:
                            p.wait(s, nm + f"_azs{zi - 2}")
                        else:
                            p.wait(s, dq_gate)
                        p.inc(s.activation(ztmp[cc % 2][:, :], pp[gi % 2][:, 0:512],
                                           AF.Identity), nm + "_aze", 1,
                              mark=nm + f"_aze{zi}")
                        p.mark_here(nm + "_aze", nm + f"_aev{gi}")
                    else:
                        if ib > 8:
                            pnf = 4 if ib - 1 < 16 else 1
                            p.wait(s, nm + f"_atr{ib - 1}_{cc * pnf + pnf - 1}")
                            if cc == 0:
                                p.wait(s, nm + f"_axo{ib - 1}")
                        elif cc == 0:
                            p.wait(s, dq_gate)
                        p.inc(s.activation(xibt[:, cc * 512: cc * 512 + nft * 128],
                                           pp[gi % 2][:, 0:nft * 128], AF.Identity),
                              nm + "_axv", 1, mark=nm + f"_axv{ib}_{cc}")
                        if ib == 16:
                            p.inc(s.activation(dttok[:, cc * 64:(cc + 1) * 64],
                                               pp[gi % 2][:, 128:192], AF.Identity),
                                  nm + "_adt", 1, mark=nm + f"_adt{cc}")
                            p.mark_here(nm + "_adt", nm + f"_aev{gi}")
                        else:
                            p.mark_here(nm + "_axv", nm + f"_aev{gi}")
                if ib >= 8:
                    for cc in range(16):
                        for j in range(nft):
                            idx = cc * nft + j
                            p.wait(s, nm + f"_atr{ib}_{idx}")
                            p.inc(s.activation(xibt[:, cc * 512 + j * 128: cc * 512 + (j + 1) * 128],
                                               pt[idx % 2][:, 0:128], AF.Identity),
                                  nm + "_atv", 1, mark=nm + f"_atv{ib}_{idx}")
                    p.mark_here(nm + "_atv", nm + f"_atvd{ib}")

    if astage == "a2":
        c1px.close()
        return [nm + "_a2done", nm + "_azs126", nm + "_azs127",
                nm + "_pcdone", nm + "_pcdone0", "atl_xpdone0", "atl_xpdone1"]
    with contextlib.ExitStack() as c3:
        xbcbuf = c3.enter_context(nc.sbuf_tensor(nm + "_xc", [128, NEB * XB], BF16))
        St = c3.enter_context(nc.sbuf_tensor(nm + "_st", [64, D_INNER], F32))
        St_bf = c3.enter_context(nc.sbuf_tensor(nm + "_stb", [64, D_INNER], BF16))
        dDiag = c3.enter_context(nc.sbuf_tensor(nm + "_dd", [128, NFT * 128], BF16))
        x_tok = c3.enter_context(nc.sbuf_tensor(nm + "_xt", [128, D_INNER], BF16))
        xsc = c3.enter_context(nc.sbuf_tensor(nm + "_xs", [128, D_INNER], BF16))
        y_sb = c3.enter_context(nc.sbuf_tensor(nm + "_ys", [128, D_INNER], BF16))
        yn_fm = c3.enter_context(nc.sbuf_tensor(nm + "_yf", [128, NFT * 512], BF16))
        zbuf = c3.enter_context(nc.sbuf_tensor(nm + "_zb", [128, D_INNER], BF16))
        isb = c3.enter_context(nc.sbuf_tensor(nm + "_is", [128, 512], BF16))
        M0 = c3.enter_context(nc.sbuf_tensor(nm + "_m0", [128, 512], BF16))
        Mh = c3.enter_context(nc.sbuf_tensor(nm + "_mh", [128, 512], BF16))
        G_sb = c3.enter_context(nc.sbuf_tensor(nm + "_gs", [128, 128], BF16))
        btok = c3.enter_context(nc.sbuf_tensor(nm + "_bt", [128, 64], BF16))
        C0 = c3.enter_context(nc.sbuf_tensor(nm + "_c0", [64, 128], BF16))
        sgt2 = c3.enter_context(nc.sbuf_tensor(nm + "_sg", [128, 128], F32))
        sgn2 = c3.enter_context(nc.sbuf_tensor(nm + "_sn", [128, 128], F32))
        esg2 = c3.enter_context(nc.sbuf_tensor(nm + "_es", [128, 128], F32))
        esl2 = c3.enter_context(nc.sbuf_tensor(nm + "_el", [128, 128], F32))
        T2p2 = c3.enter_context(nc.sbuf_tensor(nm + "_t2", [128, 128], F32))
        dtA2 = c3.enter_context(nc.sbuf_tensor(nm + "_da", [128, 128], F32))
        dtv2 = c3.enter_context(nc.sbuf_tensor(nm + "_dv", [128, 128], F32))
        sp1 = c3.enter_context(nc.sbuf_tensor(nm + "_s1", [128, 64], F32))
        sp2 = c3.enter_context(nc.sbuf_tensor(nm + "_s2", [128, 64], F32))
        spm = c3.enter_context(nc.sbuf_tensor(nm + "_s3", [128, 64], F32))
        spe = c3.enter_context(nc.sbuf_tensor(nm + "_s4", [128, 64], F32))
        spl = c3.enter_context(nc.sbuf_tensor(nm + "_s5", [128, 64], F32))
        tm1 = c3.enter_context(nc.sbuf_tensor(nm + "_t1", [128, 64], F32))
        tm2 = c3.enter_context(nc.sbuf_tensor(nm + "_t3", [128, 64], F32))
        sT2 = c3.enter_context(nc.sbuf_tensor(nm + "_sT", [64, 256], F32))
        wos = [c3.enter_context(nc.sbuf_tensor(nm + f"_ws{i}", [128, 32 * 128], BF16)) for i in range(2)]
        otok = [c3.enter_context(nc.sbuf_tensor(nm + f"_ot{i}", [128, 1024], F32)) for i in range(4)]
        xresh = c3.enter_context(nc.sbuf_tensor(nm + "_xr", [128, 1024], F32))
        ofm = c3.enter_context(nc.sbuf_tensor(nm + "_of", [128, 512], BF16))
        sgt, sgn, esg, esl = (sgt2[:, 0:64], sgn2[:, 0:64], esg2[:, 0:64],
                              esl2[:, 0:64])
        T2p, dtA, dtv = T2p2[:, 0:64], dtA2[:, 0:64], dtv2[:, 0:64]
        sT = sT2[:, 0:128]
        sR = c3.enter_context(nc.sbuf_tensor(nm + "_sR", [33, 4096], F32))
        stf = c3.enter_context(nc.sbuf_tensor(nm + "_sf", [128, 8], F32))
        dcv = [c3.enter_context(nc.sbuf_tensor(nm + f"_dc{i}", [128, 512], BF16)) for i in range(2)]
        pb0 = c3.enter_context(nc.psum_tensor(nm + "_p0", [128, 512], F32))
        pb1 = c3.enter_context(nc.psum_tensor(nm + "_p1", [128, 512], F32))
        pb2 = c3.enter_context(nc.psum_tensor(nm + "_p2", [128, 1024], BF16))
        pb3 = c3.enter_context(nc.psum_tensor(nm + "_p3", [128, 1024], BF16))
        pb4 = c3.enter_context(nc.psum_tensor(nm + "_p4", [128, 512], F32))
        pb5 = c3.enter_context(nc.psum_tensor(nm + "_p5", [128, 512], F32))
        pb6 = c3.enter_context(nc.psum_tensor(nm + "_p6", [128, 512], F32))
        pb7 = c3.enter_context(nc.psum_tensor(nm + "_p7", [128, 512], F32))

        def ebcol(eb, k):
            return xbcbuf[:, eb * XB + k: eb * XB + k + 512]

        def ebtile(eb, lc, rows=slice(0, 128)):
            c0 = eb * XB + 3 + lc * 128
            return xbcbuf[rows, c0:c0 + 128]

        @block.gpsimd
        def _(gp):
            for gg in range(ngg):
                if gg == 0:
                    p.waits(gp, nm + "_a2done", nm + "_hst",
                            "atl_xpdone0", "atl_xpdone1", nm + "_pcdone", nm + "_pcdone0",
                            nm + "_azs126", nm + "_azs127")
                else:
                    p.wait(gp, nm + f"_yev{gg * 4 - 1}")
                c0 = gg * 512 - 3
                off = 0 if gg else 3
                ncols = 515 if gg else 512
                srcd = g["xbcf_hbm"][:, max(c0, 0):gg * 512 + 512].rearrange(
                    "(a p) c -> p a c", p=128)
                dst = xbcbuf[:, :].rearrange("p (a c) -> p a c", c=XB)[:, :, off:off + ncols]
                p.inc(gp.dma_start(dst, srcd), nm + "_gld", 16, mark=nm + f"_gld{gg}")
                if gg == 0 and gath_inline is not None:
                    gath_inline(gp)
                for dh in range(2 if aout else 0):
                    for db in range(8):
                        gwo = gg * 16 + dh * 8 + db
                        if gwo < 2:
                            p.waits(gp, nm + "_pcdone", nm + "_pcdone0", nm + "_a2done",
                                    "atl_xpdone0", "atl_xpdone1")
                        else:
                            p.wait(gp, nm + f"_opm{gwo - 2}")
                        dcol = (dh * 8 + db) * 128
                        srcw = g["wo_bf"][:, dcol:dcol + 128].rearrange(
                            "(a p) c -> p a c", p=128)
                        dstw = wos[gwo % 2][:, :].rearrange("p (a c) -> p a c", c=128)
                        p.inc(gp.dma_start(dstw, srcw), nm + f"_wol{gwo % 2}", 16,
                              mark=nm + f"_wol{gwo}")

        @block.sync
        def _(sy):
            for gg in range(ngg):
                for lc in range(4):
                    cc = gg * 4 + lc
                    p.wait(sy, nm + f"_azs{7 * 16 + cc}")
                    if cc:
                        p.wait(sy, nm + f"_zsl{cc - 1}")
                    else:
                        p.waits(sy, nm + "_a2done", "atl_xpdone0",
                                "atl_xpdone1", nm + "_pcdone", nm + "_pcdone0",
                                nm + "_azs126", nm + "_azs127")
                    if lvl >= 5:
                        p.inc(sy.dma_start(zbuf[:, :], g["z_hbm"][cc * 128:(cc + 1) * 128, :]),
                              nm + "_zld", 16, mark=nm + f"_zld{cc}")
                    if cc == 0:
                        p.waits(sy, nm + "_sTe0", dq_gate)
                        p.inc(sy.dma_start(g["sg_hbm"][:, :],
                                           sT2[:, 0:128]),
                              nm + "_sgw", 16, mark=nm + "_sgw0")
                        for hs in range(2):
                            p.wait(sy, nm + "_sgw0")
                            p.inc(sy.dma_start(
                                sR[hs * 32:hs * 32 + 1, :],
                                g["sg_hbm"][hs * 32:(hs + 1) * 32, :].rearrange(
                                    "(o h) t -> o (h t)", o=1)),
                                nm + f"_sRl{hs}", 16, mark=nm + f"_sR0_{hs}")
                    if cc + 1 < 16:
                        nxp = ((cc + 1) % 2) * 128
                        p.waits(sy, nm + f"_sTe{cc + 1}", nm + f"_sR{cc}_0",
                                nm + f"_sR{cc}_1")
                        p.inc(sy.dma_start(g["sg_hbm"][:, :],
                                           sT2[:, nxp:nxp + 128]),
                              nm + "_sgw", 16, mark=nm + f"_sgw{cc + 1}")
                        for hs in range(2):
                            p.waits(sy, nm + f"_sgw{cc + 1}",
                                    nm + f"_bcm{cc * 16 + hs * 8 + 7}")
                            p.inc(sy.dma_start(
                                sR[hs * 32:hs * 32 + 1, :],
                                g["sg_hbm"][hs * 32:(hs + 1) * 32, :].rearrange(
                                    "(o h) t -> o (h t)", o=1)),
                                nm + f"_sRl{hs}", 16, mark=nm + f"_sR{cc + 1}_{hs}")
                for dh in range(2 if aout else 0):
                    for lc in range(4):
                        cc = gg * 4 + lc
                        oi = (gg * 2 + dh) * 4 + lc
                        p.wait(sy, f"atl_xp{cc}")
                        if oi >= 1:
                            p.wait(sy, nm + f"_ora{oi - 1}")
                        else:
                            p.waits(sy, nm + "_a2done", "atl_xpdone0",
                                    "atl_xpdone1", nm + "_pcdone", nm + "_pcdone0",
                                    nm + "_azs126", nm + "_azs127")
                        p.inc(sy.dma_start(xresh[:, :],
                                           g["x_res"][cc * 128:(cc + 1) * 128,
                                                      dh * 1024:(dh + 1) * 1024]),
                              nm + "_xrl", 16, mark=nm + f"_xrl{oi}")
                        p.wait(sy, nm + f"_ora{oi}")
                        p.inc(sy.dma_start(x_dst[cc * 128:(cc + 1) * 128,
                                                 dh * 1024:(dh + 1) * 1024],
                                           otok[lc][:, :]),
                              nm + f"_osl{lc}", 16, mark=nm + f"_ost{oi}")

        def SL(n):
            po = (n % 2) * 64
            return (sgt2[:, po:po + 64], sgn2[:, po:po + 64],
                    esg2[:, po:po + 64], esl2[:, po:po + 64],
                    T2p2[:, po:po + 64], dtA2[:, po:po + 64],
                    dtv2[:, po:po + 64],
                    sT2[:, (n % 2) * 128:(n % 2) * 128 + 128])

        SS = nm + "_vch"
        SSA = nm + "_sch"

        def dve_sp1(v, n):
            p.waits(v, nm + f"_adt{n}", nm + "_dbl6")
            p.selfsync(v, v.tensor_tensor(sp1[:, :], dttok[:, n * 64:(n + 1) * 64],
                                          biasbc[:, :], ALU.add), SS)
            p.selfsync(v, v.tensor_scalar(sp2[:, :], sp1[:, :], -1.0, None,
                                          ALU.mult), SS)
            p.inc(v.tensor_tensor(sp2[:, :], sp1[:, :], sp2[:, :], ALU.min),
                  nm + "_vsp", 1, mark=nm + f"_vsp{n}")

        def dve_dtv(v, n):
            _, _, _, _, _, dtA_, dtv_, _ = SL(n)
            p.waits(v, nm + f"_aspl{n}", nm + f"_aspm{n}")
            p.selfsync(v, v.tensor_tensor(dtv_, spm[:, :], spl[:, :],
                                          ALU.add), SS)
            p.wait(v, nm + "_abc")
            if n >= 2:
                p.wait(v, nm + f"_sgm{n - 2}")
            p.inc(v.tensor_tensor(dtA_, dtv_, Abc[:, :], ALU.mult),
                  nm + "_vda", 1, mark=nm + f"_vda{n}")

        def dve_t2(v, n):
            sgt_, _, _, _, T2p_, _, dtv_, _ = SL(n)
            p.waits(v, nm + f"_slm{n}", nm + f"_ael{n}")
            p.inc(v.tensor_tensor(tm1[:, :], pb1[:, 64:128], sgt_,
                                  ALU.subtract), nm + "_vt2", 1,
                  mark=nm + f"_vt2a{n}")
            p.wait(v, nm + f"_at2{n}")
            p.inc(v.tensor_tensor(T2p_, tm2[:, :], dtv_, ALU.mult),
                  nm + "_vt2", 1, mark=nm + f"_vt2{n}")

        def pe_sigma(t, n):
            sgt_, _, _, _, _, dtA_, _, _ = SL(n)
            p.wait(t, nm + f"_vda{n}")
            if n >= 1:
                p.wait(t, nm + f"_agp{n - 1}")
            if n >= 2:
                p.wait(t, nm + f"_vye{n - 2}_7")
            p.inc(t.matmul(pb1[:, 0:64], cst["ltri_f"][:, :], dtA_,
                           start=True, stop=True),
                  nm + "_sgm", 1, mark=nm + f"_sgm{n}")
            p.waits(t, nm + f"_asg{n}", nm + f"_aes{n}", nm + f"_asn{n}")
            p.inc(t.transpose(pb1[0:64, 256:384], sgt_, cst["ident_f"][:, :]),
                  nm + "_sTm", 1, mark=nm + f"_sTm{n}")
            p.wait(t, nm + f"_sTe{n}")
            p.inc(t.matmul(pb1[:, 64:128], cst["r127_f"][:, :], sgt_,
                           start=True, stop=True),
                  nm + "_slm", 1, mark=nm + f"_slm{n}")

        def act_sp(s, n):
            p.wait(s, nm + f"_vsp{n}")
            p.selfsync(s, s.activation(spm[:, :], sp1[:, :], AF.Relu), SSA)
            p.mark_here(SSA, nm + f"_aspm{n}")
            p.selfsync(s, s.activation(spe[:, :], sp2[:, :], AF.Exp), SSA)
            p.inc(s.activation(spl[:, :], spe[:, :], AF.Ln,
                               bias=cst["ones_colf"][:, 0:1]),
                  nm + "_asl", 1, mark=nm + f"_aspl{n}")

        def act_sgev(s, n):
            sgt_, sgn_, esg_, esl_, _, _, _, sT_ = SL(n)
            p.wait(s, nm + f"_sgm{n}")
            p.inc(s.activation(sgt_, pb1[:, 0:64], AF.Identity),
                  nm + "_asg", 1, mark=nm + f"_asg{n}")
            p.inc(s.activation(sgn_, pb1[:, 0:64], AF.Identity, scale=-1.0),
                  nm + "_asn", 1, mark=nm + f"_asn{n}")
            p.inc(s.activation(esg_, pb1[:, 0:64], AF.Exp),
                  nm + "_aes", 1, mark=nm + f"_aes{n}")
            p.wait(s, nm + f"_sTm{n}")
            if n >= 2:
                p.wait(s, nm + f"_sgw{n - 2}")
            p.inc(s.activation(sT_, pb1[0:64, 256:384], AF.Identity),
                  nm + "_sTe", 1, mark=nm + f"_sTe{n}")
            p.wait(s, nm + f"_slm{n}")
            if n >= 2:
                p.wait(s, nm + f"_vsb{n - 2}_7")
            p.inc(s.activation(esl_, pb1[:, 64:128], AF.Exp),
                  nm + "_ael", 1, mark=nm + f"_ael{n}")

        def act_at2(s, n):
            p.wait(s, nm + f"_vt2a{n}")
            p.inc(s.activation(tm2[:, :], tm1[:, :], AF.Exp),
                  nm + "_at2", 1, mark=nm + f"_at2{n}")

        @block.vector
        def _(v):
            p.waits(v, nm + "_a2done", "atl_xpdone0", "atl_xpdone1",
                    nm + "_pcdone", nm + "_pcdone0", nm + "_azs126", nm + "_azs127")
            for eb in range(NEB):
                p.inc(v.memset(xbcbuf[:, eb * XB: eb * XB + 3], 0.0), nm + "_hs", 1)
            p.mark_here(nm + "_hs", nm + "_hst")
            v.memset(St[:, :], 0.0)
            p.inc(v.memset(St_bf[:, :], 0.0), nm + "_suv", 1, mark=nm + "_st0")
            p.wait(v, nm + "_dexp")
            for ft in range(NFT):
                p.inc(v.tensor_scalar(dDiag[:, ft * 128:(ft + 1) * 128],
                                      cst["ident_bf"][:, :], Dexp[:, ft:ft + 1],
                                      None, ALU.mult), nm + "_sud", 1)
            p.mark_here(nm + "_sud", nm + "_ddg")
            dve_sp1(v, 0)
            dve_dtv(v, 0)
            for gg in range(ngg):
                for eb in range(NEB):
                    ci = gg * NEB + eb
                    if ci >= 2:
                        p.wait(v, nm + f"_cvm{ci - 2}")
                    elif ci == 0:
                        p.wait(v, nm + "_su0")
                    for k in range(4):
                        ins = v.tensor_scalar(dcv[eb % 2][:, k * 128:(k + 1) * 128],
                                              cst["ident_bf"][:, :],
                                              cw_sb[:, eb * 4 + k: eb * 4 + k + 1],
                                              None, ALU.mult)
                    p.inc(ins, nm + "_dcb", 1, mark=nm + f"_dcb{gg * NEB + eb}")
                for lc in range(4):
                    cc = gg * 4 + lc
                    _, _, esg_, esl_, T2p_, _, dtv_, _ = SL(cc)
                    if cc + 1 < 16:
                        dve_sp1(v, cc + 1)
                    if cc == 0:
                        dve_t2(v, 0)
                    p.wait(v, nm + f"_vt2{cc}")
                    for h in range(64):
                        if h == 0:
                            for ft in range(NFT):
                                p.wait(v, nm + f"_xtk{cc}_{ft}")
                            if cc:
                                p.waits(v, nm + f"_ctm{cc - 1}_7",
                                        nm + f"_ynt{cc - 1}_{NFT - 1}")
                        ins = v.tensor_scalar(xsc[:, h * 64:(h + 1) * 64],
                                              x_tok[:, h * 64:(h + 1) * 64],
                                              T2p_[:, h:h + 1], None, ALU.mult)
                    p.inc(ins, nm + "_vxs", 1, mark=nm + f"_vxs{cc}")
                    if cc + 1 < 16:
                        dve_dtv(v, cc + 1)
                    for g8 in range(8):
                        for hh in range(8):
                            h = g8 * 8 + hh
                            p.waits(v, nm + f"_am0{cc}_{h}", nm + f"_agp{cc}")
                            if h >= 4:
                                p.wait(v, nm + f"_inm{cc}_{h - 4}")
                            p.inc(v.scalar_tensor_tensor(Mh[:, (h % 4) * 128:(h % 4 + 1) * 128],
                                                         M0[:, (h % 4) * 128:(h % 4 + 1) * 128],
                                                         dtv_[:, h:h + 1], G_sb[:, :],
                                                         ALU.mult, ALU.mult),
                                  nm + "_vmh", 1, mark=nm + f"_vmh{cc}_{h}")
                        p.waits(v, nm + f"_iev{cc}_{g8}", nm + f"_inm{cc}_{g8 * 8 + 7}")
                        p.wait(v, nm + f"_aes{cc}")
                        ins = None
                        for hh in range(8):
                            h = g8 * 8 + hh
                            ins = v.scalar_tensor_tensor(y_sb[:, h * 64:(h + 1) * 64],
                                                         isb[:, hh * 64:(hh + 1) * 64],
                                                         esg_[:, h:h + 1],
                                                         pb7[:, hh * 64:(hh + 1) * 64],
                                                         ALU.mult, ALU.add)
                        p.inc(ins, nm + "_vye", 1, mark=nm + f"_vye{cc}_{g8}")
                        p.waits(v, nm + f"_ctm{cc}_{g8}", nm + f"_ael{cc}")
                        ins = None
                        for hh in range(8):
                            h = g8 * 8 + hh
                            ins = v.scalar_tensor_tensor(St[0:64, h * 64:(h + 1) * 64],
                                                         St[0:64, h * 64:(h + 1) * 64],
                                                         esl_[0:64, h:h + 1],
                                                         pb0[0:64, hh * 64:(hh + 1) * 64],
                                                         ALU.mult, ALU.add)
                        p.selfsync(v, ins, SS)
                        p.inc(v.tensor_scalar(St_bf[0:64, g8 * 512:(g8 + 1) * 512],
                                              St[0:64, g8 * 512:(g8 + 1) * 512],
                                              0.0, None, ALU.add),
                              nm + "_vsb", 1, mark=nm + f"_vsb{cc}_{g8}")
                    p.mark_here(nm + "_vye", nm + f"_yev{cc}")
                    p.waits(v, nm + f"_zsl{cc}", nm + f"_yev{cc}")
                    p.selfsync(v, v.tensor_tensor(x_tok[:, :], x_tok[:, :],
                                                  y_sb[:, :], ALU.mult), SS)
                    p.mark_here(nm + "_vch", nm + f"_vyg{cc}")
                    p.wait(v, nm + f"_asq{cc}")
                    p.selfsync(v, v.tensor_scalar(stf[:, 1:2], stf[:, 0:1],
                                                  1.0 / D_INNER, 1e-5, ALU.mult,
                                                  ALU.add), SS)
                    p.inc(v.tensor_scalar(stf[:, 6:7], stf[:, 1:2], 0.0, None,
                                          ALU.add), nm + "_vrv", 1,
                          mark=nm + f"_vrv{cc}")
                    p.wait(v, nm + f"_ars{cc}")
                    p.inc(v.tensor_scalar(xsc[:, :], x_tok[:, :], stf[:, 3:4],
                                          None, ALU.mult), nm + "_vyn", 1,
                          mark=nm + f"_vyn{cc}")
                    if cc + 1 < 16:
                        dve_t2(v, cc + 1)
                for dh in range(2 if aout else 0):
                    for lc in range(4):
                        oi = (gg * 2 + dh) * 4 + lc
                        p.waits(v, nm + f"_xrl{oi}",
                                nm + f"_oev{gg * 16 + dh * 8 + 7}_{lc}")
                        p.inc(v.tensor_tensor(otok[lc][:, :], otok[lc][:, :],
                                              xresh[:, :], ALU.add),
                              nm + "_ora", 1, mark=nm + f"_ora{oi}")

        @block.tensor
        def _(t):
            p.wait(t, nm + "_a2done")
            pe_sigma(t, 0)
            for gg in range(ngg):
                for eb in range(NEB):
                    ci = gg * NEB + eb
                    p.waits(t, nm + f"_gld{gg}", nm + f"_dcb{ci}")
                    if ci >= 2:
                        p.wait(t, nm + f"_cve{ci - 2}")
                    if gg and eb == 0:
                        p.wait(t, nm + f"_vsb{gg * 4 - 1}_7")
                    cvp = [pb0, pb4][ci % 2]
                    mm = None
                    for k in range(4):
                        mm = t.matmul(cvp[:, 0:512], dcv[eb % 2][:, k * 128:(k + 1) * 128],
                                      ebcol(eb, k), start=(k == 0), stop=(k == 3))
                    p.inc(mm, nm + "_cvm", 1, mark=nm + f"_cvm{ci}")
                for lc in range(4):
                    cc = gg * 4 + lc
                    for ft in range(NFT):
                        p.wait(t, nm + f"_cve{gg * NEB + ft}")
                        if ft >= 2:
                            p.wait(t, nm + f"_xtk{cc}_{ft - 2}")
                        elif cc:
                            p.wait(t, nm + f"_xtk{cc - 1}_{NFT - 2 + ft}")
                            if ft == 0:
                                p.waits(t, nm + f"_bte{cc - 1}",
                                        nm + f"_yfe{cc - 1}_{NFT - 2}")
                            else:
                                p.wait(t, nm + f"_yfe{cc - 1}_{NFT - 1}")
                        p.inc(t.transpose([pb2, pb3][ft % 2][:, 0:128],
                                          ebtile(ft, lc), cst["ident_bf"][:, :]),
                              nm + "_xtr", 1, mark=nm + f"_xtr{cc}_{ft}")
                    p.waits(t, nm + f"_cve{gg * NEB + 32}", nm + f"_xtk{cc}_{30}")
                    if cc:
                        p.wait(t, nm + f"_ctm{cc - 1}_7")
                    p.inc(t.transpose(pb2[:, 128:192], ebtile(32, lc, slice(0, 64)),
                                      cst["ident_bf"][0:64, 0:64]),
                          nm + "_btr", 1, mark=nm + f"_btr{cc}")
                    p.waits(t, nm + f"_vt2a{cc}", nm + f"_ael{cc}")
                    p.wait(t, nm + f"_ccp{cc}")
                    if cc:
                        p.wait(t, nm + f"_agp{cc - 1}")
                    p.inc(t.matmul(pb1[:, 128:256], ebtile(32, lc, slice(0, 64)),
                                   C0[:, :], start=True, stop=True),
                          nm + "_gpm", 1, mark=nm + f"_gpm{cc}")
                    if cc + 1 < 16:
                        pe_sigma(t, cc + 1)
                    for g8 in range(8):
                        for half in range(2):
                            hg4 = g8 * 2 + half
                            bi = cc * 16 + hg4
                            p.wait(t, nm + f"_sR{cc}_{hg4 // 8}")
                            if lc == 0 and hg4 == 0:
                                p.wait(t, nm + f"_cve{gg * NEB + NEB - 1}")
                            if bi:
                                p.wait(t, nm + f"_am0{cc}_{hg4 * 4 - 1}"
                                       if hg4 else nm + f"_am0{cc - 1}_{63}")
                            mm = t.matmul(pb4[:, 0:512],
                                          cst["ones_f"][(hg4 // 8) * 32:
                                                        (hg4 // 8) * 32 + 1, 0:128],
                                          sR[(hg4 // 8) * 32:(hg4 // 8) * 32 + 1,
                                             (hg4 % 8) * 512:(hg4 % 8 + 1) * 512],
                                          start=True, stop=False)
                            for j in range(4):
                                mm = t.matmul(pb4[:, j * 128:(j + 1) * 128],
                                              cst["masku_bf"][:, :],
                                              cst["ident_bf"][:, :],
                                              start=False, stop=(j == 3))
                            p.inc(mm, nm + "_bcm", 1, mark=nm + f"_bcm{cc * 16 + hg4}")
                        p.wait(t, nm + f"_vsb{cc - 1}_{g8}" if cc else nm + "_st0")
                        if g8:
                            p.wait(t, nm + f"_iev{cc}_{g8 - 1}")
                        elif cc:
                            p.wait(t, nm + f"_iev{cc - 1}_{7}")
                        p.wait(t, nm + f"_ccp{cc}")
                        p.inc(t.matmul(pb6[:, 0:512], C0[:, :],
                                       St_bf[0:64, g8 * 512:(g8 + 1) * 512],
                                       start=True, stop=True),
                              nm + "_itm", 1, mark=nm + f"_itm{cc}_{g8}")
                        if g8:
                            p.wait(t, nm + f"_vye{cc}_{g8 - 1}")
                        elif cc:
                            p.wait(t, nm + f"_vye{cc - 1}_{7}")
                        p.wait(t, nm + "_ddg")
                        p.wait(t, nm + f"_cve{gg * NEB + g8 * 4 + 3}")
                        mm = None
                        for q in range(4):
                            ft = g8 * 4 + q
                            mm = t.matmul(pb7[:, q * 128:(q + 1) * 128],
                                          ebtile(ft, lc),
                                          dDiag[:, ft * 128:(ft + 1) * 128],
                                          start=(q == 0), stop=False)
                        p.inc(mm, nm + "_ddm", 1)
                        for hh in range(8):
                            h = g8 * 8 + hh
                            p.waits(t, nm + f"_vmh{cc}_{h}", nm + f"_xtk{cc}_{h // 2}")
                            p.inc(t.matmul(pb7[:, hh * 64:(hh + 1) * 64],
                                           Mh[:, (h % 4) * 128:(h % 4 + 1) * 128],
                                           x_tok[:, h * 64:(h + 1) * 64],
                                           start=False, stop=(hh == 7)),
                                  nm + "_inm", 1, mark=nm + f"_inm{cc}_{h}")
                        p.waits(t, nm + f"_vxs{cc}", nm + f"_bte{cc}")
                        if g8:
                            p.wait(t, nm + f"_vsb{cc}_{g8 - 1}")
                        else:
                            p.wait(t, nm + f"_cve{gg * NEB + NEB - 1}")
                        p.inc(t.matmul(pb0[0:64, 0:512], btok[:, :],
                                       xsc[:, g8 * 512:(g8 + 1) * 512],
                                       start=True, stop=True),
                              nm + "_ctm", 1, mark=nm + f"_ctm{cc}_{g8}")
                    for ft in range(NFT):
                        p.wait(t, nm + f"_vyn{cc}")
                        if ft >= 2:
                            p.wait(t, nm + f"_yfe{cc}_{ft - 2}")
                        elif cc:
                            p.wait(t, nm + f"_yfe{cc - 1}_{NFT - 2 + ft}")
                        if cc >= 4 and ft < 2 and aout:
                            p.wait(t, nm + f"_opm{(cc // 4 - 1) * 16 + 15}")
                        p.inc(t.transpose([pb2, pb3][ft % 2][:, 0:128],
                                          xsc[:, ft * 128:(ft + 1) * 128],
                                          cst["ident_bf"][:, :]),
                              nm + "_ynt", 1, mark=nm + f"_ynt{cc}_{ft}")
                for dh in range(2 if aout else 0):
                    for db in range(8):
                        gwo = gg * 16 + dh * 8 + db
                        p.wait(t, nm + f"_wol{gwo}")
                        if db == 0 and dh == 0:
                            for ft in range(NFT):
                                p.wait(t, nm + f"_yfe{gg * 4 + 3}_{ft}")
                        if gwo:
                            p.wait(t, nm + f"_oev{gwo - 1}_{3}")
                        mm = None
                        for et in range(NFT):
                            mm = t.matmul(pb5[:, 0:512],
                                          wos[gwo % 2][:, et * 128:(et + 1) * 128],
                                          yn_fm[:, et * 512:(et + 1) * 512],
                                          start=(et == 0), stop=(et == NFT - 1))
                        p.inc(mm, nm + "_opm", 1, mark=nm + f"_opm{gwo}")
                        p.wait(t, nm + f"_ofe{gwo}")
                        for lc in range(4):
                            if lc:
                                p.wait(t, nm + f"_oev{gwo}_{lc - 1}")
                            mm = t.matmul(pb5[:, lc * 128:(lc + 1) * 128],
                                          ofm[:, lc * 128:(lc + 1) * 128],
                                          cst["ident_bf"][:, :],
                                          start=True, stop=True)
                            p.inc(mm, nm + "_obt", 1, mark=nm + f"_obt{gwo}_{lc}")

        @block.scalar
        def _(s):
            act_sp(s, 0)
            act_sgev(s, 0)
            for gg in range(ngg):
                for eb in range(NEB):
                    ci = gg * NEB + eb
                    p.waits(s, nm + f"_cvm{ci}", nm + "_su0")
                    p.inc(s.activation(xbcbuf[:, eb * XB + 3: eb * XB + 515],
                                       [pb0, pb4][ci % 2][:, 0:512], AF.Silu,
                                       bias=cb_sb[:, eb:eb + 1]),
                          nm + "_cve", 1, mark=nm + f"_cve{ci}")
                for lc in range(4):
                    cc = gg * 4 + lc
                    sgt_, sgn_, _, _, _, _, _, _ = SL(cc)
                    if cc == 0:
                        act_at2(s, 0)
                    for ft in range(NFT):
                        p.wait(s, nm + f"_xtr{cc}_{ft}")
                        if ft == 0 and cc:
                            p.wait(s, nm + f"_vyn{cc - 1}")
                        p.inc(s.activation(x_tok[:, ft * 128:(ft + 1) * 128],
                                           [pb2, pb3][ft % 2][:, 0:128], AF.Identity),
                              nm + "_xtk", 1, mark=nm + f"_xtk{cc}_{ft}")
                    p.wait(s, nm + f"_btr{cc}")
                    p.inc(s.activation(btok[:, :], pb2[:, 128:192], AF.Identity),
                          nm + "_bte", 1, mark=nm + f"_bte{cc}")
                    if cc:
                        p.wait(s, nm + f"_itm{cc - 1}_7")
                    p.inc(s.copy(C0[:, :], ebtile(32, lc, slice(64, 128))),
                          nm + "_ccp", 1, mark=nm + f"_ccp{cc}")
                    p.wait(s, nm + f"_gpm{cc}")
                    p.inc(s.activation(G_sb[:, :], pb1[:, 128:256], AF.Identity),
                          nm + "_agp", 1, mark=nm + f"_agp{cc}")
                    if cc + 1 < 16:
                        act_sp(s, cc + 1)
                        act_sgev(s, cc + 1)
                    for g8 in range(8):
                        for half in range(2):
                            hg4 = g8 * 2 + half
                            p.wait(s, nm + f"_bcm{cc * 16 + hg4}")
                            if hg4 == 0:
                                p.wait(s, nm + f"_asn{cc}")
                            for hh in range(4):
                                h = hg4 * 4 + hh
                                if h >= 4:
                                    p.wait(s, nm + f"_vmh{cc}_{h - 4}")
                                elif cc:
                                    p.wait(s, nm + f"_vmh{cc - 1}_{60 + h}")
                                p.inc(s.activation(M0[:, (h % 4) * 128:(h % 4 + 1) * 128],
                                                   pb4[:, hh * 128:(hh + 1) * 128],
                                                   AF.Exp, bias=sgn_[:, h:h + 1]),
                                      nm + "_am0", 1, mark=nm + f"_am0{cc}_{h}")
                        p.wait(s, nm + f"_itm{cc}_{g8}")
                        if g8:
                            p.wait(s, nm + f"_vye{cc}_{g8 - 1}")
                        elif cc:
                            p.wait(s, nm + f"_vye{cc - 1}_{7}")
                        p.inc(s.activation(isb[:, :], pb6[:, 0:512], AF.Identity),
                              nm + "_iev", 1, mark=nm + f"_iev{cc}_{g8}")
                    p.waits(s, nm + f"_zld{cc}", nm + f"_inm{cc}_{63}",
                            nm + f"_vxs{cc}")
                    p.wait(s, nm + f"_ctm{cc}_{7}")
                    p.inc(s.activation(x_tok[:, :], zbuf[:, :], AF.Silu),
                          nm + "_zsl", 1, mark=nm + f"_zsl{cc}")
                    p.wait(s, nm + f"_vyg{cc}")
                    p.selfsync(s, s.activation(xsc[:, :], x_tok[:, :], AF.Square,
                                               accum_out=stf[:, 0:1]), SSA)
                    p.mark_here(SSA, nm + f"_asq{cc}")
                    p.wait(s, nm + f"_vrv{cc}")
                    p.selfsync(s, s.activation(stf[:, 2:3], stf[:, 6:7], AF.Ln), SSA)
                    p.inc(s.activation(stf[:, 3:4], stf[:, 2:3], AF.Exp, scale=-0.5),
                          nm + "_ars", 1, mark=nm + f"_ars{cc}")
                    for ft in range(NFT):
                        p.wait(s, nm + f"_ynt{cc}_{ft}")
                        p.inc(s.activation(yn_fm[:, ft * 512 + lc * 128: ft * 512 + (lc + 1) * 128],
                                           [pb2, pb3][ft % 2][:, 0:128],
                                           AF.Identity),
                              nm + "_yfe", 1, mark=nm + f"_yfe{cc}_{ft}")
                    if cc + 1 < 16:
                        act_at2(s, cc + 1)
                for dh in range(2 if aout else 0):
                    for db in range(8):
                        gwo = gg * 16 + dh * 8 + db
                        p.wait(s, nm + f"_opm{gwo}")
                        p.inc(s.activation(ofm[:, :], pb5[:, 0:512], AF.Identity),
                              nm + "_ofe", 1, mark=nm + f"_ofe{gwo}")
                        for lc in range(4):
                            oi = (gg * 2 + dh) * 4 + lc
                            p.wait(s, nm + f"_obt{gwo}_{lc}")
                            if db == 0 and oi >= 4:
                                p.wait(s, nm + f"_ost{oi - 4}")
                            p.inc(s.activation(otok[lc][:, db * 128:(db + 1) * 128],
                                               pb5[:, lc * 128:(lc + 1) * 128],
                                               AF.Identity),
                                  nm + "_oev", 1, mark=nm + f"_oev{gwo}_{lc}")

    c1px.close()
    if not aout:
        lcc = ngg * 4 - 1
        fin = {0: [nm + f"_cve{(ngg - 1) * NEB + NEB - 1}"],
               1: [nm + f"_xtk{lcc}_{31}", nm + f"_ccp{lcc}"],
               2: [nm + f"_agp{lcc}", nm + f"_vt2{lcc}", nm + f"_sR{lcc}_1"],
               3: [nm + f"_vmh{lcc}_{63}"],
               4: [nm + f"_yev{lcc}", nm + f"_vsb{lcc}_{7}"],
               5: [nm + f"_yev{lcc}", nm + f"_vyn{lcc}"]}
        return fin[min(lvl, 5)]
    return [nm + f"_ost{(ngg * 2 - 1) * 4 + lc}" for lc in range(4)]


def build_program(prev_marks, phases=("attn_b",), io_src="x_in", detect_races=False):
    import os as _os
    nc = bass.Bass("TRN2", target_bir_lowering=False,
                   debug=bool(_os.environ.get("DUMPB")),
                   detect_race_conditions=detect_races)
    g = declare_io(nc)
    ctx = nc.ctx
    with contextlib.ExitStack() as top:
        p = Prog(nc, top, prev_marks)
        cst = alloc_consts(nc, top)
        with nc.Block() as block:
            emit_consts(p, block, g, cst)
            src = g[io_src]
            last = None
            for ph in phases:
                if ph == "ln_dbg":
                    import contextlib as _cl
                    with _cl.ExitStack() as dbgs:
                        u_dbg = dbgs.enter_context(nc.sbuf_tensor("dbg_ufm", [128, ND * T], BF16))
                        emit_ln(p, block, dbgs, g, cst, "dbg", src, u_dbg)
                        @block.gpsimd
                        def _(gp):
                            p.wait(gp, "dbg_done")
                            dstr = g["x_out"][:, :].rearrange("(a p) c -> p a c", p=128)
                            srcr = u_dbg[:, :].rearrange("p (a c) -> p a c", c=T)
                            p.inc(gp.dma_start(dstr, srcr), "dbgo", 16, mark="dbg_OUT")
                    last = ["dbg_OUT"]
                elif ph == "attn_a":
                    last = emit_attn_a(p, block, top, g, cst, src, g["x_out"],
                                       start_mark=None)
                elif ph == "attn_b":
                    last = emit_attn_b(p, block, top, g, cst, src, g["x_out"])
                elif ph == "dq8":
                    with contextlib.ExitStack() as cdq8:
                        dq_free, (emit_gather, _gi) = emit_dequant8(
                            p, block, g, cst, "dq8", start_mark=None,
                            ctx=cdq8)
                    last = dq_free + emit_gather()
                elif ph == "full":
                    dqh = {}
                    last = emit_attn_a(p, block, top, g, cst, g["x_in"],
                                       g["x_t1"], start_mark=None,
                                       with_dq=True, dq_out=dqh)
                    last = emit_ffn(p, block, top, g, cst, "ff1", g["x_t1"],
                                    g["x_t2"], start_mark=last, dequant=False,
                                    gathered=True, dq_gate=dqh["gmarks"],
                                    early_gate="ata_yev15")
                    last = emit_attn_b(p, block, top, g, cst, g["x_t2"],
                                       g["x_t3"], start_mark=last)
                    last = emit_ffn(p, block, top, g, cst, "ff2", g["x_t3"],
                                    g["x_out"], start_mark=last, dequant=False,
                                    gathered=True, dq_gate=dqh["gmarks"],
                                    early_gate="atb_qkvdone15")
                elif ph == "ffn":
                    last = emit_ffn(p, block, top, g, cst, "ff1", src,
                                    g["x_out"], dequant=True)
                else:
                    raise ValueError(ph)
            @block.gpsimd
            def _(gp):
                for m in last:
                    p.wait(gp, m)
        marks = p.marks
    return nc, marks


def build_twice(phases, io_src="x_in"):
    _, marks = build_program(None, phases, io_src)
    nc, marks2 = build_program(marks, phases, io_src)
    assert marks == marks2, "nondeterministic emission"
    return nc


def kernel(**inputs):
    w = host_prep({k: np.asarray(v) for k, v in inputs.items()})
    hs = np.asarray(inputs["L0_attn_a_hidden_states"])
    nc = build_twice(("full",))
    in_maps = []
    for b in range(NCORES):
        m = dict(w)
        m["x_in"] = np.ascontiguousarray(hs[b][:, PERM])
        m["dqgu_b"] = np.ascontiguousarray(w["gu_bytes"][b * 128:(b + 1) * 128])
        m["dqgu_s"] = np.ascontiguousarray(w["gu_scex"][b * 128:(b + 1) * 128])
        dnb, dns, pls = [], [], [0, 1]
        for s in range(6):
            blk = 6 * b + s
            if blk < 44:
                jb = blk % 22
                dnb.append(w["down_bytes"][jb * 128:(jb + 1) * 128])
                dns.append(w["dn_scex"][jb * 128:(jb + 1) * 128])
                pls.append(blk // 22)
            else:
                dnb.append(np.zeros((128, D), np.uint8))
                dns.append(np.zeros((128, D), np.float16))
                pls.append(0)
        m["dqdn_b"] = np.ascontiguousarray(np.concatenate(dnb, 0))
        m["dqdn_s"] = np.ascontiguousarray(np.concatenate(dns, 0))
        pls = np.asarray(pls, np.int32)
        msk = np.concatenate([15 * (1 - pls), 15 * pls])
        m["dq_sh"] = np.broadcast_to(msk[None, :], (128, 16)).copy()
        in_maps.append(m)
    res = run_bass_kernel_spmd(nc, in_maps, core_ids=list(range(NCORES)))
    outs = [res.results[b]["x_out"][:, INVPERM] for b in range(NCORES)]
    return np.stack(outs, axis=0)
